# version 01
# baseline (speedup 1.0000x reference)
import numpy as np
from contextlib import ExitStack

import concourse.bass as bass
import concourse.mybir as mybir
from concourse.bass_utils import run_bass_kernel_spmd

F32 = mybir.dt.float32
BF16 = mybir.dt.bfloat16
AF = mybir.ActivationFunctionType
ALU = mybir.AluOpType
AX = mybir.AxisListType

B, S, D = 4, 4096, 1024
DC = 8
W = 512
NT = 2048
MAGIC = 1.5 * 2.0 ** 23

ENGINES = ("sync", "tensor", "vector", "scalar", "gpsimd")


class Buf:
    __slots__ = ("t", "w", "readers", "name")

    def __init__(self, t, name=None):
        self.t = t
        self.w = None
        self.readers = {}
        self.name = name or getattr(t, "name", None) or str(id(t))


class KB:

    def __init__(self):
        self.ops = {e: [] for e in ENGINES}
        self.cnt = {e: 0 for e in ENGINES}
        self.water = {e: {} for e in ENGINES}
        self.pending = {e: [] for e in ENGINES}
        self.out_channels = set()

    def _wait(self, eng, tok):
        if tok is None:
            return
        pe, v = tok
        if self.water[eng].get(pe, 0) >= v:
            return
        self.ops[eng].append(("w", pe, v))
        self.water[eng][pe] = v

    def do(self, eng, fn, reads=(), writes=(), inc=1, sem_key=None):
        key = sem_key or eng
        deps = []
        for b in reads:
            deps.append(b.w)
        for b in writes:
            deps.append(b.w)
            deps.extend(b.readers.values())
        for t in deps:
            self._wait(eng, t)
        self.ops[eng].append(("o", fn, inc, key))
        if inc:
            self.cnt[key] = self.cnt.get(key, 0) + inc
            tok = (key, self.cnt[key])
            for b in self.pending[eng]:
                b.readers[key] = tok
            self.pending[eng] = []
        else:
            tok = None
        for b in reads:
            if inc:
                b.readers[key] = tok
            elif b not in self.pending[eng]:
                self.pending[eng].append(b)
        for b in writes:
            if inc:
                b.w = tok
                b.readers = {}
            elif b not in self.pending[eng]:
                self.pending[eng].append(b)
        return tok

    def barrier(self):
        snap = dict(self.cnt)
        for eng in ENGINES:
            for pe, v in snap.items():
                if pe != eng and v > 0:
                    self._wait(eng, (pe, v))

    def finalize(self, nc, es):
        sems = {k: es.enter_context(nc.semaphore("sem_" + str(i)))
                for i, k in enumerate(self.cnt)}
        for ch in sorted(self.out_channels):
            if self.cnt.get(ch, 0) > 0:
                self.ops["gpsimd"].append(("w", ch, self.cnt[ch]))
        block = es.enter_context(nc.Block())
        for ename in ENGINES:
            items = self.ops[ename]
            if not items:
                continue

            def make_body(items, ename):
                def body(eng):
                    for it in items:
                        if it[0] == "w":
                            eng.wait_ge(sems[it[1]], it[2])
                        else:
                            ins = it[1](eng)
                            if it[2]:
                                ins.then_inc(sems[it[3]], it[2])
                return body

            getattr(block, ename)(make_body(items, ename))


def dma_in(kb, buf, src_ap, dst_sl=None, q="sync"):
    d = dst_sl if dst_sl is not None else buf.t[:]
    return kb.do(q, lambda e, d=d, s=src_ap: e.dma_start(out=d, in_=s), writes=[buf], inc=16,
                 sem_key="ld_" + buf.name)


def dma_out(kb, dst_ap, buf, src_sl=None, q="gpsimd"):
    s = src_sl if src_sl is not None else buf.t[:]
    ch = "st_" + buf.name
    kb.out_channels.add(ch)
    return kb.do(q, lambda e, d=dst_ap, s=s: e.dma_start(out=d, in_=s), reads=[buf], inc=16,
                 sem_key=ch)


def cast(kb, dst_buf, dst_sl, src_buf, src_sl, eng="vector"):
    kb.do(eng, lambda v, d=dst_sl, s=src_sl: v.tensor_copy(out=d, in_=s),
          reads=[src_buf], writes=[dst_buf])


def mk_consts(nc, es, kb):
    c = {}

    def sb(n, sh, dt):
        return Buf(es.enter_context(nc.sbuf_tensor(n, sh, dt)), n)

    c["ones_row"] = sb("c_ones_row", [1, 128], F32)
    c["ones_col"] = sb("c_ones_col", [128, 1], F32)
    c["ones_bcol"] = sb("c_ones_bcol", [128, 1], BF16)
    c["ident"] = sb("c_ident", [128, 128], F32)
    kb.do("vector", lambda v: v.memset(c["ones_row"].t[:], 1.0), writes=[c["ones_row"]])
    kb.do("vector", lambda v: v.memset(c["ones_col"].t[:], 1.0), writes=[c["ones_col"]])
    kb.do("vector", lambda v: v.memset(c["ones_bcol"].t[:], 1.0), writes=[c["ones_bcol"]])
    kb.do("gpsimd", lambda g: g.iota(c["ident"].t[:], pattern=[[1, 128]], base=0,
                                     channel_multiplier=-1, allow_small_or_imprecise_dtypes=True),
          writes=[c["ident"]])
    kb.do("vector", lambda v: v.tensor_scalar(out=c["ident"].t[:], in0=c["ident"].t[:],
                                              scalar1=0.0, scalar2=None, op0=ALU.is_equal),
          reads=[c["ident"]], writes=[c["ident"]])
    return c


def emit_ln(kb, c, x_ap_fn, xn_bufs, xcol0, ncols, P, sc, out_col0=0):
    ntc = ncols // W
    xq = sc["xsq"]
    for tci in range(ntc):
        colg = xcol0 + tci * W
        colo = out_col0 + tci * W
        for ci in range(DC):
            xa = sc["x1"] if ci % 2 == 0 else sc["x2"]
            dma_in(kb, xa, x_ap_fn(ci, colg))
            kb.do("tensor", lambda t, ci=ci, xa=xa: t.matmul(P[0].t[0:1, 0:W], c["ones_col"].t[:],
                                                      xa.t[:], start=(ci == 0), stop=(ci == DC - 1)),
                  reads=[c["ones_col"], xa], writes=[P[0]])
            kb.do("scalar", lambda s, xa=xa: s.activation(out=xq.t[:], in_=xa.t[:], func=AF.Square),
                  reads=[xa], writes=[xq])
            kb.do("tensor", lambda t, ci=ci: t.matmul(P[1].t[0:1, 0:W], c["ones_col"].t[:],
                                                      xq.t[:], start=(ci == 0), stop=(ci == DC - 1)),
                  reads=[c["ones_col"], xq], writes=[P[1]])
        kb.do("scalar", lambda s: s.activation(out=sc["row1"].t[:], in_=P[0].t[0:1, 0:W],
                                               func=AF.Copy, scale=1.0 / D),
              reads=[P[0]], writes=[sc["row1"]])
        kb.do("scalar", lambda s: s.activation(out=sc["row2"].t[:], in_=P[1].t[0:1, 0:W],
                                               func=AF.Copy, scale=1.0 / D),
              reads=[P[1]], writes=[sc["row2"]])
        kb.do("vector", lambda v: v.tensor_mul(out=sc["row3"].t[:], in0=sc["row1"].t[:], in1=sc["row1"].t[:]),
              reads=[sc["row1"]], writes=[sc["row3"]])
        kb.do("vector", lambda v: v.tensor_sub(out=sc["row2"].t[:], in0=sc["row2"].t[:], in1=sc["row3"].t[:]),
              reads=[sc["row2"], sc["row3"]], writes=[sc["row2"]])
        kb.do("vector", lambda v: v.tensor_scalar(out=sc["row2"].t[:], in0=sc["row2"].t[:],
                                                  scalar1=1e-5, scalar2=None, op0=ALU.add),
              reads=[sc["row2"]], writes=[sc["row2"]])
        kb.do("scalar", lambda s: s.activation(out=sc["row2"].t[:], in_=sc["row2"].t[:],
                                               func=AF.Sqrt),
              reads=[sc["row2"]], writes=[sc["row2"]])
        kb.do("vector", lambda v: v.reciprocal(out=sc["row2"].t[:], in_=sc["row2"].t[:]),
              reads=[sc["row2"]], writes=[sc["row2"]])
        kb.do("tensor", lambda t: t.matmul(P[0].t[:, 0:W], c["ones_row"].t[:], sc["row1"].t[:],
                                           start=True, stop=True),
              reads=[c["ones_row"], sc["row1"]], writes=[P[0]])
        kb.do("tensor", lambda t: t.matmul(P[1].t[:, 0:W], c["ones_row"].t[:], sc["row2"].t[:],
                                           start=True, stop=True),
              reads=[c["ones_row"], sc["row2"]], writes=[P[1]])
        for ci in range(DC):
            xa = sc["x1"] if ci % 2 == 0 else sc["x2"]
            dma_in(kb, xa, x_ap_fn(ci, colg))
            kb.do("vector", lambda v, xa=xa: v.tensor_sub(out=xa.t[:], in0=xa.t[:], in1=P[0].t[:, 0:W]),
                  reads=[xa, P[0]], writes=[xa])
            kb.do("vector", lambda v, ci=ci, colo=colo, xa=xa: v.tensor_mul(
                out=xn_bufs[ci].t[:, colo:colo + W], in0=xa.t[:], in1=P[1].t[:, 0:W]),
                reads=[xa, P[1]], writes=[xn_bufs[ci]])


def mk_ln_scratch(nc, es, pref):
    def sb(n, sh, dt):
        return Buf(es.enter_context(nc.sbuf_tensor(pref + n, sh, dt)), pref + n)
    return {"x1": sb("_x1", [128, W], F32), "x2": sb("_x2", [128, W], F32),
            "xsq": sb("_xsq", [128, W], F32),
            "row1": sb("_row1", [1, W], F32), "row2": sb("_row2", [1, W], F32),
            "row3": sb("_row3", [1, W], F32)}



def build_l0a():
    nc = bass.Bass("TRN2", target_bir_lowering=False, debug=False)
    x_d = nc.dram_tensor("x", [D, S], F32, kind="ExternalInput").ap()
    qw_d = nc.dram_tensor("qw", [DC, 128, 512], F32, kind="ExternalInput").ap()
    kw_d = nc.dram_tensor("kw", [DC, 128, 512], F32, kind="ExternalInput").ap()
    vw_d = nc.dram_tensor("vw", [DC, 128, 512], F32, kind="ExternalInput").ap()
    ww_d = nc.dram_tensor("ww", [8, 64, 1024], F32, kind="ExternalInput").ap()
    z_d = nc.dram_tensor("z", [D, S], F32, kind="ExternalOutput").ap()

    kb = KB()
    es = ExitStack()
    with es:
        def sb(n, sh, dt):
            return Buf(es.enter_context(nc.sbuf_tensor(n, sh, dt)), n)
        c = mk_consts(nc, es, kb)
        P = [Buf(es.enter_context(nc.psum_tensor(f"P{i}", [128, 512], F32)), f"P{i}") for i in range(8)]

        mask = sb("mask", [128, 1408], F32)
        kb.do("gpsimd", lambda g: g.iota(mask.t[:], pattern=[[1, 1408]], base=-512,
                                         channel_multiplier=-1, allow_small_or_imprecise_dtypes=True),
              writes=[mask])
        mask_bf = sb("mask_bf", [128, 1408], BF16)
        kb.do("vector", lambda v: v.tensor_scalar(out=mask_bf.t[:], in0=mask.t[:], scalar1=0.0,
                                                  scalar2=None, op0=ALU.is_ge),
              reads=[mask], writes=[mask_bf])

        sc = mk_ln_scratch(nc, es, "ln")

        qwb = sb("qwb", [128, DC * 512], BF16)
        kwb = sb("kwb", [128, DC * 512], BF16)
        vwb = sb("vwb", [128, DC * 512], BF16)
        wwb = sb("wwb", [64, 8 * 1024], BF16)
        wtmp = sb("wtmp", [128, 512], F32)
        wtmp2 = sb("wtmp2", [64, 1024], F32)
        for ci in range(DC):
            for wd, wb in ((qw_d, qwb), (kw_d, kwb), (vw_d, vwb)):
                dma_in(kb, wtmp, wd[ci, :, :])
                cast(kb, wb, wb.t[:, ci * 512:(ci + 1) * 512], wtmp, wtmp.t[:])
        for h in range(8):
            dma_in(kb, wtmp2, ww_d[h, :, :])
            cast(kb, wwb, wwb.t[0:64, h * 1024:(h + 1) * 1024], wtmp2, wtmp2.t[:])

        qf = [sb(f"qf{i}", [128, S], BF16) for i in range(4)]
        kf = [sb(f"kf{i}", [128, S], BF16) for i in range(4)]
        vt = sb("vt", [128, 32 * 512], BF16)
        xn = [sb(f"xn{i}", [128, W], BF16) for i in range(DC)]

        def x_ap(ci, col):
            return x_d[ci * 128:(ci + 1) * 128, col:col + W]

        for tci in range(S // W):
            col = tci * W
            emit_ln(kb, c, x_ap, xn, col, W, P, sc, out_col0=0)
            for hp in range(4):
                for wb, pp, dst in ((qwb, P[4], qf[hp]), (kwb, P[5], kf[hp])):
                    for ci in range(DC):
                        kb.do("tensor", lambda t, ci=ci, hp=hp, wb=wb, pp=pp: t.matmul(
                            pp.t[:, 0:W], wb.t[:, ci * 512 + hp * 128:ci * 512 + hp * 128 + 128],
                            xn[ci].t[:], start=(ci == 0), stop=(ci == DC - 1)),
                            reads=[wb, xn[ci]], writes=[pp], inc=1)
                    kb.do("scalar", lambda s, pp=pp, dst=dst, col=col: s.copy(
                        out=dst.t[:, col:col + W], in_=pp.t[:, 0:W]),
                        reads=[pp], writes=[dst])
            for j in range(4):
                kt = tci * 4 + j
                for ci in range(DC):
                    kb.do("tensor", lambda t, ci=ci, j=j: t.matmul(
                        P[6].t[:, 0:W], xn[ci].t[:, j * 128:(j + 1) * 128],
                        vwb.t[:, ci * 512:(ci + 1) * 512], start=(ci == 0), stop=(ci == DC - 1)),
                        reads=[xn[ci], vwb], writes=[P[6]], inc=1)
                kb.do("scalar", lambda s, kt=kt: s.copy(out=vt.t[:, kt * 512:(kt + 1) * 512],
                                                        in_=P[6].t[:, 0:W]),
                      reads=[P[6]], writes=[vt])

        e_sb = sb("e_sb", [128, 512], BF16)
        mcol = sb("mcol", [128, 1], F32)
        rcol = sb("rcol", [128, 1], F32)
        bias = sb("bias", [128, 1], F32)
        srow = sb("srow", [1, 512], F32)
        ysb = sb("ysb", [64, 512], F32)
        yn = sb("yn", [64, 512], BF16)
        zc = [sb(f"zc{i}", [128, 512], F32) for i in range(DC)]

        for qc in range(8):
            nkt = 4 * (qc + 1)
            for h in range(8):
                hp, ho = h // 2, 64 * (h % 2)
                for kt in range(nkt):
                    kb.do("tensor", lambda t, kt=kt, hp=hp, ho=ho, qc=qc: t.matmul(
                        P[0].t[:, 0:W], kf[hp].t[ho:ho + 64, kt * 128:(kt + 1) * 128],
                        qf[hp].t[ho:ho + 64, qc * W:(qc + 1) * W], start=True, stop=True),
                        reads=[kf[hp], qf[hp]], writes=[P[0]])
                    kb.do("vector", lambda v: v.reduce_max(rcol.t[:, 0:1], P[0].t[:, 0:W], axis=AX.X),
                          reads=[P[0]], writes=[rcol])
                    if kt == 0:
                        kb.do("vector", lambda v: v.tensor_copy(out=mcol.t[:], in_=rcol.t[:]),
                              reads=[rcol], writes=[mcol])
                    else:
                        kb.do("vector", lambda v: v.tensor_max(out=mcol.t[:], in0=mcol.t[:], in1=rcol.t[:]),
                              reads=[rcol, mcol], writes=[mcol])
                kb.do("tensor", lambda t: t.transpose(P[1].t[0:1, 0:128], mcol.t[:, 0:1], c["ident"].t[:]),
                      reads=[mcol, c["ident"]], writes=[P[1]])
                kb.do("vector", lambda v: v.reduce_max(bias.t[0:1, 0:1], P[1].t[0:1, 0:128], axis=AX.X),
                      reads=[P[1]], writes=[bias])
                kb.do("tensor", lambda t: t.matmul(P[1].t[:, 0:1], c["ones_row"].t[:], bias.t[0:1, 0:1],
                                                   start=True, stop=True),
                      reads=[c["ones_row"], bias], writes=[P[1]])
                kb.do("vector", lambda v: v.tensor_scalar(out=bias.t[:, 0:1], in0=P[1].t[:, 0:1],
                                                          scalar1=-1.0 / 1024.0, scalar2=None, op0=ALU.mult),
                      reads=[P[1]], writes=[bias])
                for kt in range(nkt):
                    kb.do("tensor", lambda t, kt=kt, hp=hp, ho=ho, qc=qc: t.matmul(
                        P[0].t[:, 0:W], kf[hp].t[ho:ho + 64, kt * 128:(kt + 1) * 128],
                        qf[hp].t[ho:ho + 64, qc * W:(qc + 1) * W], start=True, stop=True),
                        reads=[kf[hp], qf[hp]], writes=[P[0]])
                    kb.do("scalar", lambda s, kt=kt: s.activation(
                        out=e_sb.t[:], in_=P[0].t[:, 0:W],
                        func=AF.Exp, bias=bias.t[:, 0:1], scale=1.0 / 1024.0),
                        reads=[P[0], bias], writes=[e_sb])
                    off = kt * 128 - qc * 512
                    if 0 <= off < 512:
                        kb.do("vector", lambda v, off=off: v.tensor_mul(
                            out=e_sb.t[:], in0=e_sb.t[:], in1=mask_bf.t[:, 512 - off:1024 - off]),
                            reads=[e_sb, mask_bf], writes=[e_sb])
                    kb.do("tensor", lambda t, kt=kt, h=h: t.matmul(
                        P[2].t[0:64, 0:W], vt.t[:, kt * 512 + h * 64:kt * 512 + h * 64 + 64],
                        e_sb.t[:], start=(kt == 0), stop=(kt == nkt - 1)),
                        reads=[vt, e_sb], writes=[P[2]], inc=1)
                    kb.do("tensor", lambda t, kt=kt: t.matmul(
                        P[3].t[0:1, 0:W], c["ones_bcol"].t[:], e_sb.t[:],
                        start=(kt == 0), stop=(kt == nkt - 1)),
                        reads=[c["ones_bcol"], e_sb], writes=[P[3]], inc=1)
                kb.do("scalar", lambda s: s.copy(out=srow.t[:], in_=P[3].t[0:1, 0:W]),
                      reads=[P[3]], writes=[srow])
                kb.do("vector", lambda v: v.reciprocal(out=srow.t[:], in_=srow.t[:]),
                      reads=[srow], writes=[srow])
                kb.do("tensor", lambda t: t.matmul(P[4].t[0:64, 0:W], c["ones_row"].t[0:1, 0:64],
                                                   srow.t[:], start=True, stop=True),
                      reads=[c["ones_row"], srow], writes=[P[4]])
                kb.do("scalar", lambda s: s.copy(out=ysb.t[:], in_=P[2].t[0:64, 0:W]),
                      reads=[P[2]], writes=[ysb])
                kb.do("vector", lambda v: v.tensor_mul(out=yn.t[:], in0=ysb.t[:], in1=P[4].t[0:64, 0:W]),
                      reads=[ysb, P[4]], writes=[yn])
                for ci in range(DC):
                    kb.do("tensor", lambda t, ci=ci, h=h: t.matmul(
                        P[5].t[:, 0:W], wwb.t[0:64, h * 1024 + ci * 128:h * 1024 + ci * 128 + 128],
                        yn.t[:], start=True, stop=True),
                        reads=[wwb, yn], writes=[P[5]])
                    if h == 0:
                        kb.do("vector", lambda v, ci=ci: v.tensor_copy(out=zc[ci].t[:], in_=P[5].t[:, 0:W]),
                              reads=[P[5]], writes=[zc[ci]])
                    else:
                        kb.do("vector", lambda v, ci=ci: v.tensor_add(out=zc[ci].t[:], in0=zc[ci].t[:],
                                                                     in1=P[5].t[:, 0:W]),
                              reads=[P[5], zc[ci]], writes=[zc[ci]])
            for ci in range(DC):
                dma_out(kb, z_d[ci * 128:(ci + 1) * 128, qc * W:(qc + 1) * W], zc[ci])

        kb.finalize(nc, es)
    return nc



DEBUG_L0B = False


def build_l0b():
    TT = 1024
    nc = bass.Bass("TRN2", target_bir_lowering=False, debug=False)
    if DEBUG_L0B:
        dbg_a = nc.dram_tensor("dbg_a", [128, 32 * TT], F32, kind="ExternalOutput").ap()
        dbg_b = nc.dram_tensor("dbg_b", [128, 32 * TT], F32, kind="ExternalOutput").ap()
        dbg_c = nc.dram_tensor("dbg_c", [128, 32 * TT], F32, kind="ExternalOutput").ap()
        dbg_xn = nc.dram_tensor("dbg_xn", [128, DC * 512], F32, kind="ExternalOutput").ap()
    x_d = nc.dram_tensor("x", [D, NT], F32, kind="ExternalInput").ap()
    za_d = nc.dram_tensor("za", [D, NT], F32, kind="ExternalInput").ap()
    zb_d = nc.dram_tensor("zb", [D, NT], F32, kind="ExternalInput").ap()
    w1_d = nc.dram_tensor("w1", [128, DC * 4096], F32, kind="ExternalInput").ap()
    w2_d = nc.dram_tensor("w2", [128, DC * 4096], F32, kind="ExternalInput").ap()
    h1_d = nc.dram_tensor("h1", [128, 32 * 4096], F32, kind="ExternalInput").ap()
    h2_d = nc.dram_tensor("h2", [128, 32 * 4096], F32, kind="ExternalInput").ap()
    w3_d = nc.dram_tensor("w3", [128, 32 * 1024], F32, kind="ExternalInput").ap()
    s1_d = nc.dram_tensor("s1", [128, 32], F32, kind="ExternalInput").ap()
    s2_d = nc.dram_tensor("s2", [128, 32], F32, kind="ExternalInput").ap()
    sh1_d = nc.dram_tensor("sh1", [128, 32], F32, kind="ExternalInput").ap()
    sh2_d = nc.dram_tensor("sh2", [128, 32], F32, kind="ExternalInput").ap()
    s3_d = nc.dram_tensor("s3", [128, 8], F32, kind="ExternalInput").ap()
    xo_d = nc.dram_tensor("xo", [D, NT], F32, kind="ExternalOutput").ap()

    kb = KB()
    es = ExitStack()
    with es:
        def sb(n, sh, dt):
            return Buf(es.enter_context(nc.sbuf_tensor(n, sh, dt)), n)
        c = mk_consts(nc, es, kb)
        P = [Buf(es.enter_context(nc.psum_tensor(f"P{i}", [128, 512], F32)), f"P{i}") for i in range(8)]
        sc = mk_ln_scratch(nc, es, "ln")

        xn = [sb(f"xn{i}", [128, NT], BF16) for i in range(DC)]
        szt = sb("szt", [128, W], F32)
        szt2 = sb("szt2", [128, W], F32)

        def x_ap(ci, col):
            return x_d[ci * 128:(ci + 1) * 128, col:col + W]

        emit_ln(kb, c, x_ap, xn, 0, NT, P, sc, out_col0=0)

        sca = sb("sca", [128, 32 * 5], F32)
        for i, sd in enumerate((s1_d, s2_d, sh1_d, sh2_d)):
            dma_in(kb, sca, sd[:, :], dst_sl=sca.t[:, i * 32:(i + 1) * 32])
        dma_in(kb, sca, s3_d[:, :], dst_sl=sca.t[:, 128:136])

        a_sb = sb("a_sb", [128, 32 * TT], BF16)
        b_sb = sb("b_sb", [128, 32 * TT], BF16)
        wstp = [sb(f"wst{i}", [128, 1024], F32) for i in range(2)]
        wbfp = [sb(f"wbf{i}", [128, 1024], BF16) for i in range(2)]
        t1 = sb("t1", [128, TT], F32)
        t2 = sb("t2", [128, TT], F32)
        xr = sb("xr", [128, W], F32)
        sg = sb("sg", [128, W], BF16)

        NN = TT // W
        h1_r = h1_d.rearrange("p (k f) -> p k f", f=4096)
        h2_r = h2_d.rearrange("p (k f) -> p k f", f=4096)
        w3_r = w3_d.rearrange("p (k f) -> p k f", f=1024)
        w1_r = w1_d.rearrange("p (k f) -> p k f", f=4096)
        w2_r = w2_d.rearrange("p (k f) -> p k f", f=4096)

        for tc in range(NT // TT):
            tc0 = tc * TT
            for fo in range(32):
                for wi, (wr, pp) in enumerate(((w1_r, P[0]), (w2_r, P[1]))):
                    wst, wbf = wstp[wi], wbfp[wi]
                    kb.do("sync", lambda e, wr=wr, fo=fo, wst=wst: e.dma_start(
                        out=wst.t[:].rearrange("p (k f) -> p k f", f=128),
                        in_=wr[:, :, fo * 128:fo * 128 + 128]),
                        writes=[wst], inc=16, sem_key="ld_" + wst.name)
                    cast(kb, wbf, wbf.t[:, 0:1024], wst, wst.t[:, 0:1024])
                    for nn in range(NN):
                        for ci in range(DC):
                            kb.do("tensor", lambda t, ci=ci, nn=nn, pp=pp, tc0=tc0, wbf=wbf: t.matmul(
                                pp.t[:, 0:W],
                                wbf.t[:, ci * 128:(ci + 1) * 128],
                                xn[ci].t[:, tc0 + nn * W:tc0 + nn * W + W],
                                start=(ci == 0), stop=(ci == DC - 1)),
                                reads=[wbf, xn[ci]], writes=[pp], inc=1)
                        if pp is P[0]:
                            kb.do("scalar", lambda s, fo=fo, nn=nn: s.activation(
                                out=t1.t[:, nn * W:nn * W + W], in_=P[0].t[:, 0:W], func=AF.Silu,
                                scale=sca.t[:, fo:fo + 1]),
                                reads=[P[0], sca], writes=[t1])
                        else:
                            kb.do("scalar", lambda s, fo=fo, nn=nn: s.activation(
                                out=t2.t[:, nn * W:nn * W + W], in_=P[1].t[:, 0:W], func=AF.Copy,
                                scale=sca.t[:, 32 + fo:32 + fo + 1]),
                                reads=[P[1], sca], writes=[t2])
                kb.do("vector", lambda v, fo=fo: v.tensor_mul(
                    out=a_sb.t[:, fo * TT:(fo + 1) * TT], in0=t1.t[:], in1=t2.t[:]),
                    reads=[t1, t2], writes=[a_sb])
            if DEBUG_L0B and tc == 1:
                for ci in range(DC):
                    cast(kb, sc["x1"], sc["x1"].t[:], xn[ci], xn[ci].t[:, 0:512])
                    dma_out(kb, dbg_xn[:, ci * 512:(ci + 1) * 512], sc["x1"])
                for fo in range(32):
                    cast(kb, t1, t1.t[:], a_sb, a_sb.t[:, fo * TT:(fo + 1) * TT])
                    dma_out(kb, dbg_a[:, fo * TT:(fo + 1) * TT], t1)
            for asrc, dst, hr, scoff in ((a_sb, b_sb, h1_r, 64), (b_sb, a_sb, h2_r, 96)):
                for fo in range(32):
                    for fig in range(DC):
                        wst, wbf = wstp[fig % 2], wbfp[fig % 2]
                        kb.do("sync", lambda e, hr=hr, fig=fig, fo=fo, wst=wst: e.dma_start(
                            out=wst.t[:, 0:512].rearrange("p (k f) -> p k f", f=128),
                            in_=hr[:, fig * 4:fig * 4 + 4, fo * 128:fo * 128 + 128]),
                            writes=[wst], inc=16, sem_key="ld_" + wst.name)
                        cast(kb, wbf, wbf.t[:, 0:512], wst, wst.t[:, 0:512])
                        for nn in range(NN):
                            for k in range(4):
                                fi = fig * 4 + k
                                kb.do("tensor", lambda t, k=k, nn=nn, fi=fi, asrc=asrc, wbf=wbf: t.matmul(
                                    P[2 + nn].t[:, 0:W], wbf.t[:, k * 128:(k + 1) * 128],
                                    asrc.t[:, fi * TT + nn * W:fi * TT + nn * W + W],
                                    start=(fi == 0), stop=(fi == 31)),
                                    reads=[wbf, asrc], writes=[P[2 + nn]],
                                    inc=1)
                    for nn in range(NN):
                        kb.do("scalar", lambda s, fo=fo, nn=nn, scoff=scoff, dst=dst: s.activation(
                            out=dst.t[:, fo * TT + nn * W:fo * TT + nn * W + W],
                            in_=P[2 + nn].t[:, 0:W], func=AF.Gelu, scale=sca.t[:, scoff + fo:scoff + fo + 1]),
                            reads=[P[2 + nn], sca], writes=[dst])
            if DEBUG_L0B and tc == 1:
                for fo in range(32):
                    cast(kb, t1, t1.t[:], b_sb, b_sb.t[:, fo * TT:(fo + 1) * TT])
                    dma_out(kb, dbg_b[:, fo * TT:(fo + 1) * TT], t1)
                for fo in range(32):
                    cast(kb, t1, t1.t[:], a_sb, a_sb.t[:, fo * TT:(fo + 1) * TT])
                    dma_out(kb, dbg_c[:, fo * TT:(fo + 1) * TT], t1)
            for oo in range(DC):
                for fig in range(DC):
                    wst, wbf = wstp[fig % 2], wbfp[fig % 2]
                    kb.do("sync", lambda e, fig=fig, oo=oo, wst=wst: e.dma_start(
                        out=wst.t[:, 0:512].rearrange("p (k f) -> p k f", f=128),
                        in_=w3_r[:, fig * 4:fig * 4 + 4, oo * 128:oo * 128 + 128]),
                        writes=[wst], inc=16, sem_key="ld_" + wst.name)
                    cast(kb, wbf, wbf.t[:, 0:512], wst, wst.t[:, 0:512])
                    for nn in range(NN):
                        for k in range(4):
                            fi = fig * 4 + k
                            kb.do("tensor", lambda t, k=k, nn=nn, fi=fi, wbf=wbf: t.matmul(
                                P[2 + nn].t[:, 0:W], wbf.t[:, k * 128:(k + 1) * 128],
                                a_sb.t[:, fi * TT + nn * W:fi * TT + nn * W + W],
                                start=(fi == 0), stop=(fi == 31)),
                                reads=[wbf, a_sb], writes=[P[2 + nn]],
                                inc=1)
                for nn in range(NN):
                    col = tc0 + nn * W
                    kb.do("scalar", lambda s, oo=oo, nn=nn: s.activation(
                        out=sg.t[:], in_=P[2 + nn].t[:, 0:W], func=AF.Silu,
                        scale=sca.t[:, 128 + oo:128 + oo + 1]),
                        reads=[P[2 + nn], sca], writes=[sg])
                    dma_in(kb, szt, za_d[oo * 128:(oo + 1) * 128, col:col + W])
                    dma_in(kb, szt2, zb_d[oo * 128:(oo + 1) * 128, col:col + W])
                    dma_in(kb, xr, x_d[oo * 128:(oo + 1) * 128, col:col + W])
                    kb.do("vector", lambda v: v.tensor_add(out=szt.t[:], in0=szt.t[:], in1=szt2.t[:]),
                          reads=[szt, szt2], writes=[szt])
                    kb.do("vector", lambda v: v.scalar_tensor_tensor(
                        out=szt.t[:], in0=szt.t[:], scalar=1.0 / 16.0, in1=sg.t[:],
                        op0=ALU.mult, op1=ALU.mult),
                        reads=[szt, sg], writes=[szt])
                    kb.do("vector", lambda v: v.tensor_add(out=xr.t[:], in0=xr.t[:], in1=szt.t[:]),
                          reads=[szt, xr], writes=[xr])
                    dma_out(kb, xo_d[oo * 128:(oo + 1) * 128, col:col + W], xr)

        kb.finalize(nc, es)
    return nc




def build_l1():
    nc = bass.Bass("TRN2", target_bir_lowering=False, debug=False)
    x_d = nc.dram_tensor("x", [D, S], F32, kind="ExternalInput").ap()
    wq_d = nc.dram_tensor("wq", [128, DC * 1024], F32, kind="ExternalInput").ap()
    wk_d = nc.dram_tensor("wk", [128, DC * 1024], F32, kind="ExternalInput").ap()
    wv_d = nc.dram_tensor("wv", [128, DC * 1024], F32, kind="ExternalInput").ap()
    wo_d = nc.dram_tensor("wo", [128, DC * 1024], F32, kind="ExternalInput").ap()
    bq_d = nc.dram_tensor("bq", [128, 8], F32, kind="ExternalInput").ap()
    bk_d = nc.dram_tensor("bk", [1, 1024], F32, kind="ExternalInput").ap()
    bv_d = nc.dram_tensor("bv", [1, 1024], F32, kind="ExternalInput").ap()
    bo_d = nc.dram_tensor("bo", [128, 8], F32, kind="ExternalInput").ap()
    stdp_d = nc.dram_tensor("stdp", [1, 16], F32, kind="ExternalInput").ap()
    rw_d = nc.dram_tensor("rw", [128, DC * 16], F32, kind="ExternalInput").ap()
    eb_d = nc.dram_tensor("eb", [1, 16], F32, kind="ExternalInput").ap()
    gw_d = nc.dram_tensor("gw", [16, 128, DC * 512], F32, kind="ExternalInput").ap()
    uw_d = nc.dram_tensor("uw", [16, 128, DC * 512], F32, kind="ExternalInput").ap()
    dw_d = nc.dram_tensor("dw", [16, 128, 4 * 1024], F32, kind="ExternalInput").ap()
    shg_d = nc.dram_tensor("shg", [128, DC * 1024], F32, kind="ExternalInput").ap()
    shu_d = nc.dram_tensor("shu", [128, DC * 1024], F32, kind="ExternalInput").ap()
    shd_d = nc.dram_tensor("shd", [128, DC * 1024], F32, kind="ExternalInput").ap()
    s1sp_d = nc.dram_tensor("s1sp", [D, NT], F32, kind="Internal").ap()
    kvb_d = nc.dram_tensor("kvb", [64, 1024], BF16, kind="Internal").ap()
    xo_d = nc.dram_tensor("xo", [D, NT], F32, kind="ExternalOutput").ap()

    kb = KB()
    es = ExitStack()
    with es:
        def sb(n, sh, dt, stack=None):
            st = stack or es
            return Buf(st.enter_context(nc.sbuf_tensor(n, sh, dt)), n)
        c = mk_consts(nc, es, kb)
        P = [Buf(es.enter_context(nc.psum_tensor(f"P{i}", [128, 512], F32)), f"P{i}") for i in range(5)]
        sc = mk_ln_scratch(nc, es, "ln")
        xn = [sb(f"xn{i}", [128, NT], BF16) for i in range(DC)]
        kv_sb = sb("kv_sb", [128, 1024], BF16)
        ks_sb = sb("ks_sb", [128, 8], BF16)
        comb = sb("comb", [128, 16 * 16], F32)
        ctr = sb("ctr", [16, NT], F32)
        sigB = sb("sigB", [64, 16], F32)
        rowt = sb("rowt", [1, 512], F32)

        def x_ap(ci, col):
            return x_d[ci * 128:(ci + 1) * 128, col:col + W]

        s1 = ExitStack()
        with s1:
            wkb = sb("wkb", [128, DC * 1024], BF16, s1)
            wvb = sb("wvb", [128, DC * 1024], BF16, s1)
            wstg = sb("wstg", [128, 1024], F32, s1)
            bkB = sb("bkB", [128, 1024], F32, s1)
            bvB = sb("bvB", [128, 1024], F32, s1)
            brow = sb("brow", [1, 1024], F32, s1)
            ktile = sb("ktile", [128, 1024], BF16, s1)
            vtile = sb("vtile", [128, 1024], BF16, s1)
            tfa = sb("tfa", [128, 1024], F32, s1)
            tfb = sb("tfb", [128, 1024], F32, s1)
            xnt = [sb(f"xnt{i}", [128, W], BF16, s1) for i in range(DC)]
            Pkv = Buf(es.enter_context(nc.psum_tensor("Pkv", [64, 1024], F32)), "Pkv")
            Pks = Buf(es.enter_context(nc.psum_tensor("Pks", [128, 8], F32)), "Pks")
            for ci in range(DC):
                dma_in(kb, wstg, wk_d[:, ci * 1024:(ci + 1) * 1024])
                cast(kb, wkb, wkb.t[:, ci * 1024:(ci + 1) * 1024], wstg, wstg.t[:])
                dma_in(kb, wstg, wv_d[:, ci * 1024:(ci + 1) * 1024])
                cast(kb, wvb, wvb.t[:, ci * 1024:(ci + 1) * 1024], wstg, wstg.t[:])
            dma_in(kb, brow, bk_d[:, :])
            kb.do("tensor", lambda t: t.matmul(P[4].t[:, 0:512], c["ones_row"].t[:], brow.t[0:1, 0:512],
                                               start=True, stop=True), reads=[c["ones_row"], brow], writes=[P[4]])
            kb.do("vector", lambda v: v.tensor_copy(out=bkB.t[:, 0:512], in_=P[4].t[:, 0:512]),
                  reads=[P[4]], writes=[bkB])
            kb.do("tensor", lambda t: t.matmul(P[4].t[:, 0:512], c["ones_row"].t[:], brow.t[0:1, 512:1024],
                                               start=True, stop=True), reads=[c["ones_row"], brow], writes=[P[4]])
            kb.do("vector", lambda v: v.tensor_copy(out=bkB.t[:, 512:1024], in_=P[4].t[:, 0:512]),
                  reads=[P[4]], writes=[bkB])
            dma_in(kb, brow, bv_d[:, :])
            kb.do("tensor", lambda t: t.matmul(P[4].t[:, 0:512], c["ones_row"].t[:], brow.t[0:1, 0:512],
                                               start=True, stop=True), reads=[c["ones_row"], brow], writes=[P[4]])
            kb.do("vector", lambda v: v.tensor_copy(out=bvB.t[:, 0:512], in_=P[4].t[:, 0:512]),
                  reads=[P[4]], writes=[bvB])
            kb.do("tensor", lambda t: t.matmul(P[4].t[:, 0:512], c["ones_row"].t[:], brow.t[0:1, 512:1024],
                                               start=True, stop=True), reads=[c["ones_row"], brow], writes=[P[4]])
            kb.do("vector", lambda v: v.tensor_copy(out=bvB.t[:, 512:1024], in_=P[4].t[:, 0:512]),
                  reads=[P[4]], writes=[bvB])

            nkt_total = S // 128
            for tci in range(S // W):
                col = tci * W
                emit_ln(kb, c, x_ap, xnt, col, W, P, sc, out_col0=0)
                if tci < NT // W:
                    for ci in range(DC):
                        kb.do("vector", lambda v, ci=ci, col=col: v.tensor_copy(
                            out=xn[ci].t[:, col:col + W], in_=xnt[ci].t[:]),
                            reads=[xnt[ci]], writes=[xn[ci]])
                for j in range(4):
                    kt = tci * 4 + j
                    for wb, bB, dst, tf, do_elu in ((wkb, bkB, ktile, tfa, True), (wvb, bvB, vtile, tfb, False)):
                        for o2 in range(2):
                            for ci in range(DC):
                                kb.do("tensor", lambda t, ci=ci, o2=o2, j=j, wb=wb: t.matmul(
                                    P[4].t[:, 0:W], xnt[ci].t[:, j * 128:(j + 1) * 128],
                                    wb.t[:, ci * 1024 + o2 * 512:ci * 1024 + o2 * 512 + 512],
                                    start=(ci == 0), stop=(ci == DC - 1)),
                                    reads=[xnt[ci], wb], writes=[P[4]], inc=1)
                            if do_elu:
                                kb.do("vector", lambda v, o2=o2, tf=tf, bB=bB: v.tensor_add(
                                    out=tf.t[:, o2 * 512:o2 * 512 + 512], in0=bB.t[:, o2 * 512:o2 * 512 + 512],
                                    in1=P[4].t[:, 0:W]), reads=[P[4], bB], writes=[tf])
                            else:
                                kb.do("vector", lambda v, o2=o2, dst=dst, bB=bB: v.tensor_add(
                                    out=dst.t[:, o2 * 512:o2 * 512 + 512], in0=bB.t[:, o2 * 512:o2 * 512 + 512],
                                    in1=P[4].t[:, 0:W]), reads=[P[4], bB], writes=[dst])
                        if do_elu:
                            kb.do("vector", lambda v, tf=tf: v.tensor_scalar(
                                out=tfb.t[:], in0=tf.t[:], scalar1=0.0, scalar2=None, op0=ALU.min),
                                reads=[tf], writes=[tfb])
                            kb.do("scalar", lambda s: s.activation(out=tfb.t[:], in_=tfb.t[:], func=AF.Exp),
                                  reads=[tfb], writes=[tfb])
                            kb.do("vector", lambda v, tf=tf: v.tensor_scalar(
                                out=tf.t[:], in0=tf.t[:], scalar1=0.0, scalar2=None, op0=ALU.max),
                                reads=[tf], writes=[tf])
                            kb.do("vector", lambda v, tf=tf, dst=dst: v.tensor_add(
                                out=dst.t[:], in0=tf.t[:], in1=tfb.t[:]),
                                reads=[tf, tfb], writes=[dst])
                    for h in range(16):
                        kb.do("tensor", lambda t, h=h, kt=kt: t.matmul(
                            Pkv.t[0:64, h * 64:(h + 1) * 64], ktile.t[:, h * 64:(h + 1) * 64],
                            vtile.t[:, h * 64:(h + 1) * 64], start=(kt == 0), stop=(kt == nkt_total - 1)),
                            reads=[ktile, vtile], writes=[Pkv], inc=1)
                    for oo in range(DC):
                        kb.do("tensor", lambda t, oo=oo, kt=kt: t.matmul(
                            Pks.t[:, oo:oo + 1], ktile.t[:, oo * 128:(oo + 1) * 128],
                            c["ones_bcol"].t[:], start=(kt == 0), stop=(kt == nkt_total - 1)),
                            reads=[ktile, c["ones_bcol"]], writes=[Pks], inc=1)
            kb.do("vector", lambda v: v.tensor_copy(out=kv_sb.t[0:64, :], in_=Pkv.t[0:64, :]),
                  reads=[Pkv], writes=[kv_sb])
            dma_out(kb, kvb_d[:, :], kv_sb, src_sl=kv_sb.t[0:64, :])
            dma_in(kb, kv_sb, kvb_d[:, :], dst_sl=kv_sb.t[64:128, :])
            kb.do("vector", lambda v: v.tensor_copy(out=ks_sb.t[:], in_=Pks.t[:, :]),
                  reads=[Pks], writes=[ks_sb])
            kb.barrier()

        s2 = ExitStack()
        with s2:
            wqb = sb("wqb", [128, DC * 1024], BF16, s2)
            wstg = sb("wstg2", [128, 1024], F32, s2)
            q_sb = sb("q_sb", [128, DC * NT], BF16, s2)
            att = sb("att", [128, DC * NT], BF16, s2)
            bcol = sb("bcol", [128, 8], F32, s2)
            qta = sb("qta2", [128, W], F32, s2)
            qtb = sb("qtb2", [128, W], F32, s2)
            nsb = sb("nsb", [64, W], F32, s2)
            srow = sb("srow1", [1, 16], F32, s2)
            for ci in range(DC):
                dma_in(kb, wstg, wq_d[:, ci * 1024:(ci + 1) * 1024])
                cast(kb, wqb, wqb.t[:, ci * 1024:(ci + 1) * 1024], wstg, wstg.t[:])
            dma_in(kb, bcol, bq_d[:, :])
            dma_in(kb, srow, stdp_d[:, :])
            kb.do("scalar", lambda s: s.activation(out=srow.t[:], in_=srow.t[:], func=AF.Sigmoid),
                  reads=[srow], writes=[srow])
            kb.do("tensor", lambda t: t.matmul(P[4].t[0:64, 0:16], c["ones_row"].t[0:1, 0:64], srow.t[:],
                                               start=True, stop=True), reads=[c["ones_row"], srow], writes=[P[4]])
            kb.do("vector", lambda v: v.tensor_copy(out=sigB.t[:], in_=P[4].t[0:64, 0:16]),
                  reads=[P[4]], writes=[sigB])
            for oo in range(DC):
                for tc in range(NT // W):
                    col = tc * W
                    for ci in range(DC):
                        kb.do("tensor", lambda t, ci=ci, oo=oo, col=col: t.matmul(
                            P[0].t[:, 0:W], wqb.t[:, ci * 1024 + oo * 128:ci * 1024 + oo * 128 + 128],
                            xn[ci].t[:, col:col + W], start=(ci == 0), stop=(ci == DC - 1)),
                            reads=[wqb, xn[ci]], writes=[P[0]], inc=1)
                    kb.do("vector", lambda v, oo=oo: v.tensor_scalar(
                        out=qta.t[:], in0=P[0].t[:, 0:W], scalar1=bcol.t[:, oo:oo + 1], scalar2=None,
                        op0=ALU.add), reads=[P[0], bcol], writes=[qta])
                    kb.do("vector", lambda v: v.tensor_scalar(
                        out=qtb.t[:], in0=qta.t[:], scalar1=0.0, scalar2=None, op0=ALU.min),
                        reads=[qta], writes=[qtb])
                    kb.do("scalar", lambda s: s.activation(out=qtb.t[:], in_=qtb.t[:], func=AF.Exp),
                          reads=[qtb], writes=[qtb])
                    kb.do("vector", lambda v: v.tensor_scalar(
                        out=qta.t[:], in0=qta.t[:], scalar1=0.0, scalar2=None, op0=ALU.max),
                        reads=[qta], writes=[qta])
                    kb.do("vector", lambda v, oo=oo, col=col: v.tensor_add(
                        out=q_sb.t[:, oo * NT + col:oo * NT + col + W], in0=qta.t[:], in1=qtb.t[:]),
                        reads=[qta, qtb], writes=[q_sb])
            for h in range(16):
                hp, ho = h // 2, 64 * (h % 2)
                for tc in range(NT // W):
                    col = tc * W
                    qsl = q_sb.t[ho:ho + 64, hp * NT + col:hp * NT + col + W]
                    kb.do("tensor", lambda t, h=h, ho=ho, qsl=qsl: t.matmul(
                        P[0].t[0:64, 0:W], kv_sb.t[ho:ho + 64, h * 64:(h + 1) * 64], qsl,
                        start=True, stop=True), reads=[kv_sb, q_sb], writes=[P[0]])
                    kb.do("tensor", lambda t, h=h, hp=hp, ho=ho, qsl=qsl: t.matmul(
                        P[1].t[0:1, 0:W], ks_sb.t[ho:ho + 64, hp:hp + 1], qsl,
                        start=True, stop=True), reads=[ks_sb, q_sb], writes=[P[1]])
                    kb.do("vector", lambda v: v.tensor_scalar(
                        out=rowt.t[:], in0=P[1].t[0:1, 0:W], scalar1=1e-6, scalar2=None, op0=ALU.add),
                        reads=[P[1]], writes=[rowt])
                    kb.do("vector", lambda v: v.reciprocal(out=rowt.t[:], in_=rowt.t[:]),
                          reads=[rowt], writes=[rowt])
                    kb.do("tensor", lambda t: t.matmul(P[2].t[0:64, 0:W], c["ones_row"].t[0:1, 0:64],
                                                       rowt.t[:], start=True, stop=True),
                          reads=[c["ones_row"], rowt], writes=[P[2]])
                    kb.do("scalar", lambda s: s.copy(out=nsb.t[:], in_=P[0].t[0:64, 0:W]),
                          reads=[P[0]], writes=[nsb])
                    kb.do("vector", lambda v, h=h, hp=hp, ho=ho, col=col: v.scalar_tensor_tensor(
                        out=att.t[ho:ho + 64, hp * NT + col:hp * NT + col + W], in0=nsb.t[:],
                        scalar=sigB.t[0:64, h:h + 1], in1=P[2].t[0:64, 0:W],
                        op0=ALU.mult, op1=ALU.mult), reads=[nsb, sigB, P[2]], writes=[att])
            for ci in range(DC):
                dma_in(kb, wstg, wo_d[:, ci * 1024:(ci + 1) * 1024])
                cast(kb, wqb, wqb.t[:, ci * 1024:(ci + 1) * 1024], wstg, wstg.t[:])
            dma_in(kb, bcol, bo_d[:, :])
            for oo in range(DC):
                for tc in range(NT // W):
                    col = tc * W
                    for ci in range(DC):
                        kb.do("tensor", lambda t, ci=ci, oo=oo, col=col: t.matmul(
                            P[0].t[:, 0:W], wqb.t[:, ci * 1024 + oo * 128:ci * 1024 + oo * 128 + 128],
                            att.t[:, ci * NT + col:ci * NT + col + W], start=(ci == 0), stop=(ci == DC - 1)),
                            reads=[wqb, att], writes=[P[0]], inc=1)
                    kb.do("vector", lambda v, oo=oo: v.tensor_scalar(
                        out=qta.t[:], in0=P[0].t[:, 0:W], scalar1=bcol.t[:, oo:oo + 1], scalar2=None,
                        op0=ALU.add), reads=[P[0], bcol], writes=[qta])
                    dma_out(kb, s1sp_d[oo * 128:(oo + 1) * 128, col:col + W], qta)
            kb.barrier()

        s3 = ExitStack()
        with s3:
            racc = [sb(f"racc{i}", [128, NT], F32, s3) for i in range(DC)]
            wstg = sb("wstg3", [128, 1024], F32, s3)
            rwb = sb("rwb", [128, DC * 16], F32, s3)
            rwbb = sb("rwbb", [128, DC * 16], BF16, s3)
            ebB = sb("ebB", [128, 16], F32, s3)
            erow = sb("erow", [1, 16], F32, s3)
            iot = sb("iot", [128, 16], F32, s3)
            iot99 = sb("iot99", [128, 16], F32, s3)
            sct = sb("sct", [128, 16], F32, s3)
            scs = sb("scs", [128, 16], F32, s3)
            cur = sb("cur", [128, 16], F32, s3)
            tmp = sb("tmp", [128, 16], F32, s3)
            tmp2 = sb("tmp2", [128, 16], F32, s3)
            soh = sb("soh", [128, 16], F32, s3)
            hi = sb("hi", [128, 8], F32, s3)
            lo = sb("lo", [128, 8], F32, s3)
            gsb = sb("gsb", [128, 4], F32, s3)
            col1 = sb("col1", [128, 1], F32, s3)
            col2 = sb("col2", [128, 1], F32, s3)
            col3 = sb("col3", [128, 1], F32, s3)
            tsum = sb("tsum", [128, 1], F32, s3)
            tw = sb("tw", [128, 1], F32, s3)
            t1 = sb("t1l", [128, W], F32, s3)
            t2 = sb("t2l", [128, W], F32, s3)
            xrr = sb("xrr", [128, W], F32, s3)

            dma_in(kb, rwb, rw_d[:, :])
            cast(kb, rwbb, rwbb.t[:], rwb, rwb.t[:])
            dma_in(kb, erow, eb_d[:, :])
            kb.do("tensor", lambda t: t.matmul(P[4].t[:, 0:16], c["ones_row"].t[:], erow.t[:],
                                               start=True, stop=True), reads=[c["ones_row"], erow], writes=[P[4]])
            kb.do("vector", lambda v: v.tensor_copy(out=ebB.t[:], in_=P[4].t[:, 0:16]),
                  reads=[P[4]], writes=[ebB])
            kb.do("gpsimd", lambda g: g.iota(iot.t[:], pattern=[[1, 16]], base=0, channel_multiplier=0,
                                             allow_small_or_imprecise_dtypes=True), writes=[iot])
            kb.do("gpsimd", lambda g: g.iota(iot99.t[:], pattern=[[1, 16]], base=-99, channel_multiplier=0,
                                             allow_small_or_imprecise_dtypes=True), writes=[iot99])

            for tt in range(NT // 128):
                for ci in range(DC):
                    kb.do("tensor", lambda t, ci=ci, tt=tt: t.matmul(
                        P[0].t[:, 0:16], xn[ci].t[:, tt * 128:(tt + 1) * 128],
                        rwbb.t[:, ci * 16:(ci + 1) * 16], start=(ci == 0), stop=(ci == DC - 1)),
                        reads=[xn[ci], rwbb], writes=[P[0]], inc=1)
                kb.do("scalar", lambda s: s.activation(out=scs.t[:], in_=P[0].t[:, 0:16], func=AF.Sigmoid),
                      reads=[P[0]], writes=[scs])
                kb.do("vector", lambda v: v.tensor_add(out=sct.t[:], in0=scs.t[:], in1=ebB.t[:]),
                      reads=[scs, ebB], writes=[sct])
                kb.do("vector", lambda v: v.tensor_tensor(out=hi.t[:], in0=sct.t[:, 0:15:2],
                                                          in1=sct.t[:, 1:16:2], op=ALU.max),
                      reads=[sct], writes=[hi])
                kb.do("vector", lambda v: v.tensor_tensor(out=lo.t[:], in0=sct.t[:, 0:15:2],
                                                          in1=sct.t[:, 1:16:2], op=ALU.min),
                      reads=[sct], writes=[lo])
                kb.do("vector", lambda v: v.tensor_tensor(out=gsb.t[:], in0=hi.t[:, 0:7:2],
                                                          in1=hi.t[:, 1:8:2], op=ALU.max),
                      reads=[hi], writes=[gsb])
                kb.do("vector", lambda v: v.tensor_tensor(out=tmp.t[:, 0:4], in0=hi.t[:, 0:7:2],
                                                          in1=hi.t[:, 1:8:2], op=ALU.min),
                      reads=[hi], writes=[tmp])
                kb.do("vector", lambda v: v.tensor_tensor(out=tmp.t[:, 4:8], in0=lo.t[:, 0:7:2],
                                                          in1=lo.t[:, 1:8:2], op=ALU.max),
                      reads=[lo], writes=[tmp])
                kb.do("vector", lambda v: v.tensor_tensor(out=tmp.t[:, 0:4], in0=tmp.t[:, 0:4],
                                                          in1=tmp.t[:, 4:8], op=ALU.max),
                      reads=[tmp], writes=[tmp])
                kb.do("vector", lambda v: v.tensor_add(out=gsb.t[:], in0=gsb.t[:], in1=tmp.t[:, 0:4]),
                      reads=[tmp, gsb], writes=[gsb])
                kb.do("vector", lambda v: v.tensor_tensor(out=hi.t[:, 0:2], in0=gsb.t[:, 0:3:2],
                                                          in1=gsb.t[:, 1:4:2], op=ALU.max),
                      reads=[gsb], writes=[hi])
                kb.do("vector", lambda v: v.tensor_tensor(out=lo.t[:, 0:2], in0=gsb.t[:, 0:3:2],
                                                          in1=gsb.t[:, 1:4:2], op=ALU.min),
                      reads=[gsb], writes=[lo])
                kb.do("vector", lambda v: v.tensor_tensor(out=col1.t[:], in0=hi.t[:, 0:1],
                                                          in1=hi.t[:, 1:2], op=ALU.min),
                      reads=[hi], writes=[col1])
                kb.do("vector", lambda v: v.tensor_tensor(out=col2.t[:], in0=lo.t[:, 0:1],
                                                          in1=lo.t[:, 1:2], op=ALU.max),
                      reads=[lo], writes=[col2])
                kb.do("vector", lambda v: v.tensor_tensor(out=col1.t[:], in0=col1.t[:],
                                                          in1=col2.t[:], op=ALU.max),
                      reads=[col1, col2], writes=[col1])
                kb.do("vector", lambda v: v.tensor_scalar(out=gsb.t[:], in0=gsb.t[:],
                                                          scalar1=col1.t[:, 0:1], scalar2=None, op0=ALU.is_ge),
                      reads=[gsb, col1], writes=[gsb])
                for g in range(4):
                    kb.do("vector", lambda v, g=g: v.tensor_scalar(
                        out=cur.t[:, g * 4:(g + 1) * 4], in0=sct.t[:, g * 4:(g + 1) * 4],
                        scalar1=gsb.t[:, g:g + 1], scalar2=None, op0=ALU.mult),
                        reads=[sct, gsb], writes=[cur])
                kb.do("vector", lambda v: v.memset(tsum.t[:], 1e-20), writes=[tsum])
                kb.do("vector", lambda v, tt=tt: v.memset(comb.t[:, tt * 16:(tt + 1) * 16], 0.0),
                      writes=[comb])
                for j in range(4):
                    kb.do("vector", lambda v: v.reduce_max(col1.t[:], cur.t[:], axis=AX.X),
                          reads=[cur], writes=[col1])
                    kb.do("vector", lambda v: v.tensor_scalar(out=tmp.t[:], in0=cur.t[:],
                                                              scalar1=col1.t[:, 0:1], scalar2=None, op0=ALU.is_ge),
                          reads=[cur, col1], writes=[tmp])
                    kb.do("vector", lambda v: v.tensor_mul(out=tmp.t[:], in0=tmp.t[:], in1=iot99.t[:]),
                          reads=[tmp, iot99], writes=[tmp])
                    kb.do("vector", lambda v: v.tensor_scalar(out=tmp.t[:], in0=tmp.t[:],
                                                              scalar1=99.0, scalar2=None, op0=ALU.add),
                          reads=[tmp], writes=[tmp])
                    kb.do("vector", lambda v: v.tensor_reduce(out=col2.t[:], in_=tmp.t[:], axis=AX.X,
                                                              op=ALU.min),
                          reads=[tmp], writes=[col2])
                    kb.do("vector", lambda v: v.tensor_scalar(out=soh.t[:], in0=iot.t[:],
                                                              scalar1=col2.t[:, 0:1], scalar2=None, op0=ALU.is_equal),
                          reads=[iot, col2], writes=[soh])
                    kb.do("vector", lambda v: v.tensor_mul(out=tmp2.t[:], in0=soh.t[:], in1=scs.t[:]),
                          reads=[soh, scs], writes=[tmp2])
                    kb.do("vector", lambda v: v.reduce_sum(tw.t[:], tmp2.t[:], axis=AX.X),
                          reads=[tmp2], writes=[tw])
                    kb.do("vector", lambda v: v.tensor_add(out=tsum.t[:], in0=tsum.t[:], in1=tw.t[:]),
                          reads=[tw, tsum], writes=[tsum])
                    kb.do("vector", lambda v, tt=tt: v.scalar_tensor_tensor(
                        out=comb.t[:, tt * 16:(tt + 1) * 16], in0=soh.t[:], scalar=tw.t[:, 0:1],
                        in1=comb.t[:, tt * 16:(tt + 1) * 16], op0=ALU.mult, op1=ALU.add),
                        reads=[soh, tw, comb], writes=[comb])
                    kb.do("vector", lambda v: v.scalar_tensor_tensor(
                        out=cur.t[:], in0=soh.t[:], scalar=-1e9, in1=cur.t[:], op0=ALU.mult, op1=ALU.add),
                        reads=[soh, cur], writes=[cur])
                kb.do("vector", lambda v: v.reciprocal(out=tsum.t[:], in_=tsum.t[:]),
                      reads=[tsum], writes=[tsum])
                kb.do("vector", lambda v: v.tensor_scalar(out=tsum.t[:], in0=tsum.t[:],
                                                          scalar1=2.5, scalar2=None, op0=ALU.mult),
                      reads=[tsum], writes=[tsum])
                kb.do("vector", lambda v, tt=tt: v.tensor_scalar(
                    out=comb.t[:, tt * 16:(tt + 1) * 16], in0=comb.t[:, tt * 16:(tt + 1) * 16],
                    scalar1=tsum.t[:, 0:1], scalar2=None, op0=ALU.mult),
                    reads=[comb, tsum], writes=[comb])
                kb.do("tensor", lambda t, tt=tt: t.transpose(P[3].t[0:16, 0:128],
                                                             comb.t[:, tt * 16:(tt + 1) * 16], c["ident"].t[:]),
                      reads=[comb, c["ident"]], writes=[P[3]])
                kb.do("vector", lambda v, tt=tt: v.tensor_copy(out=ctr.t[0:16, tt * 128:(tt + 1) * 128],
                                                               in_=P[3].t[0:16, 0:128]),
                      reads=[P[3]], writes=[ctr])

            s3b = ExitStack()
            s3b.__enter__()
            shw = sb("shw", [128, DC * 1024], BF16, s3b)
            hs = sb("hs", [128, DC * NT], BF16, s3b)
            for ci in range(DC):
                dma_in(kb, wstg, shg_d[:, ci * 1024:(ci + 1) * 1024])
                cast(kb, shw, shw.t[:, ci * 1024:(ci + 1) * 1024], wstg, wstg.t[:])
            for oo in range(DC):
                for tc in range(NT // W):
                    col = tc * W
                    for ci in range(DC):
                        kb.do("tensor", lambda t, ci=ci, oo=oo, col=col: t.matmul(
                            P[0].t[:, 0:W], shw.t[:, ci * 1024 + oo * 128:ci * 1024 + oo * 128 + 128],
                            xn[ci].t[:, col:col + W], start=(ci == 0), stop=(ci == DC - 1)),
                            reads=[shw, xn[ci]], writes=[P[0]], inc=1)
                    kb.do("scalar", lambda s, oo=oo, col=col: s.activation(
                        out=hs.t[:, oo * NT + col:oo * NT + col + W], in_=P[0].t[:, 0:W], func=AF.Silu),
                        reads=[P[0]], writes=[hs])
            for ci in range(DC):
                dma_in(kb, wstg, shu_d[:, ci * 1024:(ci + 1) * 1024])
                cast(kb, shw, shw.t[:, ci * 1024:(ci + 1) * 1024], wstg, wstg.t[:])
            for oo in range(DC):
                for tc in range(NT // W):
                    col = tc * W
                    for ci in range(DC):
                        kb.do("tensor", lambda t, ci=ci, oo=oo, col=col: t.matmul(
                            P[0].t[:, 0:W], shw.t[:, ci * 1024 + oo * 128:ci * 1024 + oo * 128 + 128],
                            xn[ci].t[:, col:col + W], start=(ci == 0), stop=(ci == DC - 1)),
                            reads=[shw, xn[ci]], writes=[P[0]], inc=1)
                    kb.do("vector", lambda v, oo=oo, col=col: v.tensor_mul(
                        out=hs.t[:, oo * NT + col:oo * NT + col + W],
                        in0=hs.t[:, oo * NT + col:oo * NT + col + W], in1=P[0].t[:, 0:W]),
                        reads=[P[0], hs], writes=[hs])
            for ci in range(DC):
                dma_in(kb, wstg, shd_d[:, ci * 1024:(ci + 1) * 1024])
                cast(kb, shw, shw.t[:, ci * 1024:(ci + 1) * 1024], wstg, wstg.t[:])
            for oo in range(DC):
                for tc in range(NT // W):
                    col = tc * W
                    for ci in range(DC):
                        kb.do("tensor", lambda t, ci=ci, oo=oo, col=col: t.matmul(
                            P[0].t[:, 0:W], shw.t[:, ci * 1024 + oo * 128:ci * 1024 + oo * 128 + 128],
                            hs.t[:, ci * NT + col:ci * NT + col + W], start=(ci == 0), stop=(ci == DC - 1)),
                            reads=[shw, hs], writes=[P[0]], inc=1)
                    kb.do("scalar", lambda s, oo=oo, col=col: s.copy(
                        out=racc[oo].t[:, col:col + W], in_=P[0].t[:, 0:W]),
                        reads=[P[0]], writes=[racc[oo]])

            kb.barrier()
            s3b.__exit__(None, None, None)

            s3c = ExitStack()
            s3c.__enter__()
            h_all = sb("h_all", [128, 4 * NT], BF16, s3c)
            gwb = sb("gwb", [128, DC * 512], BF16, s3c)
            uwb = sb("uwb", [128, DC * 512], BF16, s3c)
            dwb = sb("dwb", [128, 4 * 1024], BF16, s3c)
            oneh = sb("oneh", [16, 16 * 128], F32, s3c)
            kb.do("gpsimd", lambda g: g.iota(oneh.t[:], pattern=[[1, 16], [0, 128]], base=0,
                                             channel_multiplier=-1, allow_small_or_imprecise_dtypes=True),
                  writes=[oneh])
            kb.do("vector", lambda v: v.tensor_scalar(out=oneh.t[:], in0=oneh.t[:], scalar1=0.0,
                                                      scalar2=None, op0=ALU.is_equal),
                  reads=[oneh], writes=[oneh])
            for e in range(16):
                for wd, wb, ncol in ((gw_d, gwb, DC * 512), (uw_d, uwb, DC * 512), (dw_d, dwb, 4 * 1024)):
                    for g4 in range(ncol // 1024):
                        kb.do("sync", lambda eng, wd=wd, e=e, g4=g4: eng.dma_start(
                            out=wstg.t[:], in_=wd[e, :, g4 * 1024:(g4 + 1) * 1024]),
                            writes=[wstg], inc=16, sem_key="ld_" + wstg.name)
                        cast(kb, wb, wb.t[:, g4 * 1024:(g4 + 1) * 1024], wstg, wstg.t[:])
                for tc in range(NT // W):
                    col = tc * W
                    kb.do("tensor", lambda t, e=e, col=col: t.matmul(
                        P[4].t[:, 0:W], oneh.t[0:16, e * 128:(e + 1) * 128],
                        ctr.t[0:16, col:col + W], start=True, stop=True),
                        reads=[oneh, ctr], writes=[P[4]])
                    for io in range(4):
                        for ci in range(DC):
                            kb.do("tensor", lambda t, ci=ci, io=io, col=col: t.matmul(
                                P[0].t[:, 0:W], gwb.t[:, ci * 512 + io * 128:ci * 512 + io * 128 + 128],
                                xn[ci].t[:, col:col + W], start=(ci == 0), stop=(ci == DC - 1)),
                                reads=[gwb, xn[ci]], writes=[P[0]], inc=1)
                            kb.do("tensor", lambda t, ci=ci, io=io, col=col: t.matmul(
                                P[1].t[:, 0:W], uwb.t[:, ci * 512 + io * 128:ci * 512 + io * 128 + 128],
                                xn[ci].t[:, col:col + W], start=(ci == 0), stop=(ci == DC - 1)),
                                reads=[uwb, xn[ci]], writes=[P[1]], inc=1)
                        kb.do("scalar", lambda s: s.activation(out=t1.t[:], in_=P[0].t[:, 0:W], func=AF.Silu),
                              reads=[P[0]], writes=[t1])
                        kb.do("vector", lambda v: v.tensor_mul(out=t2.t[:], in0=t1.t[:], in1=P[1].t[:, 0:W]),
                              reads=[t1, P[1]], writes=[t2])
                        kb.do("vector", lambda v, io=io, col=col: v.tensor_mul(
                            out=h_all.t[:, io * NT + col:io * NT + col + W], in0=t2.t[:], in1=P[4].t[:, 0:W]),
                            reads=[t2, P[4]], writes=[h_all])
                    for oo in range(DC):
                        for io in range(4):
                            kb.do("tensor", lambda t, io=io, oo=oo, col=col: t.matmul(
                                P[2].t[:, 0:W], dwb.t[:, io * 1024 + oo * 128:io * 1024 + oo * 128 + 128],
                                h_all.t[:, io * NT + col:io * NT + col + W],
                                start=(io == 0), stop=(io == 3)),
                                reads=[dwb, h_all], writes=[P[2]], inc=1)
                        kb.do("vector", lambda v, oo=oo, col=col: v.tensor_add(
                            out=racc[oo].t[:, col:col + W], in0=racc[oo].t[:, col:col + W],
                            in1=P[2].t[:, 0:W]), reads=[P[2], racc[oo]], writes=[racc[oo]])

            for oo in range(DC):
                for tc in range(NT // W):
                    col = tc * W
                    kb.do("scalar", lambda s, oo=oo, col=col: s.activation(
                        out=t1.t[:], in_=racc[oo].t[:, col:col + W], func=AF.Silu),
                        reads=[racc[oo]], writes=[t1])
                    dma_in(kb, t2, s1sp_d[oo * 128:(oo + 1) * 128, col:col + W])
                    dma_in(kb, xrr, x_d[oo * 128:(oo + 1) * 128, col:col + W])
                    kb.do("vector", lambda v: v.tensor_mul(out=t1.t[:], in0=t1.t[:], in1=t2.t[:]),
                          reads=[t1, t2], writes=[t1])
                    kb.do("vector", lambda v: v.tensor_add(out=xrr.t[:], in0=xrr.t[:], in1=t1.t[:]),
                          reads=[t1, xrr], writes=[xrr])
                    dma_out(kb, xo_d[oo * 128:(oo + 1) * 128, col:col + W], xrr)
            kb.barrier()
            s3c.__exit__(None, None, None)
        kb.finalize(nc, es)
    return nc



def _fm(x):
    return np.ascontiguousarray(np.transpose(np.asarray(x), (0, 2, 1)))


def _prep_l0a(inputs):
    Q = np.asarray(inputs["L0_attn_Q"])
    K = np.asarray(inputs["L0_attn_K"])
    V = np.asarray(inputs["L0_attn_V"])
    Wt = np.asarray(inputs["L0_attn_W"])
    Qm = np.ascontiguousarray(np.transpose(Q, (1, 2, 0)).reshape(D, 1024))
    Km = np.ascontiguousarray(np.transpose(K, (1, 2, 0)).reshape(D, 1024))
    Vm = np.ascontiguousarray(np.transpose(V, (1, 2, 0)).reshape(D, 1024))
    Wm = np.ascontiguousarray(np.transpose(Wt, (1, 2, 0)).reshape(1024, 1024))
    ww_all = np.ascontiguousarray(np.transpose(Wt, (2, 1, 0)))
    out = []
    for hh in range(2):
        cols = slice(hh * 512, (hh + 1) * 512)
        out.append({
            "qw": np.ascontiguousarray(Qm[:, cols].reshape(DC, 128, 512)),
            "kw": np.ascontiguousarray(Km[:, cols].reshape(DC, 128, 512)),
            "vw": np.ascontiguousarray(Vm[:, cols].reshape(DC, 128, 512)),
            "ww": np.ascontiguousarray(ww_all[hh * 8:(hh + 1) * 8]),
        })
    return out


def _chunk_w(wT, ncols):
    din, dout = wT.shape
    nch = din // 128
    return np.ascontiguousarray(wT.reshape(nch, 128, dout).transpose(1, 0, 2).reshape(128, nch * dout))


def _prep_l0b(inputs):
    w1T = np.ascontiguousarray(np.asarray(inputs["L0_ffn_w1_w"]).T)
    w2T = np.ascontiguousarray(np.asarray(inputs["L0_ffn_w2_w"]).T)
    h1T = np.ascontiguousarray(np.asarray(inputs["L0_ffn_h1_w"]).T)
    h2T = np.ascontiguousarray(np.asarray(inputs["L0_ffn_h2_w"]).T)
    w3T = np.ascontiguousarray(np.asarray(inputs["L0_ffn_w3_w"]).T)
    d = {
        "w1": _chunk_w(w1T, 4096), "w2": _chunk_w(w2T, 4096),
        "h1": _chunk_w(h1T, 4096), "h2": _chunk_w(h2T, 4096),
        "w3": _chunk_w(w3T, 1024),
        "s1": np.ascontiguousarray(np.asarray(inputs["L0_ffn_w1_s"]).reshape(32, 128).T),
        "s2": np.ascontiguousarray(np.asarray(inputs["L0_ffn_w2_s"]).reshape(32, 128).T),
        "sh1": np.ascontiguousarray(np.asarray(inputs["L0_ffn_h1_s"]).reshape(32, 128).T),
        "sh2": np.ascontiguousarray(np.asarray(inputs["L0_ffn_h2_s"]).reshape(32, 128).T),
        "s3": np.ascontiguousarray(np.asarray(inputs["L0_ffn_w3_s"]).reshape(8, 128).T),
    }
    return d


def run_l0(X0fm, inputs):
    heads = _prep_l0a(inputs)
    nc_a = build_l0a()
    in_maps = []
    for core in range(8):
        b, hh = core // 2, core % 2
        m = {"x": X0fm[b]}
        m.update(heads[hh])
        in_maps.append(m)
    res = run_bass_kernel_spmd(nc_a, in_maps, core_ids=list(range(8)))
    zparts = [res.results[c]["z"] for c in range(8)]

    ffn = _prep_l0b(inputs)
    nc_b = build_l0b()
    in_maps = []
    for core in range(8):
        b, half = core // 2, core % 2
        colsl = slice(half * NT, (half + 1) * NT)
        m = {"x": np.ascontiguousarray(X0fm[b][:, colsl]),
             "za": np.ascontiguousarray(zparts[2 * b][:, colsl]),
             "zb": np.ascontiguousarray(zparts[2 * b + 1][:, colsl])}
        m.update(ffn)
        in_maps.append(m)
    res = run_bass_kernel_spmd(nc_b, in_maps, core_ids=list(range(8)))
    X1 = np.empty((B, D, S), np.float32)
    for core in range(8):
        b, half = core // 2, core % 2
        X1[b][:, half * NT:(half + 1) * NT] = res.results[core]["xo"]
    return X1



def build_l2():
    nc = bass.Bass("TRN2", target_bir_lowering=False, debug=False)
    x_d = nc.dram_tensor("x", [D, S], F32, kind="ExternalInput").ap()
    wq_d = nc.dram_tensor("wq", [128, DC * 1024], F32, kind="ExternalInput").ap()
    wk_d = nc.dram_tensor("wk", [128, DC * 1024], F32, kind="ExternalInput").ap()
    wv_d = nc.dram_tensor("wv", [128, DC * 1024], F32, kind="ExternalInput").ap()
    wo_d = nc.dram_tensor("wo", [128, DC * 1024], F32, kind="ExternalInput").ap()
    bq_d = nc.dram_tensor("bq", [128, 8], F32, kind="ExternalInput").ap()
    bk_d = nc.dram_tensor("bk", [128, 8], F32, kind="ExternalInput").ap()
    bv_d = nc.dram_tensor("bv", [1, 1024], F32, kind="ExternalInput").ap()
    bo_d = nc.dram_tensor("bo", [128, 8], F32, kind="ExternalInput").ap()
    rm_d = nc.dram_tensor("rm", [64, 256], F32, kind="ExternalInput").ap()
    wr_d = nc.dram_tensor("wr", [128, DC * 1024], F32, kind="ExternalInput").ap()
    wk2_d = nc.dram_tensor("wk2", [128, DC * 1024], F32, kind="ExternalInput").ap()
    wv2_d = nc.dram_tensor("wv2", [128, DC * 1024], F32, kind="ExternalInput").ap()
    mur_d = nc.dram_tensor("mur", [128, 8], F32, kind="ExternalInput").ap()
    muk_d = nc.dram_tensor("muk", [128, 8], F32, kind="ExternalInput").ap()
    pm_d = nc.dram_tensor("pm", [1, 1], F32, kind="ExternalInput").ap()
    s2sp_d = nc.dram_tensor("s2sp", [D, NT], F32, kind="Internal").ap()
    xo_d = nc.dram_tensor("xo", [D, NT], F32, kind="ExternalOutput").ap()

    kb = KB()
    es = ExitStack()
    with es:
        def sb(n, sh, dt, stack=None):
            st = stack or es
            return Buf(st.enter_context(nc.sbuf_tensor(n, sh, dt)), n)
        c = mk_consts(nc, es, kb)
        P = [Buf(es.enter_context(nc.psum_tensor(f"P{i}", [128, 512], F32)), f"P{i}") for i in range(3)]
        sc = mk_ln_scratch(nc, es, "ln")
        xn = [sb(f"xn{i}", [128, NT], BF16) for i in range(DC)]
        xn_last = sb("xn_last", [128, 8], BF16)
        kv_sb = [sb(f"kv2_{i}", [128, 1024], BF16) for i in range(2)]
        ks_sb = sb("ks2", [128, 32], BF16)
        rmb = sb("rmb", [128, 256], BF16)
        rmf = sb("rmf", [64, 256], F32)
        rowt = sb("rowt2", [1, 512], F32)

        dma_in(kb, rmf, rm_d[:, :])
        cast(kb, rmb, rmb.t[0:64, :], rmf, rmf.t[:])
        cast(kb, rmb, rmb.t[64:128, :], rmf, rmf.t[:])

        def x_ap(ci, col):
            return x_d[ci * 128:(ci + 1) * 128, col:col + W]

        s1 = ExitStack()
        with s1:
            Pkv = [Buf(es.enter_context(nc.psum_tensor(f"Pkv2_{i}", [128, 1024], F32)), f"Pkv2_{i}")
                   for i in range(2)]
            Pks = Buf(es.enter_context(nc.psum_tensor("Pks2", [128, 32], F32)), "Pks2")
            wkb = sb("wkb", [128, DC * 1024], BF16, s1)
            wvb = sb("wvb", [128, DC * 1024], BF16, s1)
            wstg = sb("wstgA", [128, 1024], F32, s1)
            bkc = sb("bkc", [128, 8], F32, s1)
            bvB = sb("bvB", [128, 1024], F32, s1)
            brow = sb("browA", [1, 1024], F32, s1)
            kfm = sb("kfm", [128, DC * 512], BF16, s1)
            vtm = sb("vtm", [128, 4 * 1024], BF16, s1)
            kp = sb("kp", [128, 256], BF16, s1)
            fta = sb("ftaA", [128, 256], F32, s1)
            ftb = sb("ftbA", [128, 256], F32, s1)
            xnt = [sb(f"xntA{i}", [128, W], BF16, s1) for i in range(DC)]
            for ci in range(DC):
                dma_in(kb, wstg, wk_d[:, ci * 1024:(ci + 1) * 1024])
                cast(kb, wkb, wkb.t[:, ci * 1024:(ci + 1) * 1024], wstg, wstg.t[:])
                dma_in(kb, wstg, wv_d[:, ci * 1024:(ci + 1) * 1024])
                cast(kb, wvb, wvb.t[:, ci * 1024:(ci + 1) * 1024], wstg, wstg.t[:])
            dma_in(kb, bkc, bk_d[:, :])
            dma_in(kb, brow, bv_d[:, :])
            for o2 in range(2):
                kb.do("tensor", lambda t, o2=o2: t.matmul(P[2].t[:, 0:512], c["ones_row"].t[:],
                                                          brow.t[0:1, o2 * 512:(o2 + 1) * 512],
                                                          start=True, stop=True),
                      reads=[c["ones_row"], brow], writes=[P[2]])
                kb.do("vector", lambda v, o2=o2: v.tensor_copy(out=bvB.t[:, o2 * 512:(o2 + 1) * 512],
                                                               in_=P[2].t[:, 0:512]),
                      reads=[P[2]], writes=[bvB])
            for tci in range(S // W):
                col = tci * W
                emit_ln(kb, c, x_ap, xnt, col, W, P, sc, out_col0=0)
                if tci < NT // W:
                    for ci in range(DC):
                        kb.do("vector", lambda v, ci=ci, col=col: v.tensor_copy(
                            out=xn[ci].t[:, col:col + W], in_=xnt[ci].t[:]),
                            reads=[xnt[ci]], writes=[xn[ci]])
                if tci == S // W - 1:
                    for ci in range(DC):
                        kb.do("vector", lambda v, ci=ci: v.tensor_copy(
                            out=xn_last.t[:, ci:ci + 1], in_=xnt[ci].t[:, W - 1:W]),
                            reads=[xnt[ci]], writes=[xn_last])
                for oo in range(DC):
                    for ci in range(DC):
                        kb.do("tensor", lambda t, ci=ci, oo=oo: t.matmul(
                            P[2].t[:, 0:W], wkb.t[:, ci * 1024 + oo * 128:ci * 1024 + oo * 128 + 128],
                            xnt[ci].t[:], start=(ci == 0), stop=(ci == DC - 1)),
                            reads=[wkb, xnt[ci]], writes=[P[2]], inc=1)
                    kb.do("vector", lambda v, oo=oo: v.tensor_scalar(
                        out=kfm.t[:, oo * W:(oo + 1) * W], in0=P[2].t[:, 0:W],
                        scalar1=bkc.t[:, oo:oo + 1], scalar2=None, op0=ALU.add),
                        reads=[P[2], bkc], writes=[kfm])
                for j in range(4):
                    for o2 in range(2):
                        for ci in range(DC):
                            kb.do("tensor", lambda t, ci=ci, j=j, o2=o2: t.matmul(
                                P[2].t[:, 0:W], xnt[ci].t[:, j * 128:(j + 1) * 128],
                                wvb.t[:, ci * 1024 + o2 * 512:ci * 1024 + o2 * 512 + 512],
                                start=(ci == 0), stop=(ci == DC - 1)),
                                reads=[xnt[ci], wvb], writes=[P[2]], inc=1)
                        kb.do("vector", lambda v, j=j, o2=o2: v.tensor_add(
                            out=vtm.t[:, j * 1024 + o2 * 512:j * 1024 + o2 * 512 + 512],
                            in0=bvB.t[:, o2 * 512:(o2 + 1) * 512], in1=P[2].t[:, 0:W]),
                            reads=[P[2], bvB], writes=[vtm])
                for h in range(16):
                    hp, ho = h // 2, 64 * (h % 2)
                    for j in range(4):
                        kt_first = (tci == 0 and j == 0)
                        kt_last = (tci == S // W - 1 and j == 3)
                        kb.do("tensor", lambda t, h=h, hp=hp, ho=ho, j=j: t.matmul(
                            P[2].t[:, 0:256], kfm.t[ho:ho + 64, hp * W + j * 128:hp * W + j * 128 + 128],
                            rmb.t[ho:ho + 64, :], start=True, stop=True),
                            reads=[kfm, rmb], writes=[P[2]])
                        kb.do("vector", lambda v: v.tensor_scalar(
                            out=fta.t[:], in0=P[2].t[:, 0:256], scalar1=0.0, scalar2=None, op0=ALU.min),
                            reads=[P[2]], writes=[fta])
                        kb.do("scalar", lambda s: s.activation(out=fta.t[:], in_=fta.t[:], func=AF.Exp),
                              reads=[fta], writes=[fta])
                        kb.do("vector", lambda v: v.tensor_scalar(
                            out=ftb.t[:], in0=P[2].t[:, 0:256], scalar1=0.0, scalar2=None, op0=ALU.max),
                            reads=[P[2]], writes=[ftb])
                        kb.do("vector", lambda v: v.tensor_add(out=fta.t[:], in0=fta.t[:], in1=ftb.t[:]),
                              reads=[ftb, fta], writes=[fta])
                        kb.do("vector", lambda v: v.tensor_scalar(
                            out=kp.t[:], in0=fta.t[:], scalar1=0.0625, scalar2=None, op0=ALU.mult),
                            reads=[fta], writes=[kp])
                        for mc in range(2):
                            kb.do("tensor", lambda t, mc=mc, h=h, j=j, kt_first=kt_first, kt_last=kt_last: t.matmul(
                                Pkv[mc].t[:, h * 64:(h + 1) * 64], kp.t[:, mc * 128:(mc + 1) * 128],
                                vtm.t[:, j * 1024 + h * 64:j * 1024 + h * 64 + 64],
                                start=kt_first, stop=kt_last),
                                reads=[kp, vtm], writes=[Pkv[mc]], inc=1)
                            kb.do("tensor", lambda t, mc=mc, h=h, kt_first=kt_first, kt_last=kt_last: t.matmul(
                                Pks.t[:, mc * 16 + h:mc * 16 + h + 1], kp.t[:, mc * 128:(mc + 1) * 128],
                                c["ones_bcol"].t[:], start=kt_first, stop=kt_last),
                                reads=[kp, c["ones_bcol"]], writes=[Pks], inc=1)
            for mc in range(2):
                kb.do("vector", lambda v, mc=mc: v.tensor_copy(out=kv_sb[mc].t[:], in_=Pkv[mc].t[:, :]),
                      reads=[Pkv[mc]], writes=[kv_sb[mc]])
            kb.do("vector", lambda v: v.tensor_copy(out=ks_sb.t[:], in_=Pks.t[:, :]),
                  reads=[Pks], writes=[ks_sb])
            kb.barrier()

        s2s = ExitStack()
        with s2s:
            wqb = sb("wqbC", [128, DC * 1024], BF16, s2s)
            wstg = sb("wstgC", [128, 1024], F32, s2s)
            bqc = sb("bqc", [128, 8], F32, s2s)
            qfm = sb("qfm", [128, DC * 512], BF16, s2s)
            qp = sb("qp", [128, 1024], BF16, s2s)
            att = sb("attC", [128, DC * NT], BF16, s2s)
            cta = sb("ctaC", [128, 512], F32, s2s)
            ctb = sb("ctbC", [128, 512], F32, s2s)
            nsb = sb("nsbC", [64, 512], F32, s2s)
            for ci in range(DC):
                dma_in(kb, wstg, wq_d[:, ci * 1024:(ci + 1) * 1024])
                cast(kb, wqb, wqb.t[:, ci * 1024:(ci + 1) * 1024], wstg, wstg.t[:])
            dma_in(kb, bqc, bq_d[:, :])
            for tc in range(NT // W):
                col = tc * W
                for oo in range(DC):
                    for ci in range(DC):
                        kb.do("tensor", lambda t, ci=ci, oo=oo, col=col: t.matmul(
                            P[2].t[:, 0:W], wqb.t[:, ci * 1024 + oo * 128:ci * 1024 + oo * 128 + 128],
                            xn[ci].t[:, col:col + W], start=(ci == 0), stop=(ci == DC - 1)),
                            reads=[wqb, xn[ci]], writes=[P[2]], inc=1)
                    kb.do("vector", lambda v, oo=oo: v.tensor_scalar(
                        out=qfm.t[:, oo * W:(oo + 1) * W], in0=P[2].t[:, 0:W],
                        scalar1=bqc.t[:, oo:oo + 1], scalar2=None, op0=ALU.add),
                        reads=[P[2], bqc], writes=[qfm])
                for h in range(16):
                    hp, ho = h // 2, 64 * (h % 2)
                    for mc in range(2):
                        kb.do("tensor", lambda t, h=h, hp=hp, ho=ho, mc=mc: t.matmul(
                            P[2].t[:, 0:W], rmb.t[ho:ho + 64, mc * 128:(mc + 1) * 128],
                            qfm.t[ho:ho + 64, hp * W:hp * W + W], start=True, stop=True),
                            reads=[rmb, qfm], writes=[P[2]])
                        kb.do("vector", lambda v: v.tensor_scalar(
                            out=cta.t[:], in0=P[2].t[:, 0:W], scalar1=0.0, scalar2=None, op0=ALU.min),
                            reads=[P[2]], writes=[cta])
                        kb.do("scalar", lambda s: s.activation(out=cta.t[:], in_=cta.t[:], func=AF.Exp),
                              reads=[cta], writes=[cta])
                        kb.do("vector", lambda v: v.tensor_scalar(
                            out=ctb.t[:], in0=P[2].t[:, 0:W], scalar1=0.0, scalar2=None, op0=ALU.max),
                            reads=[P[2]], writes=[ctb])
                        kb.do("vector", lambda v: v.tensor_add(out=cta.t[:], in0=cta.t[:], in1=ctb.t[:]),
                              reads=[ctb, cta], writes=[cta])
                        kb.do("vector", lambda v, mc=mc: v.tensor_scalar(
                            out=qp.t[:, mc * 512:(mc + 1) * 512], in0=cta.t[:], scalar1=0.0625,
                            scalar2=None, op0=ALU.mult),
                            reads=[cta], writes=[qp])
                    for mc in range(2):
                        kb.do("tensor", lambda t, h=h, mc=mc: t.matmul(
                            P[0].t[0:64, 0:W], kv_sb[mc].t[:, h * 64:(h + 1) * 64],
                            qp.t[:, mc * 512:(mc + 1) * 512], start=(mc == 0), stop=(mc == 1)),
                            reads=[kv_sb[mc], qp], writes=[P[0]], inc=1)
                        kb.do("tensor", lambda t, h=h, mc=mc: t.matmul(
                            P[1].t[0:1, 0:W], ks_sb.t[:, mc * 16 + h:mc * 16 + h + 1],
                            qp.t[:, mc * 512:(mc + 1) * 512], start=(mc == 0), stop=(mc == 1)),
                            reads=[ks_sb, qp], writes=[P[1]], inc=1)
                    kb.do("vector", lambda v: v.tensor_scalar(
                        out=rowt.t[:], in0=P[1].t[0:1, 0:W], scalar1=1e-6, scalar2=None, op0=ALU.max),
                        reads=[P[1]], writes=[rowt])
                    kb.do("vector", lambda v: v.reciprocal(out=rowt.t[:], in_=rowt.t[:]),
                          reads=[rowt], writes=[rowt])
                    kb.do("tensor", lambda t: t.matmul(P[1].t[0:64, 0:W], c["ones_row"].t[0:1, 0:64],
                                                       rowt.t[:], start=True, stop=True),
                          reads=[c["ones_row"], rowt], writes=[P[1]])
                    kb.do("scalar", lambda s: s.copy(out=nsb.t[:], in_=P[0].t[0:64, 0:W]),
                          reads=[P[0]], writes=[nsb])
                    kb.do("vector", lambda v, h=h, hp=hp, ho=ho, col=col: v.tensor_mul(
                        out=att.t[ho:ho + 64, hp * NT + col:hp * NT + col + W], in0=nsb.t[:],
                        in1=P[1].t[0:64, 0:W]), reads=[nsb, P[1]], writes=[att])
            for ci in range(DC):
                dma_in(kb, wstg, wo_d[:, ci * 1024:(ci + 1) * 1024])
                cast(kb, wqb, wqb.t[:, ci * 1024:(ci + 1) * 1024], wstg, wstg.t[:])
            dma_in(kb, bqc, bo_d[:, :])
            for oo in range(DC):
                for tc in range(NT // W):
                    col = tc * W
                    for ci in range(DC):
                        kb.do("tensor", lambda t, ci=ci, oo=oo, col=col: t.matmul(
                            P[2].t[:, 0:W], wqb.t[:, ci * 1024 + oo * 128:ci * 1024 + oo * 128 + 128],
                            att.t[:, ci * NT + col:ci * NT + col + W], start=(ci == 0), stop=(ci == DC - 1)),
                            reads=[wqb, att], writes=[P[2]], inc=1)
                    kb.do("vector", lambda v, oo=oo: v.tensor_scalar(
                        out=cta.t[:], in0=P[2].t[:, 0:W], scalar1=bqc.t[:, oo:oo + 1], scalar2=None,
                        op0=ALU.add), reads=[P[2], bqc], writes=[cta])
                    dma_out(kb, s2sp_d[oo * 128:(oo + 1) * 128, col:col + W], cta)
            kb.barrier()

        s3s = ExitStack()
        with s3s:
            r_all = [sb(f"r{i}", [128, NT], BF16, s3s) for i in range(DC)]
            ka_all = [sb(f"ka{i}", [128, NT], BF16, s3s) for i in range(DC)]
            wrb = sb("wrbE", [128, DC * 1024], BF16, s3s)
            wstg = sb("wstgE", [128, 1024], F32, s3s)
            tfa = sb("tfaE", [128, 512], F32, s3s)
            tfb = sb("tfbE", [128, 512], F32, s3s)
            xrr = sb("xrrE", [128, 512], F32, s3s)
            murc = sb("murc", [128, 8], F32, s3s)
            mukc = sb("mukc", [128, 8], F32, s3s)
            pcol = sb("pcol", [128, 1], F32, s3s)
            prow = sb("prow", [1, 1], F32, s3s)
            dd = sb("dd", [128, NT], F32, s3s)
            colt = sb("colt", [128, 1], F32, s3s)
            colt2 = sb("colt2", [128, 1], F32, s3s)
            dma_in(kb, murc, mur_d[:, :])
            dma_in(kb, mukc, muk_d[:, :])
            dma_in(kb, prow, pm_d[:, :])
            kb.do("tensor", lambda t: t.matmul(P[2].t[:, 0:1], c["ones_row"].t[:], prow.t[:],
                                               start=True, stop=True),
                  reads=[c["ones_row"], prow], writes=[P[2]])
            kb.do("vector", lambda v: v.tensor_copy(out=pcol.t[:], in_=P[2].t[:, 0:1]),
                  reads=[P[2]], writes=[pcol])
            def build_shift(mu, xall):
                for ci in range(DC):
                    dst = xall[ci]
                    kb.do("vector", lambda v, ci=ci: v.tensor_scalar(
                        out=colt.t[:], in0=xn_last.t[:, ci:ci + 1], scalar1=pcol.t[:, 0:1],
                        scalar2=None, op0=ALU.mult), reads=[xn_last, pcol], writes=[colt])
                    kb.do("vector", lambda v, ci=ci: v.tensor_tensor(
                        out=colt2.t[:], in0=xn[ci].t[:, 0:1], in1=colt.t[:], op=ALU.subtract),
                        reads=[xn[ci], colt], writes=[colt2])
                    kb.do("vector", lambda v, ci=ci, mu=mu, dst=dst: v.scalar_tensor_tensor(
                        out=dst.t[:, 0:1], in0=colt2.t[:], scalar=mu.t[:, ci:ci + 1], in1=colt.t[:],
                        op0=ALU.mult, op1=ALU.add), reads=[colt2, mu, colt], writes=[dst])
                    kb.do("vector", lambda v, ci=ci: v.tensor_tensor(
                        out=dd.t[:, 1:NT], in0=xn[ci].t[:, 1:NT], in1=xn[ci].t[:, 0:NT - 1],
                        op=ALU.subtract), reads=[xn[ci]], writes=[dd])
                    kb.do("vector", lambda v, ci=ci, mu=mu, dst=dst: v.scalar_tensor_tensor(
                        out=dst.t[:, 1:NT], in0=dd.t[:, 1:NT], scalar=mu.t[:, ci:ci + 1],
                        in1=xn[ci].t[:, 0:NT - 1], op0=ALU.mult, op1=ALU.add),
                        reads=[dd, mu, xn[ci]], writes=[dst])

            sE1 = ExitStack()
            sE1.__enter__()
            xr_all = [sb(f"xr{i}", [128, NT], BF16, sE1) for i in range(DC)]
            build_shift(murc, xr_all)
            for ci in range(DC):
                dma_in(kb, wstg, wr_d[:, ci * 1024:(ci + 1) * 1024])
                cast(kb, wrb, wrb.t[:, ci * 1024:(ci + 1) * 1024], wstg, wstg.t[:])
            for oo in range(DC):
                for tc in range(NT // W):
                    col = tc * W
                    for ci in range(DC):
                        kb.do("tensor", lambda t, ci=ci, oo=oo, col=col: t.matmul(
                            P[2].t[:, 0:W], wrb.t[:, ci * 1024 + oo * 128:ci * 1024 + oo * 128 + 128],
                            xr_all[ci].t[:, col:col + W], start=(ci == 0), stop=(ci == DC - 1)),
                            reads=[wrb, xr_all[ci]], writes=[P[2]], inc=1)
                    kb.do("scalar", lambda s, oo=oo, col=col: s.activation(
                        out=r_all[oo].t[:, col:col + W], in_=P[2].t[:, 0:W], func=AF.Sigmoid),
                        reads=[P[2]], writes=[r_all[oo]])
            kb.barrier()
            sE1.__exit__(None, None, None)
            sE2 = ExitStack()
            sE2.__enter__()
            xk_all = [sb(f"xk{i}", [128, NT], BF16, sE2) for i in range(DC)]
            build_shift(mukc, xk_all)
            for ci in range(DC):
                dma_in(kb, wstg, wk2_d[:, ci * 1024:(ci + 1) * 1024])
                cast(kb, wrb, wrb.t[:, ci * 1024:(ci + 1) * 1024], wstg, wstg.t[:])
            for oo in range(DC):
                for tc in range(NT // W):
                    col = tc * W
                    for ci in range(DC):
                        kb.do("tensor", lambda t, ci=ci, oo=oo, col=col: t.matmul(
                            P[2].t[:, 0:W], wrb.t[:, ci * 1024 + oo * 128:ci * 1024 + oo * 128 + 128],
                            xk_all[ci].t[:, col:col + W], start=(ci == 0), stop=(ci == DC - 1)),
                            reads=[wrb, xk_all[ci]], writes=[P[2]], inc=1)
                    kb.do("vector", lambda v: v.tensor_scalar(
                        out=tfa.t[:], in0=P[2].t[:, 0:W], scalar1=0.0, scalar2=None, op0=ALU.max),
                        reads=[P[2]], writes=[tfa])
                    kb.do("scalar", lambda s, oo=oo, col=col: s.activation(
                        out=ka_all[oo].t[:, col:col + W], in_=tfa.t[:], func=AF.Square),
                        reads=[tfa], writes=[ka_all[oo]])
            kb.barrier()
            sE2.__exit__(None, None, None)
            for ci in range(DC):
                dma_in(kb, wstg, wv2_d[:, ci * 1024:(ci + 1) * 1024])
                cast(kb, wrb, wrb.t[:, ci * 1024:(ci + 1) * 1024], wstg, wstg.t[:])
            for oo in range(DC):
                for tc in range(NT // W):
                    col = tc * W
                    for ci in range(DC):
                        kb.do("tensor", lambda t, ci=ci, oo=oo, col=col: t.matmul(
                            P[2].t[:, 0:W], wrb.t[:, ci * 1024 + oo * 128:ci * 1024 + oo * 128 + 128],
                            ka_all[ci].t[:, col:col + W], start=(ci == 0), stop=(ci == DC - 1)),
                            reads=[wrb, ka_all[ci]], writes=[P[2]], inc=1)
                    kb.do("vector", lambda v, oo=oo, col=col: v.tensor_mul(
                        out=tfa.t[:], in0=r_all[oo].t[:, col:col + W], in1=P[2].t[:, 0:W]),
                        reads=[r_all[oo], P[2]], writes=[tfa])
                    kb.do("scalar", lambda s: s.activation(out=tfa.t[:], in_=tfa.t[:], func=AF.Silu),
                          reads=[tfa], writes=[tfa])
                    dma_in(kb, tfb, s2sp_d[oo * 128:(oo + 1) * 128, col:col + W])
                    dma_in(kb, xrr, x_d[oo * 128:(oo + 1) * 128, col:col + W])
                    kb.do("vector", lambda v: v.tensor_mul(out=tfa.t[:], in0=tfa.t[:], in1=tfb.t[:]),
                          reads=[tfa, tfb], writes=[tfa])
                    kb.do("vector", lambda v: v.tensor_add(out=xrr.t[:], in0=xrr.t[:], in1=tfa.t[:]),
                          reads=[tfa, xrr], writes=[xrr])
                    dma_out(kb, xo_d[oo * 128:(oo + 1) * 128, col:col + W], xrr)
        kb.finalize(nc, es)
    return nc


def _prep_l2(inputs):
    T = lambda a: np.ascontiguousarray(np.asarray(a).T)
    A = lambda a: np.ascontiguousarray(np.asarray(a))
    return {
        "wq": _chunk_w(A(inputs["L2_attn_Wq"]), 1024),
        "wk": _chunk_w(A(inputs["L2_attn_Wk"]), 1024),
        "wv": _chunk_w(A(inputs["L2_attn_Wv"]), 1024),
        "wo": _chunk_w(A(inputs["L2_attn_Wo"]), 1024),
        "bq": np.ascontiguousarray(np.asarray(inputs["L2_attn_bq"]).reshape(8, 128).T),
        "bk": np.ascontiguousarray(np.asarray(inputs["L2_attn_bk"]).reshape(8, 128).T),
        "bv": np.ascontiguousarray(np.asarray(inputs["L2_attn_bv"]).reshape(1, 1024)),
        "bo": np.ascontiguousarray(np.asarray(inputs["L2_attn_bo"]).reshape(8, 128).T),
        "rm": A(inputs["L2_attn_random_matrix"]),
        "wr": _chunk_w(T(inputs["L2_ffn_W_r"]), 1024),
        "wk2": _chunk_w(T(inputs["L2_ffn_W_k"]), 1024),
        "wv2": _chunk_w(T(inputs["L2_ffn_W_v"]), 1024),
        "mur": np.ascontiguousarray(np.asarray(inputs["L2_ffn_mu_r"]).reshape(8, 128).T),
        "muk": np.ascontiguousarray(np.asarray(inputs["L2_ffn_mu_k"]).reshape(8, 128).T),
    }


def run_l2(Xfm, inputs):
    pre = _prep_l2(inputs)
    nc = build_l2()
    in_maps = []
    for core in range(8):
        half = core % 2
        m = {"x": _perm_x(Xfm, core), "pm": np.full((1, 1), float(half), np.float32)}
        m.update(pre)
        in_maps.append(m)
    res = run_bass_kernel_spmd(nc, in_maps, core_ids=list(range(8)))
    X3 = np.empty((B, D, S), np.float32)
    for core in range(8):
        b, half = core // 2, core % 2
        X3[b][:, half * NT:(half + 1) * NT] = res.results[core]["xo"]
    return X3



def build_l3():
    TT = 1024
    nc = bass.Bass("TRN2", target_bir_lowering=False, debug=False)
    xa_d = nc.dram_tensor("xa", [4, D, NT], F32, kind="ExternalInput").ap()
    xo_in_d = nc.dram_tensor("xown", [D, NT], F32, kind="ExternalInput").ap()
    dcol_d = nc.dram_tensor("dcol", [128, DC * 16], F32, kind="ExternalInput").ap()
    acol_d = nc.dram_tensor("acol", [128, DC * 16], F32, kind="ExternalInput").ap()
    bcol_d = nc.dram_tensor("bcol", [128, DC * 16], F32, kind="ExternalInput").ap()
    gcol_d = nc.dram_tensor("gcol", [128, DC * 16], F32, kind="ExternalInput").ap()
    om_d = nc.dram_tensor("om", [128, 8], F32, kind="ExternalInput").ap()
    msel_d = nc.dram_tensor("msel", [1, 16], F32, kind="ExternalInput").ap()
    w1_d = nc.dram_tensor("w1", [128, DC * 4096], F32, kind="ExternalInput").ap()
    b1_d = nc.dram_tensor("b1", [128, 32], F32, kind="ExternalInput").ap()
    w2_d = nc.dram_tensor("w2", [128, 32 * 1024], F32, kind="ExternalInput").ap()
    b2_d = nc.dram_tensor("b2", [128, 8], F32, kind="ExternalInput").ap()
    w1t_d = nc.dram_tensor("w1t", [128, DC * 4096], BF16, kind="Internal").ap()
    w2t_d = nc.dram_tensor("w2t", [128, 32 * 1024], BF16, kind="Internal").ap()
    xo_d = nc.dram_tensor("xo", [D, NT], F32, kind="ExternalOutput").ap()

    kb = KB()
    es = ExitStack()
    with es:
        def sb(n, sh, dt, stack=None):
            st = stack or es
            return Buf(st.enter_context(nc.sbuf_tensor(n, sh, dt)), n)
        c = mk_consts(nc, es, kb)
        P = [Buf(es.enter_context(nc.psum_tensor(f"P{i}", [128, 512], F32)), f"P{i}") for i in range(4)]
        sc = mk_ln_scratch(nc, es, "ln")
        xn_own = [sb(f"xno{i}", [128, NT], BF16) for i in range(DC)]
        s3buf = [sb(f"s3_{i}", [128, NT], BF16) for i in range(DC)]
        omc = sb("omc", [128, 8], F32)
        ccols = sb("ccols", [128, DC * 4], F32)
        dma_in(kb, omc, om_d[:, :])

        s0 = ExitStack()
        with s0:
            dk = sb("dk", [128, DC * 16], F32, s0)
            ak = sb("ak", [128, DC * 16], F32, s0)
            bk = sb("bk", [128, DC * 16], F32, s0)
            gk = sb("gk", [128, DC * 16], F32, s0)
            base = sb("base", [128, DC * 16], F32, s0)
            qq = sb("qq", [128, DC * 16], F32, s0)
            kcol = sb("kcol", [128, DC * 4], F32, s0)
            mrow = sb("mrow", [1, 16], F32, s0)
            mB = sb("mB", [128, 16], F32, s0)
            tmp4 = sb("tmp4", [128, 4], F32, s0)
            dma_in(kb, dk, dcol_d[:, :])
            dma_in(kb, ak, acol_d[:, :])
            dma_in(kb, bk, bcol_d[:, :])
            dma_in(kb, gk, gcol_d[:, :])
            kb.do("scalar", lambda s: s.activation(out=dk.t[:], in_=dk.t[:], func=AF.Sigmoid),
                  reads=[dk], writes=[dk])
            kb.do("scalar", lambda s: s.activation(out=ak.t[:], in_=ak.t[:], func=AF.Sigmoid),
                  reads=[ak], writes=[ak])
            kb.do("vector", lambda v: v.tensor_mul(out=qq.t[:], in0=dk.t[:], in1=ak.t[:]),
                  reads=[dk, ak], writes=[qq])
            kb.do("vector", lambda v: v.tensor_scalar(out=qq.t[:], in0=qq.t[:], scalar1=-1.0,
                                                      scalar2=1.0, op0=ALU.mult, op1=ALU.add),
                  reads=[qq], writes=[qq])
            kb.do("vector", lambda v: v.tensor_mul(out=base.t[:], in0=dk.t[:], in1=bk.t[:]),
                  reads=[dk, bk], writes=[base])
            kb.do("vector", lambda v: v.tensor_mul(out=base.t[:], in0=base.t[:], in1=gk.t[:]),
                  reads=[gk, base], writes=[base])
            kb.do("vector", lambda v: v.tensor_scalar(out=base.t[:], in0=base.t[:], scalar1=0.25,
                                                      scalar2=None, op0=ALU.mult),
                  reads=[base], writes=[base])
            for l in range(4):
                for ci in range(DC):
                    kb.do("vector", lambda v, l=l, ci=ci: v.reduce_sum(
                        kcol.t[:, ci * 4 + l:ci * 4 + l + 1], base.t[:, ci * 16:(ci + 1) * 16], axis=AX.X),
                        reads=[base], writes=[kcol])
                if l < 3:
                    kb.do("vector", lambda v: v.tensor_mul(out=base.t[:], in0=base.t[:], in1=qq.t[:]),
                          reads=[qq, base], writes=[base])
            dma_in(kb, mrow, msel_d[:, :])
            kb.do("tensor", lambda t: t.matmul(P[3].t[:, 0:16], c["ones_row"].t[:], mrow.t[:],
                                               start=True, stop=True),
                  reads=[c["ones_row"], mrow], writes=[P[3]])
            kb.do("vector", lambda v: v.tensor_copy(out=mB.t[:], in_=P[3].t[:, 0:16]),
                  reads=[P[3]], writes=[mB])
            for ci in range(DC):
                for tau in range(4):
                    kb.do("vector", lambda v, ci=ci, tau=tau: v.tensor_mul(
                        out=tmp4.t[:], in0=kcol.t[:, ci * 4:(ci + 1) * 4], in1=mB.t[:, tau * 4:(tau + 1) * 4]),
                        reads=[kcol, mB], writes=[tmp4])
                    kb.do("vector", lambda v, ci=ci, tau=tau: v.reduce_sum(
                        ccols.t[:, ci * 4 + tau:ci * 4 + tau + 1], tmp4.t[:], axis=AX.X),
                        reads=[tmp4], writes=[ccols])
            kb.barrier()

        def xo_ap(ci, col):
            return xo_in_d[ci * 128:(ci + 1) * 128, col:col + W]

        emit_ln(kb, c, xo_ap, xn_own, 0, NT, P, sc, out_col0=0)

        s1 = ExitStack()
        with s1:
            xtile = sb("xtile", [128, W], F32, s1)
            xtile2 = sb("xtile2", [128, W], F32, s1)
            for tc in range(NT // W):
                col = tc * W
                for tau in range(4):
                    for ci in range(DC):
                        xa = xtile if ci % 2 == 0 else xtile2
                        dma_in(kb, xa, xa_d[tau, ci * 128:(ci + 1) * 128, col:col + W])
                        kb.do("tensor", lambda t, ci=ci, xa=xa: t.matmul(
                            P[0].t[0:1, 0:W], c["ones_col"].t[:], xa.t[:],
                            start=(ci == 0), stop=(ci == DC - 1)),
                            reads=[c["ones_col"], xa], writes=[P[0]], inc=1)
                        kb.do("scalar", lambda s, xa=xa: s.activation(out=sc["xsq"].t[:], in_=xa.t[:],
                                                                      func=AF.Square),
                              reads=[xa], writes=[sc["xsq"]])
                        kb.do("tensor", lambda t, ci=ci: t.matmul(
                            P[1].t[0:1, 0:W], c["ones_col"].t[:], sc["xsq"].t[:],
                            start=(ci == 0), stop=(ci == DC - 1)),
                            reads=[c["ones_col"], sc["xsq"]], writes=[P[1]], inc=1)
                    kb.do("scalar", lambda s: s.activation(out=sc["row1"].t[:], in_=P[0].t[0:1, 0:W],
                                                           func=AF.Copy, scale=1.0 / D),
                          reads=[P[0]], writes=[sc["row1"]])
                    kb.do("scalar", lambda s: s.activation(out=sc["row2"].t[:], in_=P[1].t[0:1, 0:W],
                                                           func=AF.Copy, scale=1.0 / D),
                          reads=[P[1]], writes=[sc["row2"]])
                    kb.do("vector", lambda v: v.tensor_mul(out=sc["row3"].t[:], in0=sc["row1"].t[:],
                                                           in1=sc["row1"].t[:]),
                          reads=[sc["row1"]], writes=[sc["row3"]])
                    kb.do("vector", lambda v: v.tensor_sub(out=sc["row2"].t[:], in0=sc["row2"].t[:],
                                                           in1=sc["row3"].t[:]),
                          reads=[sc["row2"], sc["row3"]], writes=[sc["row2"]])
                    kb.do("vector", lambda v: v.tensor_scalar(out=sc["row2"].t[:], in0=sc["row2"].t[:],
                                                              scalar1=1e-5, scalar2=None, op0=ALU.add),
                          reads=[sc["row2"]], writes=[sc["row2"]])
                    kb.do("scalar", lambda s: s.activation(out=sc["row2"].t[:], in_=sc["row2"].t[:],
                                                           func=AF.Sqrt),
                          reads=[sc["row2"]], writes=[sc["row2"]])
                    kb.do("vector", lambda v: v.reciprocal(out=sc["row2"].t[:], in_=sc["row2"].t[:]),
                          reads=[sc["row2"]], writes=[sc["row2"]])
                    kb.do("tensor", lambda t: t.matmul(P[0].t[:, 0:W], c["ones_row"].t[:], sc["row1"].t[:],
                                                       start=True, stop=True),
                          reads=[c["ones_row"], sc["row1"]], writes=[P[0]])
                    kb.do("tensor", lambda t: t.matmul(P[1].t[:, 0:W], c["ones_row"].t[:], sc["row2"].t[:],
                                                       start=True, stop=True),
                          reads=[c["ones_row"], sc["row2"]], writes=[P[1]])
                    for ci in range(DC):
                        xa = xtile if ci % 2 == 0 else xtile2
                        dma_in(kb, xa, xa_d[tau, ci * 128:(ci + 1) * 128, col:col + W])
                        kb.do("vector", lambda v, xa=xa: v.tensor_sub(out=xa.t[:], in0=xa.t[:],
                                                                      in1=P[0].t[:, 0:W]),
                              reads=[xa, P[0]], writes=[xa])
                        kb.do("vector", lambda v, xa=xa: v.tensor_mul(out=xa.t[:], in0=xa.t[:],
                                                                      in1=P[1].t[:, 0:W]),
                              reads=[xa, P[1]], writes=[xa])
                        if tau == 0:
                            kb.do("vector", lambda v, ci=ci, tau=tau, xa=xa, col=col: v.tensor_scalar(
                                out=s3buf[ci].t[:, col:col + W], in0=xa.t[:],
                                scalar1=ccols.t[:, ci * 4 + tau:ci * 4 + tau + 1], scalar2=None,
                                op0=ALU.mult), reads=[xa, ccols], writes=[s3buf[ci]])
                        else:
                            kb.do("vector", lambda v, ci=ci, tau=tau, xa=xa, col=col: v.scalar_tensor_tensor(
                                out=s3buf[ci].t[:, col:col + W], in0=xa.t[:],
                                scalar=ccols.t[:, ci * 4 + tau:ci * 4 + tau + 1],
                                in1=s3buf[ci].t[:, col:col + W], op0=ALU.mult, op1=ALU.add),
                                reads=[xa, ccols, s3buf[ci]], writes=[s3buf[ci]])
                for ci in range(DC):
                    kb.do("vector", lambda v, ci=ci, col=col: v.scalar_tensor_tensor(
                        out=sc["x1"].t[:], in0=xn_own[ci].t[:, col:col + W],
                        scalar=omc.t[:, ci:ci + 1], in1=s3buf[ci].t[:, col:col + W],
                        op0=ALU.mult, op1=ALU.add),
                        reads=[xn_own[ci], omc, s3buf[ci]], writes=[sc["x1"]])
                    kb.do("scalar", lambda s, ci=ci, col=col: s.activation(
                        out=s3buf[ci].t[:, col:col + W], in_=sc["x1"].t[:], func=AF.Silu),
                        reads=[sc["x1"]], writes=[s3buf[ci]])
            kb.barrier()

        s2t = ExitStack()
        with s2t:
            wt = sb("wtT", [128, 512], F32, s2t)
            wt2 = sb("wtT2", [128, 512], F32, s2t)
            colsum = sb("colsumT", [128, 1], F32, s2t)
            rcol = sb("rcolT", [128, 1], F32, s2t)
            one11 = sb("one11", [1, 1], F32, s2t)
            wob = sb("wobT", [128, 512], BF16, s2t)
            for wd, wtd, ncols in ((w1_d, w1t_d, DC * 4096), (w2_d, w2t_d, 32 * 1024)):
                ntile = ncols // 512
                for i in range(ntile):
                    wx = wt if i % 2 == 0 else wt2
                    dma_in(kb, wx, wd[:, i * 512:(i + 1) * 512])
                    kb.do("vector", lambda v, wx=wx: v.reduce_sum(
                        sc["x1"].t[:, 0:1], wx.t[:], axis=AX.X, apply_absolute_value=True),
                        reads=[wx], writes=[sc["x1"]])
                    if i == 0:
                        kb.do("vector", lambda v: v.tensor_copy(out=colsum.t[:], in_=sc["x1"].t[:, 0:1]),
                              reads=[sc["x1"]], writes=[colsum])
                    else:
                        kb.do("vector", lambda v: v.tensor_add(out=colsum.t[:], in0=colsum.t[:],
                                                               in1=sc["x1"].t[:, 0:1]),
                              reads=[sc["x1"], colsum], writes=[colsum])
                kb.do("tensor", lambda t: t.matmul(P[3].t[0:1, 0:1], colsum.t[:, 0:1],
                                                   c["ones_col"].t[:, 0:1], start=True, stop=True),
                      reads=[colsum, c["ones_col"]], writes=[P[3]])
                kb.do("vector", lambda v, ncols=ncols: v.tensor_scalar(
                    out=one11.t[:], in0=P[3].t[0:1, 0:1], scalar1=1.0 / (128.0 * ncols),
                    scalar2=1e-8, op0=ALU.mult, op1=ALU.max),
                    reads=[P[3]], writes=[one11])
                kb.do("vector", lambda v: v.reciprocal(out=one11.t[:], in_=one11.t[:]),
                      reads=[one11], writes=[one11])
                kb.do("tensor", lambda t: t.matmul(P[3].t[:, 0:1], c["ones_row"].t[:], one11.t[:],
                                                   start=True, stop=True),
                      reads=[c["ones_row"], one11], writes=[P[3]])
                kb.do("vector", lambda v: v.tensor_copy(out=rcol.t[:], in_=P[3].t[:, 0:1]),
                      reads=[P[3]], writes=[rcol])
                for i in range(ntile):
                    wx = wt if i % 2 == 0 else wt2
                    dma_in(kb, wx, wd[:, i * 512:(i + 1) * 512])
                    kb.do("vector", lambda v, wx=wx: v.tensor_scalar(
                        out=wx.t[:], in0=wx.t[:], scalar1=rcol.t[:, 0:1], scalar2=MAGIC,
                        op0=ALU.mult, op1=ALU.add), reads=[wx, rcol], writes=[wx])
                    kb.do("vector", lambda v, wx=wx: v.tensor_scalar(
                        out=wx.t[:], in0=wx.t[:], scalar1=MAGIC, scalar2=1.0,
                        op0=ALU.subtract, op1=ALU.min), reads=[wx], writes=[wx])
                    kb.do("vector", lambda v, wx=wx: v.tensor_scalar(
                        out=wob.t[:], in0=wx.t[:], scalar1=-1.0, scalar2=None, op0=ALU.max),
                        reads=[wx], writes=[wob])
                    dma_out(kb, wtd[:, i * 512:(i + 1) * 512], wob)
            kb.barrier()

        s3f = ExitStack()
        with s3f:
            h_sb = sb("h_sb", [128, 32 * TT], BF16, s3f)
            wbf = sb("wbfL3", [128, 1024], BF16, s3f)
            wbf2 = sb("wbfL3b", [128, 1024], BF16, s3f)
            b1c = sb("b1c", [128, 32], F32, s3f)
            b2c = sb("b2c", [128, 8], F32, s3f)
            tfa = sb("tfaL3", [128, W], F32, s3f)
            xrr = sb("xrrL3", [128, W], F32, s3f)
            dma_in(kb, b1c, b1_d[:, :])
            dma_in(kb, b2c, b2_d[:, :])
            w1t_r = w1t_d.rearrange("p (k f) -> p k f", f=4096)
            w2t_r = w2t_d.rearrange("p (k f) -> p k f", f=1024)
            NN = TT // W
            for tc in range(NT // TT):
                tc0 = tc * TT
                for fo in range(32):
                    wx = wbf if fo % 2 == 0 else wbf2
                    kb.do("sync", lambda e, fo=fo, wx=wx: e.dma_start(
                        out=wx.t[:].rearrange("p (k f) -> p k f", f=128),
                        in_=w1t_r[:, :, fo * 128:fo * 128 + 128]),
                        writes=[wx], inc=16, sem_key="ld_" + wx.name)
                    for nn in range(NN):
                        for ci in range(DC):
                            kb.do("tensor", lambda t, ci=ci, nn=nn, tc0=tc0, wx=wx: t.matmul(
                                P[2 + nn].t[:, 0:W], wx.t[:, ci * 128:(ci + 1) * 128],
                                xn_own[ci].t[:, tc0 + nn * W:tc0 + nn * W + W],
                                start=(ci == 0), stop=(ci == DC - 1)),
                                reads=[wx, xn_own[ci]], writes=[P[2 + nn]], inc=1)
                        kb.do("scalar", lambda s, fo=fo, nn=nn: s.activation(
                            out=h_sb.t[:, fo * TT + nn * W:fo * TT + nn * W + W],
                            in_=P[2 + nn].t[:, 0:W], func=AF.Gelu, bias=b1c.t[:, fo:fo + 1]),
                            reads=[P[2 + nn], b1c], writes=[h_sb])
                for oo in range(DC):
                    for fig in range(DC):
                        wx = wbf if fig % 2 == 0 else wbf2
                        kb.do("sync", lambda e, fig=fig, oo=oo, wx=wx: e.dma_start(
                            out=wx.t[:, 0:512].rearrange("p (k f) -> p k f", f=128),
                            in_=w2t_r[:, fig * 4:fig * 4 + 4, oo * 128:oo * 128 + 128]),
                            writes=[wx], inc=16, sem_key="ld_" + wx.name)
                        for nn in range(NN):
                            for k in range(4):
                                fi = fig * 4 + k
                                kb.do("tensor", lambda t, k=k, nn=nn, fi=fi, wx=wx: t.matmul(
                                    P[2 + nn].t[:, 0:W], wx.t[:, k * 128:(k + 1) * 128],
                                    h_sb.t[:, fi * TT + nn * W:fi * TT + nn * W + W],
                                    start=(fi == 0), stop=(fi == 31)),
                                    reads=[wx, h_sb], writes=[P[2 + nn]], inc=1)
                    for nn in range(NN):
                        col = tc0 + nn * W
                        kb.do("scalar", lambda s, oo=oo, nn=nn: s.activation(
                            out=tfa.t[:], in_=P[2 + nn].t[:, 0:W], func=AF.Silu,
                            bias=b2c.t[:, oo:oo + 1]),
                            reads=[P[2 + nn], b2c], writes=[tfa])
                        dma_in(kb, xrr, xo_in_d[oo * 128:(oo + 1) * 128, col:col + W])
                        kb.do("vector", lambda v, oo=oo, col=col: v.tensor_mul(
                            out=tfa.t[:], in0=tfa.t[:], in1=s3buf[oo].t[:, col:col + W]),
                            reads=[tfa, s3buf[oo]], writes=[tfa])
                        kb.do("vector", lambda v: v.tensor_add(out=xrr.t[:], in0=xrr.t[:], in1=tfa.t[:]),
                              reads=[tfa, xrr], writes=[xrr])
                        dma_out(kb, xo_d[oo * 128:(oo + 1) * 128, col:col + W], xrr)

        kb.finalize(nc, es)
    return nc


def _prep_l1(inputs):
    T = lambda a: np.ascontiguousarray(np.asarray(a).T)
    d = {
        "wq": _chunk_w(T(inputs["L1_attn_Wq"]), 1024),
        "wk": _chunk_w(T(inputs["L1_attn_Wk"]), 1024),
        "wv": _chunk_w(T(inputs["L1_attn_Wv"]), 1024),
        "wo": _chunk_w(T(inputs["L1_attn_Wo"]), 1024),
        "bq": np.ascontiguousarray(np.asarray(inputs["L1_attn_bq"]).reshape(8, 128).T),
        "bk": np.ascontiguousarray(np.asarray(inputs["L1_attn_bk"]).reshape(1, 1024)),
        "bv": np.ascontiguousarray(np.asarray(inputs["L1_attn_bv"]).reshape(1, 1024)),
        "bo": np.ascontiguousarray(np.asarray(inputs["L1_attn_bo"]).reshape(8, 128).T),
        "stdp": np.ascontiguousarray(np.asarray(inputs["L1_attn_stdp"]).reshape(1, 16)),
        "rw": _chunk_w(T(inputs["L1_ffn_router_weight"]), 16),
        "eb": np.ascontiguousarray(np.asarray(inputs["L1_ffn_e_bias"]).reshape(1, 16)),
        "gw": np.stack([_chunk_w(T(np.asarray(inputs["L1_ffn_gate_w"])[e]), 512) for e in range(16)]),
        "uw": np.stack([_chunk_w(T(np.asarray(inputs["L1_ffn_up_w"])[e]), 512) for e in range(16)]),
        "dw": np.stack([_chunk_w(T(np.asarray(inputs["L1_ffn_down_w"])[e]), 1024) for e in range(16)]),
        "shg": _chunk_w(T(inputs["L1_ffn_sh_gate_w"]), 1024),
        "shu": _chunk_w(T(inputs["L1_ffn_sh_up_w"]), 1024),
        "shd": _chunk_w(T(inputs["L1_ffn_sh_down_w"]), 1024),
    }
    return d


def _perm_x(Xfm, core):
    b, half = core // 2, core % 2
    xb = Xfm[b]
    if half == 0:
        return np.ascontiguousarray(xb)
    return np.ascontiguousarray(np.concatenate([xb[:, NT:], xb[:, :NT]], axis=1))


def run_l1(Xfm, inputs):
    pre = _prep_l1(inputs)
    nc = build_l1()
    in_maps = []
    for core in range(8):
        m = {"x": _perm_x(Xfm, core)}
        m.update(pre)
        in_maps.append(m)
    res = run_bass_kernel_spmd(nc, in_maps, core_ids=list(range(8)))
    X2 = np.empty((B, D, S), np.float32)
    for core in range(8):
        b, half = core // 2, core % 2
        X2[b][:, half * NT:(half + 1) * NT] = res.results[core]["xo"]
    return X2


def _prep_l3(inputs):
    def colchunk(a):
        a = np.asarray(a).reshape(1024, -1)
        k = a.shape[1]
        return np.ascontiguousarray(a.reshape(8, 128, k).transpose(1, 0, 2).reshape(128, 8 * k))
    return {
        "dcol": colchunk(inputs["L3_attn_delta"]),
        "acol": colchunk(inputs["L3_attn_alpha"]),
        "bcol": colchunk(inputs["L3_attn_beta"]),
        "gcol": colchunk(inputs["L3_attn_gamma"]),
        "om": np.ascontiguousarray(np.asarray(inputs["L3_attn_omega"]).reshape(8, 128).T),
        "w1": _chunk_w(np.ascontiguousarray(np.asarray(inputs["L3_ffn_W1"]).T), 4096),
        "b1": np.ascontiguousarray(np.asarray(inputs["L3_ffn_b1"]).reshape(32, 128).T),
        "w2": _chunk_w(np.ascontiguousarray(np.asarray(inputs["L3_ffn_W2"]).T), 1024),
        "b2": np.ascontiguousarray(np.asarray(inputs["L3_ffn_b2"]).reshape(8, 128).T),
    }


def run_l3(Xfm, inputs):
    pre = _prep_l3(inputs)
    nc = build_l3()
    in_maps = []
    for core in range(8):
        b, half = core // 2, core % 2
        sl = slice(half * NT, (half + 1) * NT)
        msel = np.zeros((1, 16), np.float32)
        for tau in range(4):
            l = b - tau
            if 0 <= l < 4:
                msel[0, tau * 4 + l] = 1.0
        m = {"xa": np.ascontiguousarray(Xfm[:, :, sl]),
             "xown": np.ascontiguousarray(Xfm[b][:, sl]),
             "msel": msel}
        m.update(pre)
        in_maps.append(m)
    res = run_bass_kernel_spmd(nc, in_maps, core_ids=list(range(8)))
    X4 = np.empty((B, D, S), np.float32)
    for core in range(8):
        b, half = core // 2, core % 2
        X4[b][:, half * NT:(half + 1) * NT] = res.results[core]["xo"]
    return X4


def kernel(**inputs):
    X = _fm(inputs["L0_attn_x"])
    X = run_l0(X, inputs)
    X = run_l1(X, inputs)
    X = run_l2(X, inputs)
    X = run_l3(X, inputs)
    return np.ascontiguousarray(np.transpose(X, (0, 2, 1)))


# version 02 (final)
# speedup vs baseline: 1.0702x; 1.0702x over previous
import numpy as np
from contextlib import ExitStack

import concourse.bass as bass
import concourse.mybir as mybir
from concourse.bass_utils import run_bass_kernel_spmd

F32 = mybir.dt.float32
BF16 = mybir.dt.bfloat16
AF = mybir.ActivationFunctionType
ALU = mybir.AluOpType
AX = mybir.AxisListType

B, S, D = 4, 4096, 1024
DC = 8
W = 512
NT = 2048
MAGIC = 1.5 * 2.0 ** 23

ENGINES = ("sync", "tensor", "vector", "scalar", "gpsimd")


class Buf:
    __slots__ = ("t", "w", "readers", "name")

    def __init__(self, t, name=None):
        self.t = t
        self.w = None
        self.readers = {}
        self.name = name or getattr(t, "name", None) or str(id(t))


class KB:

    def __init__(self):
        self.ops = {e: [] for e in ENGINES}
        self.cnt = {e: 0 for e in ENGINES}
        self.water = {e: {} for e in ENGINES}
        self.pending = {e: [] for e in ENGINES}
        self.out_channels = set()

    def _wait(self, eng, tok):
        if tok is None:
            return
        pe, v = tok
        if self.water[eng].get(pe, 0) >= v:
            return
        self.ops[eng].append(("w", pe, v))
        self.water[eng][pe] = v

    def do(self, eng, fn, reads=(), writes=(), inc=1, sem_key=None):
        key = sem_key or eng
        deps = []
        for b in reads:
            deps.append(b.w)
        for b in writes:
            deps.append(b.w)
            deps.extend(b.readers.values())
        for t in deps:
            self._wait(eng, t)
        self.ops[eng].append(("o", fn, inc, key))
        if inc:
            self.cnt[key] = self.cnt.get(key, 0) + inc
            tok = (key, self.cnt[key])
            for b in self.pending[eng]:
                b.readers[key] = tok
            self.pending[eng] = []
        else:
            tok = None
        for b in reads:
            if inc:
                b.readers[key] = tok
            elif b not in self.pending[eng]:
                self.pending[eng].append(b)
        for b in writes:
            if inc:
                b.w = tok
                b.readers = {}
            elif b not in self.pending[eng]:
                self.pending[eng].append(b)
        return tok

    def barrier(self):
        snap = dict(self.cnt)
        for eng in ENGINES:
            for pe, v in snap.items():
                if pe != eng and v > 0:
                    self._wait(eng, (pe, v))

    def finalize(self, nc, es):
        sems = {k: es.enter_context(nc.semaphore("sem_" + str(i)))
                for i, k in enumerate(self.cnt)}
        for ch in sorted(self.out_channels):
            if self.cnt.get(ch, 0) > 0:
                self.ops["gpsimd"].append(("w", ch, self.cnt[ch]))
        block = es.enter_context(nc.Block())
        for ename in ENGINES:
            items = self.ops[ename]
            if not items:
                continue

            def make_body(items, ename):
                def body(eng):
                    for it in items:
                        if it[0] == "w":
                            eng.wait_ge(sems[it[1]], it[2])
                        else:
                            ins = it[1](eng)
                            if it[2]:
                                ins.then_inc(sems[it[3]], it[2])
                return body

            getattr(block, ename)(make_body(items, ename))


def dma_in(kb, buf, src_ap, dst_sl=None, q="sync"):
    d = dst_sl if dst_sl is not None else buf.t[:]
    return kb.do(q, lambda e, d=d, s=src_ap: e.dma_start(out=d, in_=s), writes=[buf], inc=16,
                 sem_key="ld_" + buf.name)


def dma_out(kb, dst_ap, buf, src_sl=None, q="gpsimd"):
    s = src_sl if src_sl is not None else buf.t[:]
    ch = "st_" + buf.name
    kb.out_channels.add(ch)
    return kb.do(q, lambda e, d=dst_ap, s=s: e.dma_start(out=d, in_=s), reads=[buf], inc=16,
                 sem_key=ch)


def cast(kb, dst_buf, dst_sl, src_buf, src_sl, eng="vector"):
    kb.do(eng, lambda v, d=dst_sl, s=src_sl: v.tensor_copy(out=d, in_=s),
          reads=[src_buf], writes=[dst_buf])


def mk_consts(nc, es, kb):
    c = {}

    def sb(n, sh, dt):
        return Buf(es.enter_context(nc.sbuf_tensor(n, sh, dt)), n)

    c["ones_row"] = sb("c_ones_row", [1, 128], F32)
    c["ones_col"] = sb("c_ones_col", [128, 1], F32)
    c["ones_bcol"] = sb("c_ones_bcol", [128, 1], BF16)
    c["ident"] = sb("c_ident", [128, 128], F32)
    kb.do("vector", lambda v: v.memset(c["ones_row"].t[:], 1.0), writes=[c["ones_row"]])
    kb.do("vector", lambda v: v.memset(c["ones_col"].t[:], 1.0), writes=[c["ones_col"]])
    kb.do("vector", lambda v: v.memset(c["ones_bcol"].t[:], 1.0), writes=[c["ones_bcol"]])
    kb.do("gpsimd", lambda g: g.iota(c["ident"].t[:], pattern=[[1, 128]], base=0,
                                     channel_multiplier=-1, allow_small_or_imprecise_dtypes=True),
          writes=[c["ident"]])
    kb.do("vector", lambda v: v.tensor_scalar(out=c["ident"].t[:], in0=c["ident"].t[:],
                                              scalar1=0.0, scalar2=None, op0=ALU.is_equal),
          reads=[c["ident"]], writes=[c["ident"]])
    return c


def emit_ln(kb, c, x_ap_fn, xn_bufs, xcol0, ncols, P, sc, out_col0=0):
    ntc = ncols // W
    xq = sc["xsq"]
    for tci in range(ntc):
        colg = xcol0 + tci * W
        colo = out_col0 + tci * W
        for ci in range(DC):
            xa = sc["x1"] if ci % 2 == 0 else sc["x2"]
            dma_in(kb, xa, x_ap_fn(ci, colg))
            kb.do("tensor", lambda t, ci=ci, xa=xa: t.matmul(P[0].t[0:1, 0:W], c["ones_col"].t[:],
                                                      xa.t[:], start=(ci == 0), stop=(ci == DC - 1)),
                  reads=[c["ones_col"], xa], writes=[P[0]])
            kb.do("scalar", lambda s, xa=xa: s.activation(out=xq.t[:], in_=xa.t[:], func=AF.Square),
                  reads=[xa], writes=[xq])
            kb.do("tensor", lambda t, ci=ci: t.matmul(P[1].t[0:1, 0:W], c["ones_col"].t[:],
                                                      xq.t[:], start=(ci == 0), stop=(ci == DC - 1)),
                  reads=[c["ones_col"], xq], writes=[P[1]])
        kb.do("scalar", lambda s: s.activation(out=sc["row1"].t[:], in_=P[0].t[0:1, 0:W],
                                               func=AF.Copy, scale=1.0 / D),
              reads=[P[0]], writes=[sc["row1"]])
        kb.do("scalar", lambda s: s.activation(out=sc["row2"].t[:], in_=P[1].t[0:1, 0:W],
                                               func=AF.Copy, scale=1.0 / D),
              reads=[P[1]], writes=[sc["row2"]])
        kb.do("vector", lambda v: v.tensor_mul(out=sc["row3"].t[:], in0=sc["row1"].t[:], in1=sc["row1"].t[:]),
              reads=[sc["row1"]], writes=[sc["row3"]])
        kb.do("vector", lambda v: v.tensor_sub(out=sc["row2"].t[:], in0=sc["row2"].t[:], in1=sc["row3"].t[:]),
              reads=[sc["row2"], sc["row3"]], writes=[sc["row2"]])
        kb.do("vector", lambda v: v.tensor_scalar(out=sc["row2"].t[:], in0=sc["row2"].t[:],
                                                  scalar1=1e-5, scalar2=None, op0=ALU.add),
              reads=[sc["row2"]], writes=[sc["row2"]])
        kb.do("scalar", lambda s: s.activation(out=sc["row2"].t[:], in_=sc["row2"].t[:],
                                               func=AF.Sqrt),
              reads=[sc["row2"]], writes=[sc["row2"]])
        kb.do("vector", lambda v: v.reciprocal(out=sc["row2"].t[:], in_=sc["row2"].t[:]),
              reads=[sc["row2"]], writes=[sc["row2"]])
        kb.do("tensor", lambda t: t.matmul(P[0].t[:, 0:W], c["ones_row"].t[:], sc["row1"].t[:],
                                           start=True, stop=True),
              reads=[c["ones_row"], sc["row1"]], writes=[P[0]])
        kb.do("tensor", lambda t: t.matmul(P[1].t[:, 0:W], c["ones_row"].t[:], sc["row2"].t[:],
                                           start=True, stop=True),
              reads=[c["ones_row"], sc["row2"]], writes=[P[1]])
        for ci in range(DC):
            xa = sc["x1"] if ci % 2 == 0 else sc["x2"]
            dma_in(kb, xa, x_ap_fn(ci, colg))
            kb.do("vector", lambda v, xa=xa: v.tensor_sub(out=xa.t[:], in0=xa.t[:], in1=P[0].t[:, 0:W]),
                  reads=[xa, P[0]], writes=[xa])
            kb.do("vector", lambda v, ci=ci, colo=colo, xa=xa: v.tensor_mul(
                out=xn_bufs[ci].t[:, colo:colo + W], in0=xa.t[:], in1=P[1].t[:, 0:W]),
                reads=[xa, P[1]], writes=[xn_bufs[ci]])


def mk_ln_scratch(nc, es, pref):
    def sb(n, sh, dt):
        return Buf(es.enter_context(nc.sbuf_tensor(pref + n, sh, dt)), pref + n)
    return {"x1": sb("_x1", [128, W], F32), "x2": sb("_x2", [128, W], F32),
            "xsq": sb("_xsq", [128, W], F32),
            "row1": sb("_row1", [1, W], F32), "row2": sb("_row2", [1, W], F32),
            "row3": sb("_row3", [1, W], F32)}



def build_l0a():
    nc = bass.Bass("TRN2", target_bir_lowering=False, debug=False)
    x_d = nc.dram_tensor("x", [D, S], F32, kind="ExternalInput").ap()
    qw_d = nc.dram_tensor("qw", [DC, 128, 512], F32, kind="ExternalInput").ap()
    kw_d = nc.dram_tensor("kw", [DC, 128, 512], F32, kind="ExternalInput").ap()
    vw_d = nc.dram_tensor("vw", [DC, 128, 512], F32, kind="ExternalInput").ap()
    ww_d = nc.dram_tensor("ww", [8, 64, 1024], F32, kind="ExternalInput").ap()
    z_d = nc.dram_tensor("z", [D, S], F32, kind="ExternalOutput").ap()

    kb = KB()
    es = ExitStack()
    with es:
        def sb(n, sh, dt):
            return Buf(es.enter_context(nc.sbuf_tensor(n, sh, dt)), n)
        c = mk_consts(nc, es, kb)
        P = [Buf(es.enter_context(nc.psum_tensor(f"P{i}", [128, 512], F32)), f"P{i}") for i in range(8)]

        mask = sb("mask", [128, 1408], F32)
        kb.do("gpsimd", lambda g: g.iota(mask.t[:], pattern=[[1, 1408]], base=-512,
                                         channel_multiplier=-1, allow_small_or_imprecise_dtypes=True),
              writes=[mask])
        mask_bf = sb("mask_bf", [128, 1408], BF16)
        kb.do("vector", lambda v: v.tensor_scalar(out=mask_bf.t[:], in0=mask.t[:], scalar1=0.0,
                                                  scalar2=None, op0=ALU.is_ge),
              reads=[mask], writes=[mask_bf])

        sc = mk_ln_scratch(nc, es, "ln")

        qwb = sb("qwb", [128, DC * 512], BF16)
        kwb = sb("kwb", [128, DC * 512], BF16)
        vwb = sb("vwb", [128, DC * 512], BF16)
        wwb = sb("wwb", [64, 8 * 1024], BF16)
        wtmp = sb("wtmp", [128, 512], F32)
        wtmp2 = sb("wtmp2", [64, 1024], F32)
        for ci in range(DC):
            for wd, wb in ((qw_d, qwb), (kw_d, kwb), (vw_d, vwb)):
                dma_in(kb, wtmp, wd[ci, :, :])
                cast(kb, wb, wb.t[:, ci * 512:(ci + 1) * 512], wtmp, wtmp.t[:])
        for h in range(8):
            dma_in(kb, wtmp2, ww_d[h, :, :])
            cast(kb, wwb, wwb.t[0:64, h * 1024:(h + 1) * 1024], wtmp2, wtmp2.t[:])

        qf = [sb(f"qf{i}", [128, S], BF16) for i in range(4)]
        kf = [sb(f"kf{i}", [128, S], BF16) for i in range(4)]
        vt = sb("vt", [128, 32 * 512], BF16)
        xn = [sb(f"xn{i}", [128, W], BF16) for i in range(DC)]

        def x_ap(ci, col):
            return x_d[ci * 128:(ci + 1) * 128, col:col + W]

        for tci in range(S // W):
            col = tci * W
            emit_ln(kb, c, x_ap, xn, col, W, P, sc, out_col0=0)
            for hp in range(4):
                for wb, pp, dst in ((qwb, P[4], qf[hp]), (kwb, P[5], kf[hp])):
                    for ci in range(DC):
                        kb.do("tensor", lambda t, ci=ci, hp=hp, wb=wb, pp=pp: t.matmul(
                            pp.t[:, 0:W], wb.t[:, ci * 512 + hp * 128:ci * 512 + hp * 128 + 128],
                            xn[ci].t[:], start=(ci == 0), stop=(ci == DC - 1)),
                            reads=[wb, xn[ci]], writes=[pp], inc=1)
                    kb.do("scalar", lambda s, pp=pp, dst=dst, col=col: s.copy(
                        out=dst.t[:, col:col + W], in_=pp.t[:, 0:W]),
                        reads=[pp], writes=[dst])
            for j in range(4):
                kt = tci * 4 + j
                for ci in range(DC):
                    kb.do("tensor", lambda t, ci=ci, j=j: t.matmul(
                        P[6].t[:, 0:W], xn[ci].t[:, j * 128:(j + 1) * 128],
                        vwb.t[:, ci * 512:(ci + 1) * 512], start=(ci == 0), stop=(ci == DC - 1)),
                        reads=[xn[ci], vwb], writes=[P[6]], inc=1)
                kb.do("scalar", lambda s, kt=kt: s.copy(out=vt.t[:, kt * 512:(kt + 1) * 512],
                                                        in_=P[6].t[:, 0:W]),
                      reads=[P[6]], writes=[vt])

        e_sb = sb("e_sb", [128, 512], BF16)
        mcol = sb("mcol", [128, 1], F32)
        rcol = sb("rcol", [128, 1], F32)
        bias = sb("bias", [128, 1], F32)
        srow = sb("srow", [1, 512], F32)
        ysb = sb("ysb", [64, 512], F32)
        yn = sb("yn", [64, 512], BF16)
        zc = [sb(f"zc{i}", [128, 512], F32) for i in range(DC)]

        e_sb2 = sb("e_sb2", [128, 512], BF16)
        for qc in range(8):
            nkt = 4 * (qc + 1)
            for h in range(8):
                hp, ho = h // 2, 64 * (h % 2)
                for kt in range(nkt):
                    ps = P[0] if kt % 2 == 0 else P[6]
                    eb = e_sb if kt % 2 == 0 else e_sb2
                    kb.do("tensor", lambda t, kt=kt, hp=hp, ho=ho, qc=qc, ps=ps: t.matmul(
                        ps.t[:, 0:W], kf[hp].t[ho:ho + 64, kt * 128:(kt + 1) * 128],
                        qf[hp].t[ho:ho + 64, qc * W:(qc + 1) * W], start=True, stop=True),
                        reads=[kf[hp], qf[hp]], writes=[ps])
                    kb.do("scalar", lambda s, kt=kt, ps=ps, eb=eb: s.activation(
                        out=eb.t[:], in_=ps.t[:, 0:W],
                        func=AF.Exp, scale=1.0 / 1024.0),
                        reads=[ps], writes=[eb])
                    off = kt * 128 - qc * 512
                    if 0 <= off < 512:
                        kb.do("vector", lambda v, off=off, eb=eb: v.tensor_mul(
                            out=eb.t[:], in0=eb.t[:], in1=mask_bf.t[:, 512 - off:1024 - off]),
                            reads=[eb, mask_bf], writes=[eb])
                    kb.do("tensor", lambda t, kt=kt, h=h, eb=eb: t.matmul(
                        P[2].t[0:64, 0:W], vt.t[:, kt * 512 + h * 64:kt * 512 + h * 64 + 64],
                        eb.t[:], start=(kt == 0), stop=(kt == nkt - 1)),
                        reads=[vt, eb], writes=[P[2]], inc=1)
                    kb.do("tensor", lambda t, kt=kt, eb=eb: t.matmul(
                        P[3].t[0:1, 0:W], c["ones_bcol"].t[:], eb.t[:],
                        start=(kt == 0), stop=(kt == nkt - 1)),
                        reads=[c["ones_bcol"], eb], writes=[P[3]], inc=1)
                kb.do("scalar", lambda s: s.copy(out=srow.t[:], in_=P[3].t[0:1, 0:W]),
                      reads=[P[3]], writes=[srow])
                kb.do("vector", lambda v: v.reciprocal(out=srow.t[:], in_=srow.t[:]),
                      reads=[srow], writes=[srow])
                kb.do("tensor", lambda t: t.matmul(P[4].t[0:64, 0:W], c["ones_row"].t[0:1, 0:64],
                                                   srow.t[:], start=True, stop=True),
                      reads=[c["ones_row"], srow], writes=[P[4]])
                kb.do("scalar", lambda s: s.copy(out=ysb.t[:], in_=P[2].t[0:64, 0:W]),
                      reads=[P[2]], writes=[ysb])
                kb.do("vector", lambda v: v.tensor_mul(out=yn.t[:], in0=ysb.t[:], in1=P[4].t[0:64, 0:W]),
                      reads=[ysb, P[4]], writes=[yn])
                for ci in range(DC):
                    kb.do("tensor", lambda t, ci=ci, h=h: t.matmul(
                        P[5].t[:, 0:W], wwb.t[0:64, h * 1024 + ci * 128:h * 1024 + ci * 128 + 128],
                        yn.t[:], start=True, stop=True),
                        reads=[wwb, yn], writes=[P[5]])
                    if h == 0:
                        kb.do("vector", lambda v, ci=ci: v.tensor_copy(out=zc[ci].t[:], in_=P[5].t[:, 0:W]),
                              reads=[P[5]], writes=[zc[ci]])
                    else:
                        kb.do("vector", lambda v, ci=ci: v.tensor_add(out=zc[ci].t[:], in0=zc[ci].t[:],
                                                                     in1=P[5].t[:, 0:W]),
                              reads=[P[5], zc[ci]], writes=[zc[ci]])
            for ci in range(DC):
                dma_out(kb, z_d[ci * 128:(ci + 1) * 128, qc * W:(qc + 1) * W], zc[ci])

        kb.finalize(nc, es)
    return nc



DEBUG_L0B = False


def build_l0b():
    TT = 1024
    nc = bass.Bass("TRN2", target_bir_lowering=False, debug=False)
    if DEBUG_L0B:
        dbg_a = nc.dram_tensor("dbg_a", [128, 32 * TT], F32, kind="ExternalOutput").ap()
        dbg_b = nc.dram_tensor("dbg_b", [128, 32 * TT], F32, kind="ExternalOutput").ap()
        dbg_c = nc.dram_tensor("dbg_c", [128, 32 * TT], F32, kind="ExternalOutput").ap()
        dbg_xn = nc.dram_tensor("dbg_xn", [128, DC * 512], F32, kind="ExternalOutput").ap()
    x_d = nc.dram_tensor("x", [D, NT], F32, kind="ExternalInput").ap()
    za_d = nc.dram_tensor("za", [D, NT], F32, kind="ExternalInput").ap()
    zb_d = nc.dram_tensor("zb", [D, NT], F32, kind="ExternalInput").ap()
    w1_d = nc.dram_tensor("w1", [128, DC * 4096], F32, kind="ExternalInput").ap()
    w2_d = nc.dram_tensor("w2", [128, DC * 4096], F32, kind="ExternalInput").ap()
    h1_d = nc.dram_tensor("h1", [128, 32 * 4096], F32, kind="ExternalInput").ap()
    h2_d = nc.dram_tensor("h2", [128, 32 * 4096], F32, kind="ExternalInput").ap()
    w3_d = nc.dram_tensor("w3", [128, 32 * 1024], F32, kind="ExternalInput").ap()
    s1_d = nc.dram_tensor("s1", [128, 32], F32, kind="ExternalInput").ap()
    s2_d = nc.dram_tensor("s2", [128, 32], F32, kind="ExternalInput").ap()
    sh1_d = nc.dram_tensor("sh1", [128, 32], F32, kind="ExternalInput").ap()
    sh2_d = nc.dram_tensor("sh2", [128, 32], F32, kind="ExternalInput").ap()
    s3_d = nc.dram_tensor("s3", [128, 8], F32, kind="ExternalInput").ap()
    xo_d = nc.dram_tensor("xo", [D, NT], F32, kind="ExternalOutput").ap()

    kb = KB()
    es = ExitStack()
    with es:
        def sb(n, sh, dt):
            return Buf(es.enter_context(nc.sbuf_tensor(n, sh, dt)), n)
        c = mk_consts(nc, es, kb)
        P = [Buf(es.enter_context(nc.psum_tensor(f"P{i}", [128, 512], F32)), f"P{i}") for i in range(8)]
        sc = mk_ln_scratch(nc, es, "ln")

        xn = [sb(f"xn{i}", [128, NT], BF16) for i in range(DC)]
        szt = sb("szt", [128, W], F32)
        szt2 = sb("szt2", [128, W], F32)

        def x_ap(ci, col):
            return x_d[ci * 128:(ci + 1) * 128, col:col + W]

        emit_ln(kb, c, x_ap, xn, 0, NT, P, sc, out_col0=0)

        sca = sb("sca", [128, 32 * 5], F32)
        for i, sd in enumerate((s1_d, s2_d, sh1_d, sh2_d)):
            dma_in(kb, sca, sd[:, :], dst_sl=sca.t[:, i * 32:(i + 1) * 32])
        dma_in(kb, sca, s3_d[:, :], dst_sl=sca.t[:, 128:136])

        a_sb = sb("a_sb", [128, 32 * TT], BF16)
        b_sb = sb("b_sb", [128, 32 * TT], BF16)
        wstp = [sb(f"wst{i}", [128, 1024], F32) for i in range(2)]
        wbfp = [sb(f"wbf{i}", [128, 1024], BF16) for i in range(2)]
        t1 = sb("t1", [128, TT], F32)
        t2 = sb("t2", [128, TT], F32)
        xr = sb("xr", [128, W], F32)
        sg = sb("sg", [128, W], BF16)

        NN = TT // W
        h1_r = h1_d.rearrange("p (k f) -> p k f", f=4096)
        h2_r = h2_d.rearrange("p (k f) -> p k f", f=4096)
        w3_r = w3_d.rearrange("p (k f) -> p k f", f=1024)
        w1_r = w1_d.rearrange("p (k f) -> p k f", f=4096)
        w2_r = w2_d.rearrange("p (k f) -> p k f", f=4096)

        for tc in range(NT // TT):
            tc0 = tc * TT
            for fo in range(32):
                pps = (P[0], P[1]) if fo % 2 == 0 else (P[6], P[7])
                for wi, (wr, pp) in enumerate(((w1_r, pps[0]), (w2_r, pps[1]))):
                    wst, wbf = wstp[wi], wbfp[wi]
                    kb.do("sync", lambda e, wr=wr, fo=fo, wst=wst: e.dma_start(
                        out=wst.t[:].rearrange("p (k f) -> p k f", f=128),
                        in_=wr[:, :, fo * 128:fo * 128 + 128]),
                        writes=[wst], inc=16, sem_key="ld_" + wst.name)
                    cast(kb, wbf, wbf.t[:, 0:1024], wst, wst.t[:, 0:1024])
                    for nn in range(NN):
                        for ci in range(DC):
                            kb.do("tensor", lambda t, ci=ci, nn=nn, pp=pp, tc0=tc0, wbf=wbf: t.matmul(
                                pp.t[:, 0:W],
                                wbf.t[:, ci * 128:(ci + 1) * 128],
                                xn[ci].t[:, tc0 + nn * W:tc0 + nn * W + W],
                                start=(ci == 0), stop=(ci == DC - 1)),
                                reads=[wbf, xn[ci]], writes=[pp], inc=1)
                        if pp is P[0]:
                            kb.do("scalar", lambda s, fo=fo, nn=nn: s.activation(
                                out=t1.t[:, nn * W:nn * W + W], in_=P[0].t[:, 0:W], func=AF.Silu,
                                scale=sca.t[:, fo:fo + 1]),
                                reads=[P[0], sca], writes=[t1])
                        else:
                            kb.do("scalar", lambda s, fo=fo, nn=nn: s.activation(
                                out=t2.t[:, nn * W:nn * W + W], in_=P[1].t[:, 0:W], func=AF.Copy,
                                scale=sca.t[:, 32 + fo:32 + fo + 1]),
                                reads=[P[1], sca], writes=[t2])
                kb.do("vector", lambda v, fo=fo: v.tensor_mul(
                    out=a_sb.t[:, fo * TT:(fo + 1) * TT], in0=t1.t[:], in1=t2.t[:]),
                    reads=[t1, t2], writes=[a_sb])
            if DEBUG_L0B and tc == 1:
                for ci in range(DC):
                    cast(kb, sc["x1"], sc["x1"].t[:], xn[ci], xn[ci].t[:, 0:512])
                    dma_out(kb, dbg_xn[:, ci * 512:(ci + 1) * 512], sc["x1"])
                for fo in range(32):
                    cast(kb, t1, t1.t[:], a_sb, a_sb.t[:, fo * TT:(fo + 1) * TT])
                    dma_out(kb, dbg_a[:, fo * TT:(fo + 1) * TT], t1)
            for asrc, dst, hr, scoff in ((a_sb, b_sb, h1_r, 64), (b_sb, a_sb, h2_r, 96)):
                for fo in range(32):
                    pbase = 2 if fo % 2 == 0 else 4
                    for fig in range(DC):
                        wst, wbf = wstp[fig % 2], wbfp[fig % 2]
                        kb.do("sync", lambda e, hr=hr, fig=fig, fo=fo, wst=wst: e.dma_start(
                            out=wst.t[:, 0:512].rearrange("p (k f) -> p k f", f=128),
                            in_=hr[:, fig * 4:fig * 4 + 4, fo * 128:fo * 128 + 128]),
                            writes=[wst], inc=16, sem_key="ld_" + wst.name)
                        cast(kb, wbf, wbf.t[:, 0:512], wst, wst.t[:, 0:512])
                        for nn in range(NN):
                            for k in range(4):
                                fi = fig * 4 + k
                                kb.do("tensor", lambda t, k=k, nn=nn, fi=fi, asrc=asrc, wbf=wbf, pbase=pbase: t.matmul(
                                    P[pbase + nn].t[:, 0:W], wbf.t[:, k * 128:(k + 1) * 128],
                                    asrc.t[:, fi * TT + nn * W:fi * TT + nn * W + W],
                                    start=(fi == 0), stop=(fi == 31)),
                                    reads=[wbf, asrc], writes=[P[pbase + nn]],
                                    inc=1)
                    for nn in range(NN):
                        kb.do("scalar", lambda s, fo=fo, nn=nn, scoff=scoff, dst=dst, pbase=pbase: s.activation(
                            out=dst.t[:, fo * TT + nn * W:fo * TT + nn * W + W],
                            in_=P[pbase + nn].t[:, 0:W], func=AF.Gelu, scale=sca.t[:, scoff + fo:scoff + fo + 1]),
                            reads=[P[pbase + nn], sca], writes=[dst])
            if DEBUG_L0B and tc == 1:
                for fo in range(32):
                    cast(kb, t1, t1.t[:], b_sb, b_sb.t[:, fo * TT:(fo + 1) * TT])
                    dma_out(kb, dbg_b[:, fo * TT:(fo + 1) * TT], t1)
                for fo in range(32):
                    cast(kb, t1, t1.t[:], a_sb, a_sb.t[:, fo * TT:(fo + 1) * TT])
                    dma_out(kb, dbg_c[:, fo * TT:(fo + 1) * TT], t1)
            for oo in range(DC):
                pbase = 2 if oo % 2 == 0 else 4
                for fig in range(DC):
                    wst, wbf = wstp[fig % 2], wbfp[fig % 2]
                    kb.do("sync", lambda e, fig=fig, oo=oo, wst=wst: e.dma_start(
                        out=wst.t[:, 0:512].rearrange("p (k f) -> p k f", f=128),
                        in_=w3_r[:, fig * 4:fig * 4 + 4, oo * 128:oo * 128 + 128]),
                        writes=[wst], inc=16, sem_key="ld_" + wst.name)
                    cast(kb, wbf, wbf.t[:, 0:512], wst, wst.t[:, 0:512])
                    for nn in range(NN):
                        for k in range(4):
                            fi = fig * 4 + k
                            kb.do("tensor", lambda t, k=k, nn=nn, fi=fi, wbf=wbf, pbase=pbase: t.matmul(
                                P[pbase + nn].t[:, 0:W], wbf.t[:, k * 128:(k + 1) * 128],
                                a_sb.t[:, fi * TT + nn * W:fi * TT + nn * W + W],
                                start=(fi == 0), stop=(fi == 31)),
                                reads=[wbf, a_sb], writes=[P[pbase + nn]],
                                inc=1)
                for nn in range(NN):
                    col = tc0 + nn * W
                    kb.do("scalar", lambda s, oo=oo, nn=nn, pbase=pbase: s.activation(
                        out=sg.t[:], in_=P[pbase + nn].t[:, 0:W], func=AF.Silu,
                        scale=sca.t[:, 128 + oo:128 + oo + 1]),
                        reads=[P[pbase + nn], sca], writes=[sg])
                    dma_in(kb, szt, za_d[oo * 128:(oo + 1) * 128, col:col + W])
                    dma_in(kb, szt2, zb_d[oo * 128:(oo + 1) * 128, col:col + W])
                    dma_in(kb, xr, x_d[oo * 128:(oo + 1) * 128, col:col + W])
                    kb.do("vector", lambda v: v.tensor_add(out=szt.t[:], in0=szt.t[:], in1=szt2.t[:]),
                          reads=[szt, szt2], writes=[szt])
                    kb.do("vector", lambda v: v.scalar_tensor_tensor(
                        out=szt.t[:], in0=szt.t[:], scalar=1.0 / 16.0, in1=sg.t[:],
                        op0=ALU.mult, op1=ALU.mult),
                        reads=[szt, sg], writes=[szt])
                    kb.do("vector", lambda v: v.tensor_add(out=xr.t[:], in0=xr.t[:], in1=szt.t[:]),
                          reads=[szt, xr], writes=[xr])
                    dma_out(kb, xo_d[oo * 128:(oo + 1) * 128, col:col + W], xr)

        kb.finalize(nc, es)
    return nc




def build_l1():
    nc = bass.Bass("TRN2", target_bir_lowering=False, debug=False)
    x_d = nc.dram_tensor("x", [D, S], F32, kind="ExternalInput").ap()
    wq_d = nc.dram_tensor("wq", [128, DC * 1024], F32, kind="ExternalInput").ap()
    wk_d = nc.dram_tensor("wk", [128, DC * 1024], F32, kind="ExternalInput").ap()
    wv_d = nc.dram_tensor("wv", [128, DC * 1024], F32, kind="ExternalInput").ap()
    wo_d = nc.dram_tensor("wo", [128, DC * 1024], F32, kind="ExternalInput").ap()
    bq_d = nc.dram_tensor("bq", [128, 8], F32, kind="ExternalInput").ap()
    bk_d = nc.dram_tensor("bk", [1, 1024], F32, kind="ExternalInput").ap()
    bv_d = nc.dram_tensor("bv", [1, 1024], F32, kind="ExternalInput").ap()
    bo_d = nc.dram_tensor("bo", [128, 8], F32, kind="ExternalInput").ap()
    stdp_d = nc.dram_tensor("stdp", [1, 16], F32, kind="ExternalInput").ap()
    rw_d = nc.dram_tensor("rw", [128, DC * 16], F32, kind="ExternalInput").ap()
    eb_d = nc.dram_tensor("eb", [1, 16], F32, kind="ExternalInput").ap()
    gw_d = nc.dram_tensor("gw", [16, 128, DC * 512], F32, kind="ExternalInput").ap()
    uw_d = nc.dram_tensor("uw", [16, 128, DC * 512], F32, kind="ExternalInput").ap()
    dw_d = nc.dram_tensor("dw", [16, 128, 4 * 1024], F32, kind="ExternalInput").ap()
    shg_d = nc.dram_tensor("shg", [128, DC * 1024], F32, kind="ExternalInput").ap()
    shu_d = nc.dram_tensor("shu", [128, DC * 1024], F32, kind="ExternalInput").ap()
    shd_d = nc.dram_tensor("shd", [128, DC * 1024], F32, kind="ExternalInput").ap()
    s1sp_d = nc.dram_tensor("s1sp", [D, NT], F32, kind="Internal").ap()
    kvb_d = nc.dram_tensor("kvb", [64, 1024], BF16, kind="Internal").ap()
    xo_d = nc.dram_tensor("xo", [D, NT], F32, kind="ExternalOutput").ap()

    kb = KB()
    es = ExitStack()
    with es:
        def sb(n, sh, dt, stack=None):
            st = stack or es
            return Buf(st.enter_context(nc.sbuf_tensor(n, sh, dt)), n)
        c = mk_consts(nc, es, kb)
        P = [Buf(es.enter_context(nc.psum_tensor(f"P{i}", [128, 512], F32)), f"P{i}") for i in range(5)]
        sc = mk_ln_scratch(nc, es, "ln")
        xn = [sb(f"xn{i}", [128, NT], BF16) for i in range(DC)]
        kv_sb = sb("kv_sb", [128, 1024], BF16)
        ks_sb = sb("ks_sb", [128, 8], BF16)
        comb = sb("comb", [128, 16 * 16], F32)
        ctr = sb("ctr", [16, NT], F32)
        sigB = sb("sigB", [64, 16], F32)
        rowt = sb("rowt", [1, 512], F32)

        def x_ap(ci, col):
            return x_d[ci * 128:(ci + 1) * 128, col:col + W]

        s1 = ExitStack()
        with s1:
            wkb = sb("wkb", [128, DC * 1024], BF16, s1)
            wvb = sb("wvb", [128, DC * 1024], BF16, s1)
            wstg = sb("wstg", [128, 1024], F32, s1)
            bkB = sb("bkB", [128, 1024], F32, s1)
            bvB = sb("bvB", [128, 1024], F32, s1)
            brow = sb("brow", [1, 1024], F32, s1)
            ktileP = [sb(f"ktile{i}", [128, 1024], BF16, s1) for i in range(2)]
            vtileP = [sb(f"vtile{i}", [128, 1024], BF16, s1) for i in range(2)]
            tfaP = [sb(f"tfa{i}", [128, 1024], F32, s1) for i in range(2)]
            tfbP = [sb(f"tfb{i}", [128, 1024], F32, s1) for i in range(2)]
            xnt = [sb(f"xnt{i}", [128, W], BF16, s1) for i in range(DC)]
            Pkv = Buf(es.enter_context(nc.psum_tensor("Pkv", [64, 1024], F32)), "Pkv")
            Pks = Buf(es.enter_context(nc.psum_tensor("Pks", [128, 8], F32)), "Pks")
            for ci in range(DC):
                dma_in(kb, wstg, wk_d[:, ci * 1024:(ci + 1) * 1024])
                cast(kb, wkb, wkb.t[:, ci * 1024:(ci + 1) * 1024], wstg, wstg.t[:])
                dma_in(kb, wstg, wv_d[:, ci * 1024:(ci + 1) * 1024])
                cast(kb, wvb, wvb.t[:, ci * 1024:(ci + 1) * 1024], wstg, wstg.t[:])
            dma_in(kb, brow, bk_d[:, :])
            kb.do("tensor", lambda t: t.matmul(P[4].t[:, 0:512], c["ones_row"].t[:], brow.t[0:1, 0:512],
                                               start=True, stop=True), reads=[c["ones_row"], brow], writes=[P[4]])
            kb.do("vector", lambda v: v.tensor_copy(out=bkB.t[:, 0:512], in_=P[4].t[:, 0:512]),
                  reads=[P[4]], writes=[bkB])
            kb.do("tensor", lambda t: t.matmul(P[4].t[:, 0:512], c["ones_row"].t[:], brow.t[0:1, 512:1024],
                                               start=True, stop=True), reads=[c["ones_row"], brow], writes=[P[4]])
            kb.do("vector", lambda v: v.tensor_copy(out=bkB.t[:, 512:1024], in_=P[4].t[:, 0:512]),
                  reads=[P[4]], writes=[bkB])
            dma_in(kb, brow, bv_d[:, :])
            kb.do("tensor", lambda t: t.matmul(P[4].t[:, 0:512], c["ones_row"].t[:], brow.t[0:1, 0:512],
                                               start=True, stop=True), reads=[c["ones_row"], brow], writes=[P[4]])
            kb.do("vector", lambda v: v.tensor_copy(out=bvB.t[:, 0:512], in_=P[4].t[:, 0:512]),
                  reads=[P[4]], writes=[bvB])
            kb.do("tensor", lambda t: t.matmul(P[4].t[:, 0:512], c["ones_row"].t[:], brow.t[0:1, 512:1024],
                                               start=True, stop=True), reads=[c["ones_row"], brow], writes=[P[4]])
            kb.do("vector", lambda v: v.tensor_copy(out=bvB.t[:, 512:1024], in_=P[4].t[:, 0:512]),
                  reads=[P[4]], writes=[bvB])

            nkt_total = S // 128
            for tci in range(S // W):
                col = tci * W
                emit_ln(kb, c, x_ap, xnt, col, W, P, sc, out_col0=0)
                if tci < NT // W:
                    for ci in range(DC):
                        kb.do("vector", lambda v, ci=ci, col=col: v.tensor_copy(
                            out=xn[ci].t[:, col:col + W], in_=xnt[ci].t[:]),
                            reads=[xnt[ci]], writes=[xn[ci]])
                for j in range(4):
                    kt = tci * 4 + j
                    par = kt % 2
                    ktile, vtile = ktileP[par], vtileP[par]
                    tfa, tfb = tfaP[par], tfbP[par]
                    psK = P[4] if par == 0 else P[3]
                    psV = P[2] if par == 0 else P[1]
                    for wb, bB, dst, tf, do_elu, ps in ((wkb, bkB, ktile, tfa, True, psK),
                                                        (wvb, bvB, vtile, tfb, False, psV)):
                        for o2 in range(2):
                            for ci in range(DC):
                                kb.do("tensor", lambda t, ci=ci, o2=o2, j=j, wb=wb, ps=ps: t.matmul(
                                    ps.t[:, 0:W], xnt[ci].t[:, j * 128:(j + 1) * 128],
                                    wb.t[:, ci * 1024 + o2 * 512:ci * 1024 + o2 * 512 + 512],
                                    start=(ci == 0), stop=(ci == DC - 1)),
                                    reads=[xnt[ci], wb], writes=[ps], inc=1)
                            if do_elu:
                                kb.do("vector", lambda v, o2=o2, tf=tf, bB=bB, ps=ps: v.tensor_add(
                                    out=tf.t[:, o2 * 512:o2 * 512 + 512], in0=bB.t[:, o2 * 512:o2 * 512 + 512],
                                    in1=ps.t[:, 0:W]), reads=[ps, bB], writes=[tf])
                            else:
                                kb.do("vector", lambda v, o2=o2, dst=dst, bB=bB, ps=ps: v.tensor_add(
                                    out=dst.t[:, o2 * 512:o2 * 512 + 512], in0=bB.t[:, o2 * 512:o2 * 512 + 512],
                                    in1=ps.t[:, 0:W]), reads=[ps, bB], writes=[dst])
                        if do_elu:
                            kb.do("vector", lambda v, tf=tf, tfb=tfb: v.tensor_scalar(
                                out=tfb.t[:], in0=tf.t[:], scalar1=0.0, scalar2=None, op0=ALU.min),
                                reads=[tf], writes=[tfb])
                            kb.do("scalar", lambda s, tfb=tfb: s.activation(out=tfb.t[:], in_=tfb.t[:], func=AF.Exp),
                                  reads=[tfb], writes=[tfb])
                            kb.do("vector", lambda v, tf=tf: v.tensor_scalar(
                                out=tf.t[:], in0=tf.t[:], scalar1=0.0, scalar2=None, op0=ALU.max),
                                reads=[tf], writes=[tf])
                            kb.do("vector", lambda v, tf=tf, tfb=tfb, dst=dst: v.tensor_add(
                                out=dst.t[:], in0=tf.t[:], in1=tfb.t[:]),
                                reads=[tf, tfb], writes=[dst])
                    for h in range(16):
                        kb.do("tensor", lambda t, h=h, kt=kt, ktile=ktile, vtile=vtile: t.matmul(
                            Pkv.t[0:64, h * 64:(h + 1) * 64], ktile.t[:, h * 64:(h + 1) * 64],
                            vtile.t[:, h * 64:(h + 1) * 64], start=(kt == 0), stop=(kt == nkt_total - 1)),
                            reads=[ktile, vtile], writes=[Pkv], inc=1)
                    for oo in range(DC):
                        kb.do("tensor", lambda t, oo=oo, kt=kt, ktile=ktile: t.matmul(
                            Pks.t[:, oo:oo + 1], ktile.t[:, oo * 128:(oo + 1) * 128],
                            c["ones_bcol"].t[:], start=(kt == 0), stop=(kt == nkt_total - 1)),
                            reads=[ktile, c["ones_bcol"]], writes=[Pks], inc=1)
            kb.do("vector", lambda v: v.tensor_copy(out=kv_sb.t[0:64, :], in_=Pkv.t[0:64, :]),
                  reads=[Pkv], writes=[kv_sb])
            dma_out(kb, kvb_d[:, :], kv_sb, src_sl=kv_sb.t[0:64, :])
            dma_in(kb, kv_sb, kvb_d[:, :], dst_sl=kv_sb.t[64:128, :])
            kb.do("vector", lambda v: v.tensor_copy(out=ks_sb.t[:], in_=Pks.t[:, :]),
                  reads=[Pks], writes=[ks_sb])
            kb.barrier()

        s2 = ExitStack()
        with s2:
            wqb = sb("wqb", [128, DC * 1024], BF16, s2)
            wstg = sb("wstg2", [128, 1024], F32, s2)
            q_sb = sb("q_sb", [128, DC * NT], BF16, s2)
            att = sb("att", [128, DC * NT], BF16, s2)
            bcol = sb("bcol", [128, 8], F32, s2)
            qta = sb("qta2", [128, W], F32, s2)
            qtb = sb("qtb2", [128, W], F32, s2)
            nsb = sb("nsb", [64, W], F32, s2)
            srow = sb("srow1", [1, 16], F32, s2)
            for ci in range(DC):
                dma_in(kb, wstg, wq_d[:, ci * 1024:(ci + 1) * 1024])
                cast(kb, wqb, wqb.t[:, ci * 1024:(ci + 1) * 1024], wstg, wstg.t[:])
            dma_in(kb, bcol, bq_d[:, :])
            dma_in(kb, srow, stdp_d[:, :])
            kb.do("scalar", lambda s: s.activation(out=srow.t[:], in_=srow.t[:], func=AF.Sigmoid),
                  reads=[srow], writes=[srow])
            kb.do("tensor", lambda t: t.matmul(P[4].t[0:64, 0:16], c["ones_row"].t[0:1, 0:64], srow.t[:],
                                               start=True, stop=True), reads=[c["ones_row"], srow], writes=[P[4]])
            kb.do("vector", lambda v: v.tensor_copy(out=sigB.t[:], in_=P[4].t[0:64, 0:16]),
                  reads=[P[4]], writes=[sigB])
            for oo in range(DC):
                for tc in range(NT // W):
                    col = tc * W
                    for ci in range(DC):
                        kb.do("tensor", lambda t, ci=ci, oo=oo, col=col: t.matmul(
                            P[0].t[:, 0:W], wqb.t[:, ci * 1024 + oo * 128:ci * 1024 + oo * 128 + 128],
                            xn[ci].t[:, col:col + W], start=(ci == 0), stop=(ci == DC - 1)),
                            reads=[wqb, xn[ci]], writes=[P[0]], inc=1)
                    kb.do("vector", lambda v, oo=oo: v.tensor_scalar(
                        out=qta.t[:], in0=P[0].t[:, 0:W], scalar1=bcol.t[:, oo:oo + 1], scalar2=None,
                        op0=ALU.add), reads=[P[0], bcol], writes=[qta])
                    kb.do("vector", lambda v: v.tensor_scalar(
                        out=qtb.t[:], in0=qta.t[:], scalar1=0.0, scalar2=None, op0=ALU.min),
                        reads=[qta], writes=[qtb])
                    kb.do("scalar", lambda s: s.activation(out=qtb.t[:], in_=qtb.t[:], func=AF.Exp),
                          reads=[qtb], writes=[qtb])
                    kb.do("vector", lambda v: v.tensor_scalar(
                        out=qta.t[:], in0=qta.t[:], scalar1=0.0, scalar2=None, op0=ALU.max),
                        reads=[qta], writes=[qta])
                    kb.do("vector", lambda v, oo=oo, col=col: v.tensor_add(
                        out=q_sb.t[:, oo * NT + col:oo * NT + col + W], in0=qta.t[:], in1=qtb.t[:]),
                        reads=[qta, qtb], writes=[q_sb])
            for h in range(16):
                hp, ho = h // 2, 64 * (h % 2)
                for tc in range(NT // W):
                    col = tc * W
                    qsl = q_sb.t[ho:ho + 64, hp * NT + col:hp * NT + col + W]
                    kb.do("tensor", lambda t, h=h, ho=ho, qsl=qsl: t.matmul(
                        P[0].t[0:64, 0:W], kv_sb.t[ho:ho + 64, h * 64:(h + 1) * 64], qsl,
                        start=True, stop=True), reads=[kv_sb, q_sb], writes=[P[0]])
                    kb.do("tensor", lambda t, h=h, hp=hp, ho=ho, qsl=qsl: t.matmul(
                        P[1].t[0:1, 0:W], ks_sb.t[ho:ho + 64, hp:hp + 1], qsl,
                        start=True, stop=True), reads=[ks_sb, q_sb], writes=[P[1]])
                    kb.do("vector", lambda v: v.tensor_scalar(
                        out=rowt.t[:], in0=P[1].t[0:1, 0:W], scalar1=1e-6, scalar2=None, op0=ALU.add),
                        reads=[P[1]], writes=[rowt])
                    kb.do("vector", lambda v: v.reciprocal(out=rowt.t[:], in_=rowt.t[:]),
                          reads=[rowt], writes=[rowt])
                    kb.do("tensor", lambda t: t.matmul(P[2].t[0:64, 0:W], c["ones_row"].t[0:1, 0:64],
                                                       rowt.t[:], start=True, stop=True),
                          reads=[c["ones_row"], rowt], writes=[P[2]])
                    kb.do("scalar", lambda s: s.copy(out=nsb.t[:], in_=P[0].t[0:64, 0:W]),
                          reads=[P[0]], writes=[nsb])
                    kb.do("vector", lambda v, h=h, hp=hp, ho=ho, col=col: v.scalar_tensor_tensor(
                        out=att.t[ho:ho + 64, hp * NT + col:hp * NT + col + W], in0=nsb.t[:],
                        scalar=sigB.t[0:64, h:h + 1], in1=P[2].t[0:64, 0:W],
                        op0=ALU.mult, op1=ALU.mult), reads=[nsb, sigB, P[2]], writes=[att])
            for ci in range(DC):
                dma_in(kb, wstg, wo_d[:, ci * 1024:(ci + 1) * 1024])
                cast(kb, wqb, wqb.t[:, ci * 1024:(ci + 1) * 1024], wstg, wstg.t[:])
            dma_in(kb, bcol, bo_d[:, :])
            for oo in range(DC):
                for tc in range(NT // W):
                    col = tc * W
                    for ci in range(DC):
                        kb.do("tensor", lambda t, ci=ci, oo=oo, col=col: t.matmul(
                            P[0].t[:, 0:W], wqb.t[:, ci * 1024 + oo * 128:ci * 1024 + oo * 128 + 128],
                            att.t[:, ci * NT + col:ci * NT + col + W], start=(ci == 0), stop=(ci == DC - 1)),
                            reads=[wqb, att], writes=[P[0]], inc=1)
                    kb.do("vector", lambda v, oo=oo: v.tensor_scalar(
                        out=qta.t[:], in0=P[0].t[:, 0:W], scalar1=bcol.t[:, oo:oo + 1], scalar2=None,
                        op0=ALU.add), reads=[P[0], bcol], writes=[qta])
                    dma_out(kb, s1sp_d[oo * 128:(oo + 1) * 128, col:col + W], qta)
            kb.barrier()

        s3 = ExitStack()
        with s3:
            racc = [sb(f"racc{i}", [128, NT], F32, s3) for i in range(DC)]
            wstg = sb("wstg3", [128, 1024], F32, s3)
            rwb = sb("rwb", [128, DC * 16], F32, s3)
            rwbb = sb("rwbb", [128, DC * 16], BF16, s3)
            ebB = sb("ebB", [128, 16], F32, s3)
            erow = sb("erow", [1, 16], F32, s3)
            iot = sb("iot", [128, 16], F32, s3)
            iot99 = sb("iot99", [128, 16], F32, s3)
            sct = sb("sct", [128, 16], F32, s3)
            scs = sb("scs", [128, 16], F32, s3)
            cur = sb("cur", [128, 16], F32, s3)
            tmp = sb("tmp", [128, 16], F32, s3)
            tmp2 = sb("tmp2", [128, 16], F32, s3)
            soh = sb("soh", [128, 16], F32, s3)
            hi = sb("hi", [128, 8], F32, s3)
            lo = sb("lo", [128, 8], F32, s3)
            gsb = sb("gsb", [128, 4], F32, s3)
            col1 = sb("col1", [128, 1], F32, s3)
            col2 = sb("col2", [128, 1], F32, s3)
            col3 = sb("col3", [128, 1], F32, s3)
            tsum = sb("tsum", [128, 1], F32, s3)
            tw = sb("tw", [128, 1], F32, s3)
            t1 = sb("t1l", [128, W], F32, s3)
            t2 = sb("t2l", [128, W], F32, s3)
            xrr = sb("xrr", [128, W], F32, s3)

            dma_in(kb, rwb, rw_d[:, :])
            cast(kb, rwbb, rwbb.t[:], rwb, rwb.t[:])
            dma_in(kb, erow, eb_d[:, :])
            kb.do("tensor", lambda t: t.matmul(P[4].t[:, 0:16], c["ones_row"].t[:], erow.t[:],
                                               start=True, stop=True), reads=[c["ones_row"], erow], writes=[P[4]])
            kb.do("vector", lambda v: v.tensor_copy(out=ebB.t[:], in_=P[4].t[:, 0:16]),
                  reads=[P[4]], writes=[ebB])
            kb.do("gpsimd", lambda g: g.iota(iot.t[:], pattern=[[1, 16]], base=0, channel_multiplier=0,
                                             allow_small_or_imprecise_dtypes=True), writes=[iot])
            kb.do("gpsimd", lambda g: g.iota(iot99.t[:], pattern=[[1, 16]], base=-99, channel_multiplier=0,
                                             allow_small_or_imprecise_dtypes=True), writes=[iot99])

            for tt in range(NT // 128):
                for ci in range(DC):
                    kb.do("tensor", lambda t, ci=ci, tt=tt: t.matmul(
                        P[0].t[:, 0:16], xn[ci].t[:, tt * 128:(tt + 1) * 128],
                        rwbb.t[:, ci * 16:(ci + 1) * 16], start=(ci == 0), stop=(ci == DC - 1)),
                        reads=[xn[ci], rwbb], writes=[P[0]], inc=1)
                kb.do("scalar", lambda s: s.activation(out=scs.t[:], in_=P[0].t[:, 0:16], func=AF.Sigmoid),
                      reads=[P[0]], writes=[scs])
                kb.do("vector", lambda v: v.tensor_add(out=sct.t[:], in0=scs.t[:], in1=ebB.t[:]),
                      reads=[scs, ebB], writes=[sct])
                kb.do("vector", lambda v: v.tensor_tensor(out=hi.t[:], in0=sct.t[:, 0:15:2],
                                                          in1=sct.t[:, 1:16:2], op=ALU.max),
                      reads=[sct], writes=[hi])
                kb.do("vector", lambda v: v.tensor_tensor(out=lo.t[:], in0=sct.t[:, 0:15:2],
                                                          in1=sct.t[:, 1:16:2], op=ALU.min),
                      reads=[sct], writes=[lo])
                kb.do("vector", lambda v: v.tensor_tensor(out=gsb.t[:], in0=hi.t[:, 0:7:2],
                                                          in1=hi.t[:, 1:8:2], op=ALU.max),
                      reads=[hi], writes=[gsb])
                kb.do("vector", lambda v: v.tensor_tensor(out=tmp.t[:, 0:4], in0=hi.t[:, 0:7:2],
                                                          in1=hi.t[:, 1:8:2], op=ALU.min),
                      reads=[hi], writes=[tmp])
                kb.do("vector", lambda v: v.tensor_tensor(out=tmp.t[:, 4:8], in0=lo.t[:, 0:7:2],
                                                          in1=lo.t[:, 1:8:2], op=ALU.max),
                      reads=[lo], writes=[tmp])
                kb.do("vector", lambda v: v.tensor_tensor(out=tmp.t[:, 0:4], in0=tmp.t[:, 0:4],
                                                          in1=tmp.t[:, 4:8], op=ALU.max),
                      reads=[tmp], writes=[tmp])
                kb.do("vector", lambda v: v.tensor_add(out=gsb.t[:], in0=gsb.t[:], in1=tmp.t[:, 0:4]),
                      reads=[tmp, gsb], writes=[gsb])
                kb.do("vector", lambda v: v.tensor_tensor(out=hi.t[:, 0:2], in0=gsb.t[:, 0:3:2],
                                                          in1=gsb.t[:, 1:4:2], op=ALU.max),
                      reads=[gsb], writes=[hi])
                kb.do("vector", lambda v: v.tensor_tensor(out=lo.t[:, 0:2], in0=gsb.t[:, 0:3:2],
                                                          in1=gsb.t[:, 1:4:2], op=ALU.min),
                      reads=[gsb], writes=[lo])
                kb.do("vector", lambda v: v.tensor_tensor(out=col1.t[:], in0=hi.t[:, 0:1],
                                                          in1=hi.t[:, 1:2], op=ALU.min),
                      reads=[hi], writes=[col1])
                kb.do("vector", lambda v: v.tensor_tensor(out=col2.t[:], in0=lo.t[:, 0:1],
                                                          in1=lo.t[:, 1:2], op=ALU.max),
                      reads=[lo], writes=[col2])
                kb.do("vector", lambda v: v.tensor_tensor(out=col1.t[:], in0=col1.t[:],
                                                          in1=col2.t[:], op=ALU.max),
                      reads=[col1, col2], writes=[col1])
                kb.do("vector", lambda v: v.tensor_scalar(out=gsb.t[:], in0=gsb.t[:],
                                                          scalar1=col1.t[:, 0:1], scalar2=None, op0=ALU.is_ge),
                      reads=[gsb, col1], writes=[gsb])
                for g in range(4):
                    kb.do("vector", lambda v, g=g: v.tensor_scalar(
                        out=cur.t[:, g * 4:(g + 1) * 4], in0=sct.t[:, g * 4:(g + 1) * 4],
                        scalar1=gsb.t[:, g:g + 1], scalar2=None, op0=ALU.mult),
                        reads=[sct, gsb], writes=[cur])
                kb.do("vector", lambda v: v.memset(tsum.t[:], 1e-20), writes=[tsum])
                kb.do("vector", lambda v, tt=tt: v.memset(comb.t[:, tt * 16:(tt + 1) * 16], 0.0),
                      writes=[comb])
                for j in range(4):
                    kb.do("vector", lambda v: v.reduce_max(col1.t[:], cur.t[:], axis=AX.X),
                          reads=[cur], writes=[col1])
                    kb.do("vector", lambda v: v.tensor_scalar(out=tmp.t[:], in0=cur.t[:],
                                                              scalar1=col1.t[:, 0:1], scalar2=None, op0=ALU.is_ge),
                          reads=[cur, col1], writes=[tmp])
                    kb.do("vector", lambda v: v.tensor_mul(out=tmp.t[:], in0=tmp.t[:], in1=iot99.t[:]),
                          reads=[tmp, iot99], writes=[tmp])
                    kb.do("vector", lambda v: v.tensor_scalar(out=tmp.t[:], in0=tmp.t[:],
                                                              scalar1=99.0, scalar2=None, op0=ALU.add),
                          reads=[tmp], writes=[tmp])
                    kb.do("vector", lambda v: v.tensor_reduce(out=col2.t[:], in_=tmp.t[:], axis=AX.X,
                                                              op=ALU.min),
                          reads=[tmp], writes=[col2])
                    kb.do("vector", lambda v: v.tensor_scalar(out=soh.t[:], in0=iot.t[:],
                                                              scalar1=col2.t[:, 0:1], scalar2=None, op0=ALU.is_equal),
                          reads=[iot, col2], writes=[soh])
                    kb.do("vector", lambda v: v.tensor_mul(out=tmp2.t[:], in0=soh.t[:], in1=scs.t[:]),
                          reads=[soh, scs], writes=[tmp2])
                    kb.do("vector", lambda v: v.reduce_sum(tw.t[:], tmp2.t[:], axis=AX.X),
                          reads=[tmp2], writes=[tw])
                    kb.do("vector", lambda v: v.tensor_add(out=tsum.t[:], in0=tsum.t[:], in1=tw.t[:]),
                          reads=[tw, tsum], writes=[tsum])
                    kb.do("vector", lambda v, tt=tt: v.scalar_tensor_tensor(
                        out=comb.t[:, tt * 16:(tt + 1) * 16], in0=soh.t[:], scalar=tw.t[:, 0:1],
                        in1=comb.t[:, tt * 16:(tt + 1) * 16], op0=ALU.mult, op1=ALU.add),
                        reads=[soh, tw, comb], writes=[comb])
                    kb.do("vector", lambda v: v.scalar_tensor_tensor(
                        out=cur.t[:], in0=soh.t[:], scalar=-1e9, in1=cur.t[:], op0=ALU.mult, op1=ALU.add),
                        reads=[soh, cur], writes=[cur])
                kb.do("vector", lambda v: v.reciprocal(out=tsum.t[:], in_=tsum.t[:]),
                      reads=[tsum], writes=[tsum])
                kb.do("vector", lambda v: v.tensor_scalar(out=tsum.t[:], in0=tsum.t[:],
                                                          scalar1=2.5, scalar2=None, op0=ALU.mult),
                      reads=[tsum], writes=[tsum])
                kb.do("vector", lambda v, tt=tt: v.tensor_scalar(
                    out=comb.t[:, tt * 16:(tt + 1) * 16], in0=comb.t[:, tt * 16:(tt + 1) * 16],
                    scalar1=tsum.t[:, 0:1], scalar2=None, op0=ALU.mult),
                    reads=[comb, tsum], writes=[comb])
                kb.do("tensor", lambda t, tt=tt: t.transpose(P[3].t[0:16, 0:128],
                                                             comb.t[:, tt * 16:(tt + 1) * 16], c["ident"].t[:]),
                      reads=[comb, c["ident"]], writes=[P[3]])
                kb.do("vector", lambda v, tt=tt: v.tensor_copy(out=ctr.t[0:16, tt * 128:(tt + 1) * 128],
                                                               in_=P[3].t[0:16, 0:128]),
                      reads=[P[3]], writes=[ctr])

            s3b = ExitStack()
            s3b.__enter__()
            shw = sb("shw", [128, DC * 1024], BF16, s3b)
            hs = sb("hs", [128, DC * NT], BF16, s3b)
            for ci in range(DC):
                dma_in(kb, wstg, shg_d[:, ci * 1024:(ci + 1) * 1024])
                cast(kb, shw, shw.t[:, ci * 1024:(ci + 1) * 1024], wstg, wstg.t[:])
            for oo in range(DC):
                for tc in range(NT // W):
                    col = tc * W
                    for ci in range(DC):
                        kb.do("tensor", lambda t, ci=ci, oo=oo, col=col: t.matmul(
                            P[0].t[:, 0:W], shw.t[:, ci * 1024 + oo * 128:ci * 1024 + oo * 128 + 128],
                            xn[ci].t[:, col:col + W], start=(ci == 0), stop=(ci == DC - 1)),
                            reads=[shw, xn[ci]], writes=[P[0]], inc=1)
                    kb.do("scalar", lambda s, oo=oo, col=col: s.activation(
                        out=hs.t[:, oo * NT + col:oo * NT + col + W], in_=P[0].t[:, 0:W], func=AF.Silu),
                        reads=[P[0]], writes=[hs])
            for ci in range(DC):
                dma_in(kb, wstg, shu_d[:, ci * 1024:(ci + 1) * 1024])
                cast(kb, shw, shw.t[:, ci * 1024:(ci + 1) * 1024], wstg, wstg.t[:])
            for oo in range(DC):
                for tc in range(NT // W):
                    col = tc * W
                    for ci in range(DC):
                        kb.do("tensor", lambda t, ci=ci, oo=oo, col=col: t.matmul(
                            P[0].t[:, 0:W], shw.t[:, ci * 1024 + oo * 128:ci * 1024 + oo * 128 + 128],
                            xn[ci].t[:, col:col + W], start=(ci == 0), stop=(ci == DC - 1)),
                            reads=[shw, xn[ci]], writes=[P[0]], inc=1)
                    kb.do("vector", lambda v, oo=oo, col=col: v.tensor_mul(
                        out=hs.t[:, oo * NT + col:oo * NT + col + W],
                        in0=hs.t[:, oo * NT + col:oo * NT + col + W], in1=P[0].t[:, 0:W]),
                        reads=[P[0], hs], writes=[hs])
            for ci in range(DC):
                dma_in(kb, wstg, shd_d[:, ci * 1024:(ci + 1) * 1024])
                cast(kb, shw, shw.t[:, ci * 1024:(ci + 1) * 1024], wstg, wstg.t[:])
            for oo in range(DC):
                for tc in range(NT // W):
                    col = tc * W
                    for ci in range(DC):
                        kb.do("tensor", lambda t, ci=ci, oo=oo, col=col: t.matmul(
                            P[0].t[:, 0:W], shw.t[:, ci * 1024 + oo * 128:ci * 1024 + oo * 128 + 128],
                            hs.t[:, ci * NT + col:ci * NT + col + W], start=(ci == 0), stop=(ci == DC - 1)),
                            reads=[shw, hs], writes=[P[0]], inc=1)
                    kb.do("scalar", lambda s, oo=oo, col=col: s.copy(
                        out=racc[oo].t[:, col:col + W], in_=P[0].t[:, 0:W]),
                        reads=[P[0]], writes=[racc[oo]])

            kb.barrier()
            s3b.__exit__(None, None, None)

            s3c = ExitStack()
            s3c.__enter__()
            h_all = sb("h_all", [128, 4 * NT], BF16, s3c)
            gwb = sb("gwb", [128, DC * 512], BF16, s3c)
            uwb = sb("uwb", [128, DC * 512], BF16, s3c)
            dwb = sb("dwb", [128, 4 * 1024], BF16, s3c)
            oneh = sb("oneh", [16, 16 * 128], F32, s3c)
            kb.do("gpsimd", lambda g: g.iota(oneh.t[:], pattern=[[1, 16], [0, 128]], base=0,
                                             channel_multiplier=-1, allow_small_or_imprecise_dtypes=True),
                  writes=[oneh])
            kb.do("vector", lambda v: v.tensor_scalar(out=oneh.t[:], in0=oneh.t[:], scalar1=0.0,
                                                      scalar2=None, op0=ALU.is_equal),
                  reads=[oneh], writes=[oneh])
            for e in range(16):
                for wd, wb, ncol in ((gw_d, gwb, DC * 512), (uw_d, uwb, DC * 512), (dw_d, dwb, 4 * 1024)):
                    for g4 in range(ncol // 1024):
                        kb.do("sync", lambda eng, wd=wd, e=e, g4=g4: eng.dma_start(
                            out=wstg.t[:], in_=wd[e, :, g4 * 1024:(g4 + 1) * 1024]),
                            writes=[wstg], inc=16, sem_key="ld_" + wstg.name)
                        cast(kb, wb, wb.t[:, g4 * 1024:(g4 + 1) * 1024], wstg, wstg.t[:])
                for tc in range(NT // W):
                    col = tc * W
                    kb.do("tensor", lambda t, e=e, col=col: t.matmul(
                        P[4].t[:, 0:W], oneh.t[0:16, e * 128:(e + 1) * 128],
                        ctr.t[0:16, col:col + W], start=True, stop=True),
                        reads=[oneh, ctr], writes=[P[4]])
                    for io in range(4):
                        for ci in range(DC):
                            kb.do("tensor", lambda t, ci=ci, io=io, col=col: t.matmul(
                                P[0].t[:, 0:W], gwb.t[:, ci * 512 + io * 128:ci * 512 + io * 128 + 128],
                                xn[ci].t[:, col:col + W], start=(ci == 0), stop=(ci == DC - 1)),
                                reads=[gwb, xn[ci]], writes=[P[0]], inc=1)
                            kb.do("tensor", lambda t, ci=ci, io=io, col=col: t.matmul(
                                P[1].t[:, 0:W], uwb.t[:, ci * 512 + io * 128:ci * 512 + io * 128 + 128],
                                xn[ci].t[:, col:col + W], start=(ci == 0), stop=(ci == DC - 1)),
                                reads=[uwb, xn[ci]], writes=[P[1]], inc=1)
                        kb.do("scalar", lambda s: s.activation(out=t1.t[:], in_=P[0].t[:, 0:W], func=AF.Silu),
                              reads=[P[0]], writes=[t1])
                        kb.do("vector", lambda v: v.tensor_mul(out=t2.t[:], in0=t1.t[:], in1=P[1].t[:, 0:W]),
                              reads=[t1, P[1]], writes=[t2])
                        kb.do("vector", lambda v, io=io, col=col: v.tensor_mul(
                            out=h_all.t[:, io * NT + col:io * NT + col + W], in0=t2.t[:], in1=P[4].t[:, 0:W]),
                            reads=[t2, P[4]], writes=[h_all])
                    for oo in range(DC):
                        for io in range(4):
                            kb.do("tensor", lambda t, io=io, oo=oo, col=col: t.matmul(
                                P[2].t[:, 0:W], dwb.t[:, io * 1024 + oo * 128:io * 1024 + oo * 128 + 128],
                                h_all.t[:, io * NT + col:io * NT + col + W],
                                start=(io == 0), stop=(io == 3)),
                                reads=[dwb, h_all], writes=[P[2]], inc=1)
                        kb.do("vector", lambda v, oo=oo, col=col: v.tensor_add(
                            out=racc[oo].t[:, col:col + W], in0=racc[oo].t[:, col:col + W],
                            in1=P[2].t[:, 0:W]), reads=[P[2], racc[oo]], writes=[racc[oo]])

            for oo in range(DC):
                for tc in range(NT // W):
                    col = tc * W
                    kb.do("scalar", lambda s, oo=oo, col=col: s.activation(
                        out=t1.t[:], in_=racc[oo].t[:, col:col + W], func=AF.Silu),
                        reads=[racc[oo]], writes=[t1])
                    dma_in(kb, t2, s1sp_d[oo * 128:(oo + 1) * 128, col:col + W])
                    dma_in(kb, xrr, x_d[oo * 128:(oo + 1) * 128, col:col + W])
                    kb.do("vector", lambda v: v.tensor_mul(out=t1.t[:], in0=t1.t[:], in1=t2.t[:]),
                          reads=[t1, t2], writes=[t1])
                    kb.do("vector", lambda v: v.tensor_add(out=xrr.t[:], in0=xrr.t[:], in1=t1.t[:]),
                          reads=[t1, xrr], writes=[xrr])
                    dma_out(kb, xo_d[oo * 128:(oo + 1) * 128, col:col + W], xrr)
            kb.barrier()
            s3c.__exit__(None, None, None)
        kb.finalize(nc, es)
    return nc



def _fm(x):
    return np.ascontiguousarray(np.transpose(np.asarray(x), (0, 2, 1)))


def _prep_l0a(inputs):
    Q = np.asarray(inputs["L0_attn_Q"])
    K = np.asarray(inputs["L0_attn_K"])
    V = np.asarray(inputs["L0_attn_V"])
    Wt = np.asarray(inputs["L0_attn_W"])
    Qm = np.ascontiguousarray(np.transpose(Q, (1, 2, 0)).reshape(D, 1024))
    Km = np.ascontiguousarray(np.transpose(K, (1, 2, 0)).reshape(D, 1024))
    Vm = np.ascontiguousarray(np.transpose(V, (1, 2, 0)).reshape(D, 1024))
    Wm = np.ascontiguousarray(np.transpose(Wt, (1, 2, 0)).reshape(1024, 1024))
    ww_all = np.ascontiguousarray(np.transpose(Wt, (2, 1, 0)))
    out = []
    for hh in range(2):
        cols = slice(hh * 512, (hh + 1) * 512)
        out.append({
            "qw": np.ascontiguousarray(Qm[:, cols].reshape(DC, 128, 512)),
            "kw": np.ascontiguousarray(Km[:, cols].reshape(DC, 128, 512)),
            "vw": np.ascontiguousarray(Vm[:, cols].reshape(DC, 128, 512)),
            "ww": np.ascontiguousarray(ww_all[hh * 8:(hh + 1) * 8]),
        })
    return out


def _chunk_w(wT, ncols):
    din, dout = wT.shape
    nch = din // 128
    return np.ascontiguousarray(wT.reshape(nch, 128, dout).transpose(1, 0, 2).reshape(128, nch * dout))


def _prep_l0b(inputs):
    w1T = np.ascontiguousarray(np.asarray(inputs["L0_ffn_w1_w"]).T)
    w2T = np.ascontiguousarray(np.asarray(inputs["L0_ffn_w2_w"]).T)
    h1T = np.ascontiguousarray(np.asarray(inputs["L0_ffn_h1_w"]).T)
    h2T = np.ascontiguousarray(np.asarray(inputs["L0_ffn_h2_w"]).T)
    w3T = np.ascontiguousarray(np.asarray(inputs["L0_ffn_w3_w"]).T)
    d = {
        "w1": _chunk_w(w1T, 4096), "w2": _chunk_w(w2T, 4096),
        "h1": _chunk_w(h1T, 4096), "h2": _chunk_w(h2T, 4096),
        "w3": _chunk_w(w3T, 1024),
        "s1": np.ascontiguousarray(np.asarray(inputs["L0_ffn_w1_s"]).reshape(32, 128).T),
        "s2": np.ascontiguousarray(np.asarray(inputs["L0_ffn_w2_s"]).reshape(32, 128).T),
        "sh1": np.ascontiguousarray(np.asarray(inputs["L0_ffn_h1_s"]).reshape(32, 128).T),
        "sh2": np.ascontiguousarray(np.asarray(inputs["L0_ffn_h2_s"]).reshape(32, 128).T),
        "s3": np.ascontiguousarray(np.asarray(inputs["L0_ffn_w3_s"]).reshape(8, 128).T),
    }
    return d


def run_l0(X0fm, inputs):
    heads = _prep_l0a(inputs)
    nc_a = build_l0a()
    in_maps = []
    for core in range(8):
        b, hh = core // 2, core % 2
        m = {"x": X0fm[b]}
        m.update(heads[hh])
        in_maps.append(m)
    res = run_bass_kernel_spmd(nc_a, in_maps, core_ids=list(range(8)))
    zparts = [res.results[c]["z"] for c in range(8)]

    ffn = _prep_l0b(inputs)
    nc_b = build_l0b()
    in_maps = []
    for core in range(8):
        b, half = core // 2, core % 2
        colsl = slice(half * NT, (half + 1) * NT)
        m = {"x": np.ascontiguousarray(X0fm[b][:, colsl]),
             "za": np.ascontiguousarray(zparts[2 * b][:, colsl]),
             "zb": np.ascontiguousarray(zparts[2 * b + 1][:, colsl])}
        m.update(ffn)
        in_maps.append(m)
    res = run_bass_kernel_spmd(nc_b, in_maps, core_ids=list(range(8)))
    X1 = np.empty((B, D, S), np.float32)
    for core in range(8):
        b, half = core // 2, core % 2
        X1[b][:, half * NT:(half + 1) * NT] = res.results[core]["xo"]
    return X1



def build_l2():
    nc = bass.Bass("TRN2", target_bir_lowering=False, debug=False)
    x_d = nc.dram_tensor("x", [D, S], F32, kind="ExternalInput").ap()
    wq_d = nc.dram_tensor("wq", [128, DC * 1024], F32, kind="ExternalInput").ap()
    wk_d = nc.dram_tensor("wk", [128, DC * 1024], F32, kind="ExternalInput").ap()
    wv_d = nc.dram_tensor("wv", [128, DC * 1024], F32, kind="ExternalInput").ap()
    wo_d = nc.dram_tensor("wo", [128, DC * 1024], F32, kind="ExternalInput").ap()
    bq_d = nc.dram_tensor("bq", [128, 8], F32, kind="ExternalInput").ap()
    bk_d = nc.dram_tensor("bk", [128, 8], F32, kind="ExternalInput").ap()
    bv_d = nc.dram_tensor("bv", [1, 1024], F32, kind="ExternalInput").ap()
    bo_d = nc.dram_tensor("bo", [128, 8], F32, kind="ExternalInput").ap()
    rm_d = nc.dram_tensor("rm", [64, 256], F32, kind="ExternalInput").ap()
    wr_d = nc.dram_tensor("wr", [128, DC * 1024], F32, kind="ExternalInput").ap()
    wk2_d = nc.dram_tensor("wk2", [128, DC * 1024], F32, kind="ExternalInput").ap()
    wv2_d = nc.dram_tensor("wv2", [128, DC * 1024], F32, kind="ExternalInput").ap()
    mur_d = nc.dram_tensor("mur", [128, 8], F32, kind="ExternalInput").ap()
    muk_d = nc.dram_tensor("muk", [128, 8], F32, kind="ExternalInput").ap()
    pm_d = nc.dram_tensor("pm", [1, 1], F32, kind="ExternalInput").ap()
    s2sp_d = nc.dram_tensor("s2sp", [D, NT], F32, kind="Internal").ap()
    xo_d = nc.dram_tensor("xo", [D, NT], F32, kind="ExternalOutput").ap()

    kb = KB()
    es = ExitStack()
    with es:
        def sb(n, sh, dt, stack=None):
            st = stack or es
            return Buf(st.enter_context(nc.sbuf_tensor(n, sh, dt)), n)
        c = mk_consts(nc, es, kb)
        P = [Buf(es.enter_context(nc.psum_tensor(f"P{i}", [128, 512], F32)), f"P{i}") for i in range(3)]
        sc = mk_ln_scratch(nc, es, "ln")
        xn = [sb(f"xn{i}", [128, NT], BF16) for i in range(DC)]
        xn_last = sb("xn_last", [128, 8], BF16)
        kv_sb = [sb(f"kv2_{i}", [128, 1024], BF16) for i in range(2)]
        ks_sb = sb("ks2", [128, 32], BF16)
        rmb = sb("rmb", [128, 256], BF16)
        rmf = sb("rmf", [64, 256], F32)
        rowt = sb("rowt2", [1, 512], F32)

        dma_in(kb, rmf, rm_d[:, :])
        cast(kb, rmb, rmb.t[0:64, :], rmf, rmf.t[:])
        cast(kb, rmb, rmb.t[64:128, :], rmf, rmf.t[:])

        def x_ap(ci, col):
            return x_d[ci * 128:(ci + 1) * 128, col:col + W]

        s1 = ExitStack()
        with s1:
            Pkv = [Buf(es.enter_context(nc.psum_tensor(f"Pkv2_{i}", [128, 1024], F32)), f"Pkv2_{i}")
                   for i in range(2)]
            Pks = Buf(es.enter_context(nc.psum_tensor("Pks2", [128, 32], F32)), "Pks2")
            wkb = sb("wkb", [128, DC * 1024], BF16, s1)
            wvb = sb("wvb", [128, DC * 1024], BF16, s1)
            wstg = sb("wstgA", [128, 1024], F32, s1)
            bkc = sb("bkc", [128, 8], F32, s1)
            bvB = sb("bvB", [128, 1024], F32, s1)
            brow = sb("browA", [1, 1024], F32, s1)
            kfm = sb("kfm", [128, DC * 512], BF16, s1)
            vtm = sb("vtm", [128, 4 * 1024], BF16, s1)
            kp = sb("kp", [128, 256], BF16, s1)
            fta = sb("ftaA", [128, 256], F32, s1)
            ftb = sb("ftbA", [128, 256], F32, s1)
            xnt = [sb(f"xntA{i}", [128, W], BF16, s1) for i in range(DC)]
            for ci in range(DC):
                dma_in(kb, wstg, wk_d[:, ci * 1024:(ci + 1) * 1024])
                cast(kb, wkb, wkb.t[:, ci * 1024:(ci + 1) * 1024], wstg, wstg.t[:])
                dma_in(kb, wstg, wv_d[:, ci * 1024:(ci + 1) * 1024])
                cast(kb, wvb, wvb.t[:, ci * 1024:(ci + 1) * 1024], wstg, wstg.t[:])
            dma_in(kb, bkc, bk_d[:, :])
            dma_in(kb, brow, bv_d[:, :])
            for o2 in range(2):
                kb.do("tensor", lambda t, o2=o2: t.matmul(P[2].t[:, 0:512], c["ones_row"].t[:],
                                                          brow.t[0:1, o2 * 512:(o2 + 1) * 512],
                                                          start=True, stop=True),
                      reads=[c["ones_row"], brow], writes=[P[2]])
                kb.do("vector", lambda v, o2=o2: v.tensor_copy(out=bvB.t[:, o2 * 512:(o2 + 1) * 512],
                                                               in_=P[2].t[:, 0:512]),
                      reads=[P[2]], writes=[bvB])
            for tci in range(S // W):
                col = tci * W
                emit_ln(kb, c, x_ap, xnt, col, W, P, sc, out_col0=0)
                if tci < NT // W:
                    for ci in range(DC):
                        kb.do("vector", lambda v, ci=ci, col=col: v.tensor_copy(
                            out=xn[ci].t[:, col:col + W], in_=xnt[ci].t[:]),
                            reads=[xnt[ci]], writes=[xn[ci]])
                if tci == S // W - 1:
                    for ci in range(DC):
                        kb.do("vector", lambda v, ci=ci: v.tensor_copy(
                            out=xn_last.t[:, ci:ci + 1], in_=xnt[ci].t[:, W - 1:W]),
                            reads=[xnt[ci]], writes=[xn_last])
                for oo in range(DC):
                    for ci in range(DC):
                        kb.do("tensor", lambda t, ci=ci, oo=oo: t.matmul(
                            P[2].t[:, 0:W], wkb.t[:, ci * 1024 + oo * 128:ci * 1024 + oo * 128 + 128],
                            xnt[ci].t[:], start=(ci == 0), stop=(ci == DC - 1)),
                            reads=[wkb, xnt[ci]], writes=[P[2]], inc=1)
                    kb.do("vector", lambda v, oo=oo: v.tensor_scalar(
                        out=kfm.t[:, oo * W:(oo + 1) * W], in0=P[2].t[:, 0:W],
                        scalar1=bkc.t[:, oo:oo + 1], scalar2=None, op0=ALU.add),
                        reads=[P[2], bkc], writes=[kfm])
                for j in range(4):
                    for o2 in range(2):
                        for ci in range(DC):
                            kb.do("tensor", lambda t, ci=ci, j=j, o2=o2: t.matmul(
                                P[2].t[:, 0:W], xnt[ci].t[:, j * 128:(j + 1) * 128],
                                wvb.t[:, ci * 1024 + o2 * 512:ci * 1024 + o2 * 512 + 512],
                                start=(ci == 0), stop=(ci == DC - 1)),
                                reads=[xnt[ci], wvb], writes=[P[2]], inc=1)
                        kb.do("vector", lambda v, j=j, o2=o2: v.tensor_add(
                            out=vtm.t[:, j * 1024 + o2 * 512:j * 1024 + o2 * 512 + 512],
                            in0=bvB.t[:, o2 * 512:(o2 + 1) * 512], in1=P[2].t[:, 0:W]),
                            reads=[P[2], bvB], writes=[vtm])
                for h in range(16):
                    hp, ho = h // 2, 64 * (h % 2)
                    for j in range(4):
                        kt_first = (tci == 0 and j == 0)
                        kt_last = (tci == S // W - 1 and j == 3)
                        kb.do("tensor", lambda t, h=h, hp=hp, ho=ho, j=j: t.matmul(
                            P[2].t[:, 0:256], kfm.t[ho:ho + 64, hp * W + j * 128:hp * W + j * 128 + 128],
                            rmb.t[ho:ho + 64, :], start=True, stop=True),
                            reads=[kfm, rmb], writes=[P[2]])
                        kb.do("vector", lambda v: v.tensor_scalar(
                            out=fta.t[:], in0=P[2].t[:, 0:256], scalar1=0.0, scalar2=None, op0=ALU.min),
                            reads=[P[2]], writes=[fta])
                        kb.do("scalar", lambda s: s.activation(out=fta.t[:], in_=fta.t[:], func=AF.Exp),
                              reads=[fta], writes=[fta])
                        kb.do("vector", lambda v: v.tensor_scalar(
                            out=ftb.t[:], in0=P[2].t[:, 0:256], scalar1=0.0, scalar2=None, op0=ALU.max),
                            reads=[P[2]], writes=[ftb])
                        kb.do("vector", lambda v: v.tensor_add(out=fta.t[:], in0=fta.t[:], in1=ftb.t[:]),
                              reads=[ftb, fta], writes=[fta])
                        kb.do("vector", lambda v: v.tensor_scalar(
                            out=kp.t[:], in0=fta.t[:], scalar1=0.0625, scalar2=None, op0=ALU.mult),
                            reads=[fta], writes=[kp])
                        for mc in range(2):
                            kb.do("tensor", lambda t, mc=mc, h=h, j=j, kt_first=kt_first, kt_last=kt_last: t.matmul(
                                Pkv[mc].t[:, h * 64:(h + 1) * 64], kp.t[:, mc * 128:(mc + 1) * 128],
                                vtm.t[:, j * 1024 + h * 64:j * 1024 + h * 64 + 64],
                                start=kt_first, stop=kt_last),
                                reads=[kp, vtm], writes=[Pkv[mc]], inc=1)
                            kb.do("tensor", lambda t, mc=mc, h=h, kt_first=kt_first, kt_last=kt_last: t.matmul(
                                Pks.t[:, mc * 16 + h:mc * 16 + h + 1], kp.t[:, mc * 128:(mc + 1) * 128],
                                c["ones_bcol"].t[:], start=kt_first, stop=kt_last),
                                reads=[kp, c["ones_bcol"]], writes=[Pks], inc=1)
            for mc in range(2):
                kb.do("vector", lambda v, mc=mc: v.tensor_copy(out=kv_sb[mc].t[:], in_=Pkv[mc].t[:, :]),
                      reads=[Pkv[mc]], writes=[kv_sb[mc]])
            kb.do("vector", lambda v: v.tensor_copy(out=ks_sb.t[:], in_=Pks.t[:, :]),
                  reads=[Pks], writes=[ks_sb])
            kb.barrier()

        s2s = ExitStack()
        with s2s:
            wqb = sb("wqbC", [128, DC * 1024], BF16, s2s)
            wstg = sb("wstgC", [128, 1024], F32, s2s)
            bqc = sb("bqc", [128, 8], F32, s2s)
            qfm = sb("qfm", [128, DC * 512], BF16, s2s)
            qp = sb("qp", [128, 1024], BF16, s2s)
            att = sb("attC", [128, DC * NT], BF16, s2s)
            cta = sb("ctaC", [128, 512], F32, s2s)
            ctb = sb("ctbC", [128, 512], F32, s2s)
            nsb = sb("nsbC", [64, 512], F32, s2s)
            for ci in range(DC):
                dma_in(kb, wstg, wq_d[:, ci * 1024:(ci + 1) * 1024])
                cast(kb, wqb, wqb.t[:, ci * 1024:(ci + 1) * 1024], wstg, wstg.t[:])
            dma_in(kb, bqc, bq_d[:, :])
            for tc in range(NT // W):
                col = tc * W
                for oo in range(DC):
                    for ci in range(DC):
                        kb.do("tensor", lambda t, ci=ci, oo=oo, col=col: t.matmul(
                            P[2].t[:, 0:W], wqb.t[:, ci * 1024 + oo * 128:ci * 1024 + oo * 128 + 128],
                            xn[ci].t[:, col:col + W], start=(ci == 0), stop=(ci == DC - 1)),
                            reads=[wqb, xn[ci]], writes=[P[2]], inc=1)
                    kb.do("vector", lambda v, oo=oo: v.tensor_scalar(
                        out=qfm.t[:, oo * W:(oo + 1) * W], in0=P[2].t[:, 0:W],
                        scalar1=bqc.t[:, oo:oo + 1], scalar2=None, op0=ALU.add),
                        reads=[P[2], bqc], writes=[qfm])
                for h in range(16):
                    hp, ho = h // 2, 64 * (h % 2)
                    for mc in range(2):
                        kb.do("tensor", lambda t, h=h, hp=hp, ho=ho, mc=mc: t.matmul(
                            P[2].t[:, 0:W], rmb.t[ho:ho + 64, mc * 128:(mc + 1) * 128],
                            qfm.t[ho:ho + 64, hp * W:hp * W + W], start=True, stop=True),
                            reads=[rmb, qfm], writes=[P[2]])
                        kb.do("vector", lambda v: v.tensor_scalar(
                            out=cta.t[:], in0=P[2].t[:, 0:W], scalar1=0.0, scalar2=None, op0=ALU.min),
                            reads=[P[2]], writes=[cta])
                        kb.do("scalar", lambda s: s.activation(out=cta.t[:], in_=cta.t[:], func=AF.Exp),
                              reads=[cta], writes=[cta])
                        kb.do("vector", lambda v: v.tensor_scalar(
                            out=ctb.t[:], in0=P[2].t[:, 0:W], scalar1=0.0, scalar2=None, op0=ALU.max),
                            reads=[P[2]], writes=[ctb])
                        kb.do("vector", lambda v: v.tensor_add(out=cta.t[:], in0=cta.t[:], in1=ctb.t[:]),
                              reads=[ctb, cta], writes=[cta])
                        kb.do("vector", lambda v, mc=mc: v.tensor_scalar(
                            out=qp.t[:, mc * 512:(mc + 1) * 512], in0=cta.t[:], scalar1=0.0625,
                            scalar2=None, op0=ALU.mult),
                            reads=[cta], writes=[qp])
                    for mc in range(2):
                        kb.do("tensor", lambda t, h=h, mc=mc: t.matmul(
                            P[0].t[0:64, 0:W], kv_sb[mc].t[:, h * 64:(h + 1) * 64],
                            qp.t[:, mc * 512:(mc + 1) * 512], start=(mc == 0), stop=(mc == 1)),
                            reads=[kv_sb[mc], qp], writes=[P[0]], inc=1)
                        kb.do("tensor", lambda t, h=h, mc=mc: t.matmul(
                            P[1].t[0:1, 0:W], ks_sb.t[:, mc * 16 + h:mc * 16 + h + 1],
                            qp.t[:, mc * 512:(mc + 1) * 512], start=(mc == 0), stop=(mc == 1)),
                            reads=[ks_sb, qp], writes=[P[1]], inc=1)
                    kb.do("vector", lambda v: v.tensor_scalar(
                        out=rowt.t[:], in0=P[1].t[0:1, 0:W], scalar1=1e-6, scalar2=None, op0=ALU.max),
                        reads=[P[1]], writes=[rowt])
                    kb.do("vector", lambda v: v.reciprocal(out=rowt.t[:], in_=rowt.t[:]),
                          reads=[rowt], writes=[rowt])
                    kb.do("tensor", lambda t: t.matmul(P[1].t[0:64, 0:W], c["ones_row"].t[0:1, 0:64],
                                                       rowt.t[:], start=True, stop=True),
                          reads=[c["ones_row"], rowt], writes=[P[1]])
                    kb.do("scalar", lambda s: s.copy(out=nsb.t[:], in_=P[0].t[0:64, 0:W]),
                          reads=[P[0]], writes=[nsb])
                    kb.do("vector", lambda v, h=h, hp=hp, ho=ho, col=col: v.tensor_mul(
                        out=att.t[ho:ho + 64, hp * NT + col:hp * NT + col + W], in0=nsb.t[:],
                        in1=P[1].t[0:64, 0:W]), reads=[nsb, P[1]], writes=[att])
            for ci in range(DC):
                dma_in(kb, wstg, wo_d[:, ci * 1024:(ci + 1) * 1024])
                cast(kb, wqb, wqb.t[:, ci * 1024:(ci + 1) * 1024], wstg, wstg.t[:])
            dma_in(kb, bqc, bo_d[:, :])
            for oo in range(DC):
                for tc in range(NT // W):
                    col = tc * W
                    for ci in range(DC):
                        kb.do("tensor", lambda t, ci=ci, oo=oo, col=col: t.matmul(
                            P[2].t[:, 0:W], wqb.t[:, ci * 1024 + oo * 128:ci * 1024 + oo * 128 + 128],
                            att.t[:, ci * NT + col:ci * NT + col + W], start=(ci == 0), stop=(ci == DC - 1)),
                            reads=[wqb, att], writes=[P[2]], inc=1)
                    kb.do("vector", lambda v, oo=oo: v.tensor_scalar(
                        out=cta.t[:], in0=P[2].t[:, 0:W], scalar1=bqc.t[:, oo:oo + 1], scalar2=None,
                        op0=ALU.add), reads=[P[2], bqc], writes=[cta])
                    dma_out(kb, s2sp_d[oo * 128:(oo + 1) * 128, col:col + W], cta)
            kb.barrier()

        s3s = ExitStack()
        with s3s:
            r_all = [sb(f"r{i}", [128, NT], BF16, s3s) for i in range(DC)]
            ka_all = [sb(f"ka{i}", [128, NT], BF16, s3s) for i in range(DC)]
            wrb = sb("wrbE", [128, DC * 1024], BF16, s3s)
            wstg = sb("wstgE", [128, 1024], F32, s3s)
            tfa = sb("tfaE", [128, 512], F32, s3s)
            tfb = sb("tfbE", [128, 512], F32, s3s)
            xrr = sb("xrrE", [128, 512], F32, s3s)
            murc = sb("murc", [128, 8], F32, s3s)
            mukc = sb("mukc", [128, 8], F32, s3s)
            pcol = sb("pcol", [128, 1], F32, s3s)
            prow = sb("prow", [1, 1], F32, s3s)
            dd = sb("dd", [128, NT], F32, s3s)
            colt = sb("colt", [128, 1], F32, s3s)
            colt2 = sb("colt2", [128, 1], F32, s3s)
            dma_in(kb, murc, mur_d[:, :])
            dma_in(kb, mukc, muk_d[:, :])
            dma_in(kb, prow, pm_d[:, :])
            kb.do("tensor", lambda t: t.matmul(P[2].t[:, 0:1], c["ones_row"].t[:], prow.t[:],
                                               start=True, stop=True),
                  reads=[c["ones_row"], prow], writes=[P[2]])
            kb.do("vector", lambda v: v.tensor_copy(out=pcol.t[:], in_=P[2].t[:, 0:1]),
                  reads=[P[2]], writes=[pcol])
            def build_shift(mu, xall):
                for ci in range(DC):
                    dst = xall[ci]
                    kb.do("vector", lambda v, ci=ci: v.tensor_scalar(
                        out=colt.t[:], in0=xn_last.t[:, ci:ci + 1], scalar1=pcol.t[:, 0:1],
                        scalar2=None, op0=ALU.mult), reads=[xn_last, pcol], writes=[colt])
                    kb.do("vector", lambda v, ci=ci: v.tensor_tensor(
                        out=colt2.t[:], in0=xn[ci].t[:, 0:1], in1=colt.t[:], op=ALU.subtract),
                        reads=[xn[ci], colt], writes=[colt2])
                    kb.do("vector", lambda v, ci=ci, mu=mu, dst=dst: v.scalar_tensor_tensor(
                        out=dst.t[:, 0:1], in0=colt2.t[:], scalar=mu.t[:, ci:ci + 1], in1=colt.t[:],
                        op0=ALU.mult, op1=ALU.add), reads=[colt2, mu, colt], writes=[dst])
                    kb.do("vector", lambda v, ci=ci: v.tensor_tensor(
                        out=dd.t[:, 1:NT], in0=xn[ci].t[:, 1:NT], in1=xn[ci].t[:, 0:NT - 1],
                        op=ALU.subtract), reads=[xn[ci]], writes=[dd])
                    kb.do("vector", lambda v, ci=ci, mu=mu, dst=dst: v.scalar_tensor_tensor(
                        out=dst.t[:, 1:NT], in0=dd.t[:, 1:NT], scalar=mu.t[:, ci:ci + 1],
                        in1=xn[ci].t[:, 0:NT - 1], op0=ALU.mult, op1=ALU.add),
                        reads=[dd, mu, xn[ci]], writes=[dst])

            sE1 = ExitStack()
            sE1.__enter__()
            xr_all = [sb(f"xr{i}", [128, NT], BF16, sE1) for i in range(DC)]
            build_shift(murc, xr_all)
            for ci in range(DC):
                dma_in(kb, wstg, wr_d[:, ci * 1024:(ci + 1) * 1024])
                cast(kb, wrb, wrb.t[:, ci * 1024:(ci + 1) * 1024], wstg, wstg.t[:])
            for oo in range(DC):
                for tc in range(NT // W):
                    col = tc * W
                    for ci in range(DC):
                        kb.do("tensor", lambda t, ci=ci, oo=oo, col=col: t.matmul(
                            P[2].t[:, 0:W], wrb.t[:, ci * 1024 + oo * 128:ci * 1024 + oo * 128 + 128],
                            xr_all[ci].t[:, col:col + W], start=(ci == 0), stop=(ci == DC - 1)),
                            reads=[wrb, xr_all[ci]], writes=[P[2]], inc=1)
                    kb.do("scalar", lambda s, oo=oo, col=col: s.activation(
                        out=r_all[oo].t[:, col:col + W], in_=P[2].t[:, 0:W], func=AF.Sigmoid),
                        reads=[P[2]], writes=[r_all[oo]])
            kb.barrier()
            sE1.__exit__(None, None, None)
            sE2 = ExitStack()
            sE2.__enter__()
            xk_all = [sb(f"xk{i}", [128, NT], BF16, sE2) for i in range(DC)]
            build_shift(mukc, xk_all)
            for ci in range(DC):
                dma_in(kb, wstg, wk2_d[:, ci * 1024:(ci + 1) * 1024])
                cast(kb, wrb, wrb.t[:, ci * 1024:(ci + 1) * 1024], wstg, wstg.t[:])
            for oo in range(DC):
                for tc in range(NT // W):
                    col = tc * W
                    for ci in range(DC):
                        kb.do("tensor", lambda t, ci=ci, oo=oo, col=col: t.matmul(
                            P[2].t[:, 0:W], wrb.t[:, ci * 1024 + oo * 128:ci * 1024 + oo * 128 + 128],
                            xk_all[ci].t[:, col:col + W], start=(ci == 0), stop=(ci == DC - 1)),
                            reads=[wrb, xk_all[ci]], writes=[P[2]], inc=1)
                    kb.do("vector", lambda v: v.tensor_scalar(
                        out=tfa.t[:], in0=P[2].t[:, 0:W], scalar1=0.0, scalar2=None, op0=ALU.max),
                        reads=[P[2]], writes=[tfa])
                    kb.do("scalar", lambda s, oo=oo, col=col: s.activation(
                        out=ka_all[oo].t[:, col:col + W], in_=tfa.t[:], func=AF.Square),
                        reads=[tfa], writes=[ka_all[oo]])
            kb.barrier()
            sE2.__exit__(None, None, None)
            for ci in range(DC):
                dma_in(kb, wstg, wv2_d[:, ci * 1024:(ci + 1) * 1024])
                cast(kb, wrb, wrb.t[:, ci * 1024:(ci + 1) * 1024], wstg, wstg.t[:])
            for oo in range(DC):
                for tc in range(NT // W):
                    col = tc * W
                    for ci in range(DC):
                        kb.do("tensor", lambda t, ci=ci, oo=oo, col=col: t.matmul(
                            P[2].t[:, 0:W], wrb.t[:, ci * 1024 + oo * 128:ci * 1024 + oo * 128 + 128],
                            ka_all[ci].t[:, col:col + W], start=(ci == 0), stop=(ci == DC - 1)),
                            reads=[wrb, ka_all[ci]], writes=[P[2]], inc=1)
                    kb.do("vector", lambda v, oo=oo, col=col: v.tensor_mul(
                        out=tfa.t[:], in0=r_all[oo].t[:, col:col + W], in1=P[2].t[:, 0:W]),
                        reads=[r_all[oo], P[2]], writes=[tfa])
                    kb.do("scalar", lambda s: s.activation(out=tfa.t[:], in_=tfa.t[:], func=AF.Silu),
                          reads=[tfa], writes=[tfa])
                    dma_in(kb, tfb, s2sp_d[oo * 128:(oo + 1) * 128, col:col + W])
                    dma_in(kb, xrr, x_d[oo * 128:(oo + 1) * 128, col:col + W])
                    kb.do("vector", lambda v: v.tensor_mul(out=tfa.t[:], in0=tfa.t[:], in1=tfb.t[:]),
                          reads=[tfa, tfb], writes=[tfa])
                    kb.do("vector", lambda v: v.tensor_add(out=xrr.t[:], in0=xrr.t[:], in1=tfa.t[:]),
                          reads=[tfa, xrr], writes=[xrr])
                    dma_out(kb, xo_d[oo * 128:(oo + 1) * 128, col:col + W], xrr)
        kb.finalize(nc, es)
    return nc


def _prep_l2(inputs):
    T = lambda a: np.ascontiguousarray(np.asarray(a).T)
    A = lambda a: np.ascontiguousarray(np.asarray(a))
    return {
        "wq": _chunk_w(A(inputs["L2_attn_Wq"]), 1024),
        "wk": _chunk_w(A(inputs["L2_attn_Wk"]), 1024),
        "wv": _chunk_w(A(inputs["L2_attn_Wv"]), 1024),
        "wo": _chunk_w(A(inputs["L2_attn_Wo"]), 1024),
        "bq": np.ascontiguousarray(np.asarray(inputs["L2_attn_bq"]).reshape(8, 128).T),
        "bk": np.ascontiguousarray(np.asarray(inputs["L2_attn_bk"]).reshape(8, 128).T),
        "bv": np.ascontiguousarray(np.asarray(inputs["L2_attn_bv"]).reshape(1, 1024)),
        "bo": np.ascontiguousarray(np.asarray(inputs["L2_attn_bo"]).reshape(8, 128).T),
        "rm": A(inputs["L2_attn_random_matrix"]),
        "wr": _chunk_w(T(inputs["L2_ffn_W_r"]), 1024),
        "wk2": _chunk_w(T(inputs["L2_ffn_W_k"]), 1024),
        "wv2": _chunk_w(T(inputs["L2_ffn_W_v"]), 1024),
        "mur": np.ascontiguousarray(np.asarray(inputs["L2_ffn_mu_r"]).reshape(8, 128).T),
        "muk": np.ascontiguousarray(np.asarray(inputs["L2_ffn_mu_k"]).reshape(8, 128).T),
    }


def run_l2(Xfm, inputs):
    pre = _prep_l2(inputs)
    nc = build_l2()
    in_maps = []
    for core in range(8):
        half = core % 2
        m = {"x": _perm_x(Xfm, core), "pm": np.full((1, 1), float(half), np.float32)}
        m.update(pre)
        in_maps.append(m)
    res = run_bass_kernel_spmd(nc, in_maps, core_ids=list(range(8)))
    X3 = np.empty((B, D, S), np.float32)
    for core in range(8):
        b, half = core // 2, core % 2
        X3[b][:, half * NT:(half + 1) * NT] = res.results[core]["xo"]
    return X3



def build_l3():
    TT = 1024
    nc = bass.Bass("TRN2", target_bir_lowering=False, debug=False)
    xa_d = nc.dram_tensor("xa", [4, D, NT], F32, kind="ExternalInput").ap()
    xo_in_d = nc.dram_tensor("xown", [D, NT], F32, kind="ExternalInput").ap()
    dcol_d = nc.dram_tensor("dcol", [128, DC * 16], F32, kind="ExternalInput").ap()
    acol_d = nc.dram_tensor("acol", [128, DC * 16], F32, kind="ExternalInput").ap()
    bcol_d = nc.dram_tensor("bcol", [128, DC * 16], F32, kind="ExternalInput").ap()
    gcol_d = nc.dram_tensor("gcol", [128, DC * 16], F32, kind="ExternalInput").ap()
    om_d = nc.dram_tensor("om", [128, 8], F32, kind="ExternalInput").ap()
    msel_d = nc.dram_tensor("msel", [1, 16], F32, kind="ExternalInput").ap()
    w1_d = nc.dram_tensor("w1", [128, DC * 4096], F32, kind="ExternalInput").ap()
    b1_d = nc.dram_tensor("b1", [128, 32], F32, kind="ExternalInput").ap()
    w2_d = nc.dram_tensor("w2", [128, 32 * 1024], F32, kind="ExternalInput").ap()
    b2_d = nc.dram_tensor("b2", [128, 8], F32, kind="ExternalInput").ap()
    w1t_d = nc.dram_tensor("w1t", [128, DC * 4096], BF16, kind="Internal").ap()
    w2t_d = nc.dram_tensor("w2t", [128, 32 * 1024], BF16, kind="Internal").ap()
    xo_d = nc.dram_tensor("xo", [D, NT], F32, kind="ExternalOutput").ap()

    kb = KB()
    es = ExitStack()
    with es:
        def sb(n, sh, dt, stack=None):
            st = stack or es
            return Buf(st.enter_context(nc.sbuf_tensor(n, sh, dt)), n)
        c = mk_consts(nc, es, kb)
        P = [Buf(es.enter_context(nc.psum_tensor(f"P{i}", [128, 512], F32)), f"P{i}") for i in range(4)]
        sc = mk_ln_scratch(nc, es, "ln")
        xn_own = [sb(f"xno{i}", [128, NT], BF16) for i in range(DC)]
        s3buf = [sb(f"s3_{i}", [128, NT], BF16) for i in range(DC)]
        omc = sb("omc", [128, 8], F32)
        ccols = sb("ccols", [128, DC * 4], F32)
        dma_in(kb, omc, om_d[:, :])

        s0 = ExitStack()
        with s0:
            dk = sb("dk", [128, DC * 16], F32, s0)
            ak = sb("ak", [128, DC * 16], F32, s0)
            bk = sb("bk", [128, DC * 16], F32, s0)
            gk = sb("gk", [128, DC * 16], F32, s0)
            base = sb("base", [128, DC * 16], F32, s0)
            qq = sb("qq", [128, DC * 16], F32, s0)
            kcol = sb("kcol", [128, DC * 4], F32, s0)
            mrow = sb("mrow", [1, 16], F32, s0)
            mB = sb("mB", [128, 16], F32, s0)
            tmp4 = sb("tmp4", [128, 4], F32, s0)
            dma_in(kb, dk, dcol_d[:, :])
            dma_in(kb, ak, acol_d[:, :])
            dma_in(kb, bk, bcol_d[:, :])
            dma_in(kb, gk, gcol_d[:, :])
            kb.do("scalar", lambda s: s.activation(out=dk.t[:], in_=dk.t[:], func=AF.Sigmoid),
                  reads=[dk], writes=[dk])
            kb.do("scalar", lambda s: s.activation(out=ak.t[:], in_=ak.t[:], func=AF.Sigmoid),
                  reads=[ak], writes=[ak])
            kb.do("vector", lambda v: v.tensor_mul(out=qq.t[:], in0=dk.t[:], in1=ak.t[:]),
                  reads=[dk, ak], writes=[qq])
            kb.do("vector", lambda v: v.tensor_scalar(out=qq.t[:], in0=qq.t[:], scalar1=-1.0,
                                                      scalar2=1.0, op0=ALU.mult, op1=ALU.add),
                  reads=[qq], writes=[qq])
            kb.do("vector", lambda v: v.tensor_mul(out=base.t[:], in0=dk.t[:], in1=bk.t[:]),
                  reads=[dk, bk], writes=[base])
            kb.do("vector", lambda v: v.tensor_mul(out=base.t[:], in0=base.t[:], in1=gk.t[:]),
                  reads=[gk, base], writes=[base])
            kb.do("vector", lambda v: v.tensor_scalar(out=base.t[:], in0=base.t[:], scalar1=0.25,
                                                      scalar2=None, op0=ALU.mult),
                  reads=[base], writes=[base])
            for l in range(4):
                for ci in range(DC):
                    kb.do("vector", lambda v, l=l, ci=ci: v.reduce_sum(
                        kcol.t[:, ci * 4 + l:ci * 4 + l + 1], base.t[:, ci * 16:(ci + 1) * 16], axis=AX.X),
                        reads=[base], writes=[kcol])
                if l < 3:
                    kb.do("vector", lambda v: v.tensor_mul(out=base.t[:], in0=base.t[:], in1=qq.t[:]),
                          reads=[qq, base], writes=[base])
            dma_in(kb, mrow, msel_d[:, :])
            kb.do("tensor", lambda t: t.matmul(P[3].t[:, 0:16], c["ones_row"].t[:], mrow.t[:],
                                               start=True, stop=True),
                  reads=[c["ones_row"], mrow], writes=[P[3]])
            kb.do("vector", lambda v: v.tensor_copy(out=mB.t[:], in_=P[3].t[:, 0:16]),
                  reads=[P[3]], writes=[mB])
            for ci in range(DC):
                for tau in range(4):
                    kb.do("vector", lambda v, ci=ci, tau=tau: v.tensor_mul(
                        out=tmp4.t[:], in0=kcol.t[:, ci * 4:(ci + 1) * 4], in1=mB.t[:, tau * 4:(tau + 1) * 4]),
                        reads=[kcol, mB], writes=[tmp4])
                    kb.do("vector", lambda v, ci=ci, tau=tau: v.reduce_sum(
                        ccols.t[:, ci * 4 + tau:ci * 4 + tau + 1], tmp4.t[:], axis=AX.X),
                        reads=[tmp4], writes=[ccols])
            kb.barrier()

        def xo_ap(ci, col):
            return xo_in_d[ci * 128:(ci + 1) * 128, col:col + W]

        emit_ln(kb, c, xo_ap, xn_own, 0, NT, P, sc, out_col0=0)

        s1 = ExitStack()
        with s1:
            xtile = sb("xtile", [128, W], F32, s1)
            xtile2 = sb("xtile2", [128, W], F32, s1)
            for tc in range(NT // W):
                col = tc * W
                for tau in range(4):
                    for ci in range(DC):
                        xa = xtile if ci % 2 == 0 else xtile2
                        dma_in(kb, xa, xa_d[tau, ci * 128:(ci + 1) * 128, col:col + W])
                        kb.do("tensor", lambda t, ci=ci, xa=xa: t.matmul(
                            P[0].t[0:1, 0:W], c["ones_col"].t[:], xa.t[:],
                            start=(ci == 0), stop=(ci == DC - 1)),
                            reads=[c["ones_col"], xa], writes=[P[0]], inc=1)
                        kb.do("scalar", lambda s, xa=xa: s.activation(out=sc["xsq"].t[:], in_=xa.t[:],
                                                                      func=AF.Square),
                              reads=[xa], writes=[sc["xsq"]])
                        kb.do("tensor", lambda t, ci=ci: t.matmul(
                            P[1].t[0:1, 0:W], c["ones_col"].t[:], sc["xsq"].t[:],
                            start=(ci == 0), stop=(ci == DC - 1)),
                            reads=[c["ones_col"], sc["xsq"]], writes=[P[1]], inc=1)
                    kb.do("scalar", lambda s: s.activation(out=sc["row1"].t[:], in_=P[0].t[0:1, 0:W],
                                                           func=AF.Copy, scale=1.0 / D),
                          reads=[P[0]], writes=[sc["row1"]])
                    kb.do("scalar", lambda s: s.activation(out=sc["row2"].t[:], in_=P[1].t[0:1, 0:W],
                                                           func=AF.Copy, scale=1.0 / D),
                          reads=[P[1]], writes=[sc["row2"]])
                    kb.do("vector", lambda v: v.tensor_mul(out=sc["row3"].t[:], in0=sc["row1"].t[:],
                                                           in1=sc["row1"].t[:]),
                          reads=[sc["row1"]], writes=[sc["row3"]])
                    kb.do("vector", lambda v: v.tensor_sub(out=sc["row2"].t[:], in0=sc["row2"].t[:],
                                                           in1=sc["row3"].t[:]),
                          reads=[sc["row2"], sc["row3"]], writes=[sc["row2"]])
                    kb.do("vector", lambda v: v.tensor_scalar(out=sc["row2"].t[:], in0=sc["row2"].t[:],
                                                              scalar1=1e-5, scalar2=None, op0=ALU.add),
                          reads=[sc["row2"]], writes=[sc["row2"]])
                    kb.do("scalar", lambda s: s.activation(out=sc["row2"].t[:], in_=sc["row2"].t[:],
                                                           func=AF.Sqrt),
                          reads=[sc["row2"]], writes=[sc["row2"]])
                    kb.do("vector", lambda v: v.reciprocal(out=sc["row2"].t[:], in_=sc["row2"].t[:]),
                          reads=[sc["row2"]], writes=[sc["row2"]])
                    kb.do("tensor", lambda t: t.matmul(P[0].t[:, 0:W], c["ones_row"].t[:], sc["row1"].t[:],
                                                       start=True, stop=True),
                          reads=[c["ones_row"], sc["row1"]], writes=[P[0]])
                    kb.do("tensor", lambda t: t.matmul(P[1].t[:, 0:W], c["ones_row"].t[:], sc["row2"].t[:],
                                                       start=True, stop=True),
                          reads=[c["ones_row"], sc["row2"]], writes=[P[1]])
                    for ci in range(DC):
                        xa = xtile if ci % 2 == 0 else xtile2
                        dma_in(kb, xa, xa_d[tau, ci * 128:(ci + 1) * 128, col:col + W])
                        kb.do("vector", lambda v, xa=xa: v.tensor_sub(out=xa.t[:], in0=xa.t[:],
                                                                      in1=P[0].t[:, 0:W]),
                              reads=[xa, P[0]], writes=[xa])
                        kb.do("vector", lambda v, xa=xa: v.tensor_mul(out=xa.t[:], in0=xa.t[:],
                                                                      in1=P[1].t[:, 0:W]),
                              reads=[xa, P[1]], writes=[xa])
                        if tau == 0:
                            kb.do("vector", lambda v, ci=ci, tau=tau, xa=xa, col=col: v.tensor_scalar(
                                out=s3buf[ci].t[:, col:col + W], in0=xa.t[:],
                                scalar1=ccols.t[:, ci * 4 + tau:ci * 4 + tau + 1], scalar2=None,
                                op0=ALU.mult), reads=[xa, ccols], writes=[s3buf[ci]])
                        else:
                            kb.do("vector", lambda v, ci=ci, tau=tau, xa=xa, col=col: v.scalar_tensor_tensor(
                                out=s3buf[ci].t[:, col:col + W], in0=xa.t[:],
                                scalar=ccols.t[:, ci * 4 + tau:ci * 4 + tau + 1],
                                in1=s3buf[ci].t[:, col:col + W], op0=ALU.mult, op1=ALU.add),
                                reads=[xa, ccols, s3buf[ci]], writes=[s3buf[ci]])
                for ci in range(DC):
                    kb.do("vector", lambda v, ci=ci, col=col: v.scalar_tensor_tensor(
                        out=sc["x1"].t[:], in0=xn_own[ci].t[:, col:col + W],
                        scalar=omc.t[:, ci:ci + 1], in1=s3buf[ci].t[:, col:col + W],
                        op0=ALU.mult, op1=ALU.add),
                        reads=[xn_own[ci], omc, s3buf[ci]], writes=[sc["x1"]])
                    kb.do("scalar", lambda s, ci=ci, col=col: s.activation(
                        out=s3buf[ci].t[:, col:col + W], in_=sc["x1"].t[:], func=AF.Silu),
                        reads=[sc["x1"]], writes=[s3buf[ci]])
            kb.barrier()

        s2t = ExitStack()
        with s2t:
            wt = sb("wtT", [128, 512], F32, s2t)
            wt2 = sb("wtT2", [128, 512], F32, s2t)
            colsum = sb("colsumT", [128, 1], F32, s2t)
            rcol = sb("rcolT", [128, 1], F32, s2t)
            one11 = sb("one11", [1, 1], F32, s2t)
            wob = sb("wobT", [128, 512], BF16, s2t)
            for wd, wtd, ncols in ((w1_d, w1t_d, DC * 4096), (w2_d, w2t_d, 32 * 1024)):
                ntile = ncols // 512
                for i in range(ntile):
                    wx = wt if i % 2 == 0 else wt2
                    dma_in(kb, wx, wd[:, i * 512:(i + 1) * 512])
                    kb.do("vector", lambda v, wx=wx: v.reduce_sum(
                        sc["x1"].t[:, 0:1], wx.t[:], axis=AX.X, apply_absolute_value=True),
                        reads=[wx], writes=[sc["x1"]])
                    if i == 0:
                        kb.do("vector", lambda v: v.tensor_copy(out=colsum.t[:], in_=sc["x1"].t[:, 0:1]),
                              reads=[sc["x1"]], writes=[colsum])
                    else:
                        kb.do("vector", lambda v: v.tensor_add(out=colsum.t[:], in0=colsum.t[:],
                                                               in1=sc["x1"].t[:, 0:1]),
                              reads=[sc["x1"], colsum], writes=[colsum])
                kb.do("tensor", lambda t: t.matmul(P[3].t[0:1, 0:1], colsum.t[:, 0:1],
                                                   c["ones_col"].t[:, 0:1], start=True, stop=True),
                      reads=[colsum, c["ones_col"]], writes=[P[3]])
                kb.do("vector", lambda v, ncols=ncols: v.tensor_scalar(
                    out=one11.t[:], in0=P[3].t[0:1, 0:1], scalar1=1.0 / (128.0 * ncols),
                    scalar2=1e-8, op0=ALU.mult, op1=ALU.max),
                    reads=[P[3]], writes=[one11])
                kb.do("vector", lambda v: v.reciprocal(out=one11.t[:], in_=one11.t[:]),
                      reads=[one11], writes=[one11])
                kb.do("tensor", lambda t: t.matmul(P[3].t[:, 0:1], c["ones_row"].t[:], one11.t[:],
                                                   start=True, stop=True),
                      reads=[c["ones_row"], one11], writes=[P[3]])
                kb.do("vector", lambda v: v.tensor_copy(out=rcol.t[:], in_=P[3].t[:, 0:1]),
                      reads=[P[3]], writes=[rcol])
                for i in range(ntile):
                    wx = wt if i % 2 == 0 else wt2
                    dma_in(kb, wx, wd[:, i * 512:(i + 1) * 512])
                    kb.do("vector", lambda v, wx=wx: v.tensor_scalar(
                        out=wx.t[:], in0=wx.t[:], scalar1=rcol.t[:, 0:1], scalar2=MAGIC,
                        op0=ALU.mult, op1=ALU.add), reads=[wx, rcol], writes=[wx])
                    kb.do("vector", lambda v, wx=wx: v.tensor_scalar(
                        out=wx.t[:], in0=wx.t[:], scalar1=MAGIC, scalar2=1.0,
                        op0=ALU.subtract, op1=ALU.min), reads=[wx], writes=[wx])
                    kb.do("vector", lambda v, wx=wx: v.tensor_scalar(
                        out=wob.t[:], in0=wx.t[:], scalar1=-1.0, scalar2=None, op0=ALU.max),
                        reads=[wx], writes=[wob])
                    dma_out(kb, wtd[:, i * 512:(i + 1) * 512], wob)
            kb.barrier()

        s3f = ExitStack()
        with s3f:
            h_sb = sb("h_sb", [128, 32 * TT], BF16, s3f)
            wbf = sb("wbfL3", [128, 1024], BF16, s3f)
            wbf2 = sb("wbfL3b", [128, 1024], BF16, s3f)
            b1c = sb("b1c", [128, 32], F32, s3f)
            b2c = sb("b2c", [128, 8], F32, s3f)
            tfa = sb("tfaL3", [128, W], F32, s3f)
            xrr = sb("xrrL3", [128, W], F32, s3f)
            dma_in(kb, b1c, b1_d[:, :])
            dma_in(kb, b2c, b2_d[:, :])
            w1t_r = w1t_d.rearrange("p (k f) -> p k f", f=4096)
            w2t_r = w2t_d.rearrange("p (k f) -> p k f", f=1024)
            NN = TT // W
            for tc in range(NT // TT):
                tc0 = tc * TT
                for fo in range(32):
                    wx = wbf if fo % 2 == 0 else wbf2
                    pbase = 2 if fo % 2 == 0 else 0
                    kb.do("sync", lambda e, fo=fo, wx=wx: e.dma_start(
                        out=wx.t[:].rearrange("p (k f) -> p k f", f=128),
                        in_=w1t_r[:, :, fo * 128:fo * 128 + 128]),
                        writes=[wx], inc=16, sem_key="ld_" + wx.name)
                    for nn in range(NN):
                        for ci in range(DC):
                            kb.do("tensor", lambda t, ci=ci, nn=nn, tc0=tc0, wx=wx, pbase=pbase: t.matmul(
                                P[pbase + nn].t[:, 0:W], wx.t[:, ci * 128:(ci + 1) * 128],
                                xn_own[ci].t[:, tc0 + nn * W:tc0 + nn * W + W],
                                start=(ci == 0), stop=(ci == DC - 1)),
                                reads=[wx, xn_own[ci]], writes=[P[pbase + nn]], inc=1)
                        kb.do("scalar", lambda s, fo=fo, nn=nn, pbase=pbase: s.activation(
                            out=h_sb.t[:, fo * TT + nn * W:fo * TT + nn * W + W],
                            in_=P[pbase + nn].t[:, 0:W], func=AF.Gelu, bias=b1c.t[:, fo:fo + 1]),
                            reads=[P[pbase + nn], b1c], writes=[h_sb])
                for oo in range(DC):
                    pbase = 2 if oo % 2 == 0 else 0
                    for fig in range(DC):
                        wx = wbf if fig % 2 == 0 else wbf2
                        kb.do("sync", lambda e, fig=fig, oo=oo, wx=wx: e.dma_start(
                            out=wx.t[:, 0:512].rearrange("p (k f) -> p k f", f=128),
                            in_=w2t_r[:, fig * 4:fig * 4 + 4, oo * 128:oo * 128 + 128]),
                            writes=[wx], inc=16, sem_key="ld_" + wx.name)
                        for nn in range(NN):
                            for k in range(4):
                                fi = fig * 4 + k
                                kb.do("tensor", lambda t, k=k, nn=nn, fi=fi, wx=wx, pbase=pbase: t.matmul(
                                    P[pbase + nn].t[:, 0:W], wx.t[:, k * 128:(k + 1) * 128],
                                    h_sb.t[:, fi * TT + nn * W:fi * TT + nn * W + W],
                                    start=(fi == 0), stop=(fi == 31)),
                                    reads=[wx, h_sb], writes=[P[pbase + nn]], inc=1)
                    for nn in range(NN):
                        col = tc0 + nn * W
                        kb.do("scalar", lambda s, oo=oo, nn=nn, pbase=pbase: s.activation(
                            out=tfa.t[:], in_=P[pbase + nn].t[:, 0:W], func=AF.Silu,
                            bias=b2c.t[:, oo:oo + 1]),
                            reads=[P[pbase + nn], b2c], writes=[tfa])
                        dma_in(kb, xrr, xo_in_d[oo * 128:(oo + 1) * 128, col:col + W])
                        kb.do("vector", lambda v, oo=oo, col=col: v.tensor_mul(
                            out=tfa.t[:], in0=tfa.t[:], in1=s3buf[oo].t[:, col:col + W]),
                            reads=[tfa, s3buf[oo]], writes=[tfa])
                        kb.do("vector", lambda v: v.tensor_add(out=xrr.t[:], in0=xrr.t[:], in1=tfa.t[:]),
                              reads=[tfa, xrr], writes=[xrr])
                        dma_out(kb, xo_d[oo * 128:(oo + 1) * 128, col:col + W], xrr)

        kb.finalize(nc, es)
    return nc


def _prep_l1(inputs):
    T = lambda a: np.ascontiguousarray(np.asarray(a).T)
    d = {
        "wq": _chunk_w(T(inputs["L1_attn_Wq"]), 1024),
        "wk": _chunk_w(T(inputs["L1_attn_Wk"]), 1024),
        "wv": _chunk_w(T(inputs["L1_attn_Wv"]), 1024),
        "wo": _chunk_w(T(inputs["L1_attn_Wo"]), 1024),
        "bq": np.ascontiguousarray(np.asarray(inputs["L1_attn_bq"]).reshape(8, 128).T),
        "bk": np.ascontiguousarray(np.asarray(inputs["L1_attn_bk"]).reshape(1, 1024)),
        "bv": np.ascontiguousarray(np.asarray(inputs["L1_attn_bv"]).reshape(1, 1024)),
        "bo": np.ascontiguousarray(np.asarray(inputs["L1_attn_bo"]).reshape(8, 128).T),
        "stdp": np.ascontiguousarray(np.asarray(inputs["L1_attn_stdp"]).reshape(1, 16)),
        "rw": _chunk_w(T(inputs["L1_ffn_router_weight"]), 16),
        "eb": np.ascontiguousarray(np.asarray(inputs["L1_ffn_e_bias"]).reshape(1, 16)),
        "gw": np.stack([_chunk_w(T(np.asarray(inputs["L1_ffn_gate_w"])[e]), 512) for e in range(16)]),
        "uw": np.stack([_chunk_w(T(np.asarray(inputs["L1_ffn_up_w"])[e]), 512) for e in range(16)]),
        "dw": np.stack([_chunk_w(T(np.asarray(inputs["L1_ffn_down_w"])[e]), 1024) for e in range(16)]),
        "shg": _chunk_w(T(inputs["L1_ffn_sh_gate_w"]), 1024),
        "shu": _chunk_w(T(inputs["L1_ffn_sh_up_w"]), 1024),
        "shd": _chunk_w(T(inputs["L1_ffn_sh_down_w"]), 1024),
    }
    return d


def _perm_x(Xfm, core):
    b, half = core // 2, core % 2
    xb = Xfm[b]
    if half == 0:
        return np.ascontiguousarray(xb)
    return np.ascontiguousarray(np.concatenate([xb[:, NT:], xb[:, :NT]], axis=1))


def run_l1(Xfm, inputs):
    pre = _prep_l1(inputs)
    nc = build_l1()
    in_maps = []
    for core in range(8):
        m = {"x": _perm_x(Xfm, core)}
        m.update(pre)
        in_maps.append(m)
    res = run_bass_kernel_spmd(nc, in_maps, core_ids=list(range(8)))
    X2 = np.empty((B, D, S), np.float32)
    for core in range(8):
        b, half = core // 2, core % 2
        X2[b][:, half * NT:(half + 1) * NT] = res.results[core]["xo"]
    return X2


def _prep_l3(inputs):
    def colchunk(a):
        a = np.asarray(a).reshape(1024, -1)
        k = a.shape[1]
        return np.ascontiguousarray(a.reshape(8, 128, k).transpose(1, 0, 2).reshape(128, 8 * k))
    return {
        "dcol": colchunk(inputs["L3_attn_delta"]),
        "acol": colchunk(inputs["L3_attn_alpha"]),
        "bcol": colchunk(inputs["L3_attn_beta"]),
        "gcol": colchunk(inputs["L3_attn_gamma"]),
        "om": np.ascontiguousarray(np.asarray(inputs["L3_attn_omega"]).reshape(8, 128).T),
        "w1": _chunk_w(np.ascontiguousarray(np.asarray(inputs["L3_ffn_W1"]).T), 4096),
        "b1": np.ascontiguousarray(np.asarray(inputs["L3_ffn_b1"]).reshape(32, 128).T),
        "w2": _chunk_w(np.ascontiguousarray(np.asarray(inputs["L3_ffn_W2"]).T), 1024),
        "b2": np.ascontiguousarray(np.asarray(inputs["L3_ffn_b2"]).reshape(8, 128).T),
    }


def run_l3(Xfm, inputs):
    pre = _prep_l3(inputs)
    nc = build_l3()
    in_maps = []
    for core in range(8):
        b, half = core // 2, core % 2
        sl = slice(half * NT, (half + 1) * NT)
        msel = np.zeros((1, 16), np.float32)
        for tau in range(4):
            l = b - tau
            if 0 <= l < 4:
                msel[0, tau * 4 + l] = 1.0
        m = {"xa": np.ascontiguousarray(Xfm[:, :, sl]),
             "xown": np.ascontiguousarray(Xfm[b][:, sl]),
             "msel": msel}
        m.update(pre)
        in_maps.append(m)
    res = run_bass_kernel_spmd(nc, in_maps, core_ids=list(range(8)))
    X4 = np.empty((B, D, S), np.float32)
    for core in range(8):
        b, half = core // 2, core % 2
        X4[b][:, half * NT:(half + 1) * NT] = res.results[core]["xo"]
    return X4


def kernel(**inputs):
    X = _fm(inputs["L0_attn_x"])
    X = run_l0(X, inputs)
    X = run_l1(X, inputs)
    X = run_l2(X, inputs)
    X = run_l3(X, inputs)
    return np.ascontiguousarray(np.transpose(X, (0, 2, 1)))


# version 03
# speedup vs baseline: 1.0734x; 1.0030x over previous
import numpy as np
from contextlib import ExitStack

import concourse.bass as bass
import concourse.mybir as mybir
from concourse.bass_utils import run_bass_kernel_spmd

F32 = mybir.dt.float32
BF16 = mybir.dt.bfloat16
AF = mybir.ActivationFunctionType
ALU = mybir.AluOpType
AX = mybir.AxisListType

B, S, D = 4, 4096, 1024
DC = 8
W = 512
NT = 2048
MAGIC = 1.5 * 2.0 ** 23

ENGINES = ("sync", "tensor", "vector", "scalar", "gpsimd")


class Buf:
    __slots__ = ("t", "w", "readers", "name")

    def __init__(self, t, name=None):
        self.t = t
        self.w = None
        self.readers = {}
        self.name = name or getattr(t, "name", None) or str(id(t))


class KB:

    def __init__(self):
        self.ops = {e: [] for e in ENGINES}
        self.cnt = {e: 0 for e in ENGINES}
        self.water = {e: {} for e in ENGINES}
        self.pending = {e: [] for e in ENGINES}
        self.out_channels = set()

    def _wait(self, eng, tok):
        if tok is None:
            return
        pe, v = tok
        if self.water[eng].get(pe, 0) >= v:
            return
        self.ops[eng].append(("w", pe, v))
        self.water[eng][pe] = v

    def do(self, eng, fn, reads=(), writes=(), inc=1, sem_key=None):
        key = sem_key or eng
        deps = []
        for b in reads:
            deps.append(b.w)
        for b in writes:
            deps.append(b.w)
            deps.extend(b.readers.values())
        for t in deps:
            self._wait(eng, t)
        self.ops[eng].append(("o", fn, inc, key))
        if inc:
            self.cnt[key] = self.cnt.get(key, 0) + inc
            tok = (key, self.cnt[key])
            for b in self.pending[eng]:
                b.readers[key] = tok
            self.pending[eng] = []
        else:
            tok = None
        for b in reads:
            if inc:
                b.readers[key] = tok
            elif b not in self.pending[eng]:
                self.pending[eng].append(b)
        for b in writes:
            if inc:
                b.w = tok
                b.readers = {}
            elif b not in self.pending[eng]:
                self.pending[eng].append(b)
        return tok

    def barrier(self):
        snap = dict(self.cnt)
        for eng in ENGINES:
            for pe, v in snap.items():
                if pe != eng and v > 0:
                    self._wait(eng, (pe, v))

    def finalize(self, nc, es):
        sems = {k: es.enter_context(nc.semaphore("sem_" + str(i)))
                for i, k in enumerate(self.cnt)}
        for ch in sorted(self.out_channels):
            if self.cnt.get(ch, 0) > 0:
                self.ops["gpsimd"].append(("w", ch, self.cnt[ch]))
        block = es.enter_context(nc.Block())
        for ename in ENGINES:
            items = self.ops[ename]
            if not items:
                continue

            def make_body(items, ename):
                def body(eng):
                    for it in items:
                        if it[0] == "w":
                            eng.wait_ge(sems[it[1]], it[2])
                        else:
                            ins = it[1](eng)
                            if it[2]:
                                ins.then_inc(sems[it[3]], it[2])
                return body

            getattr(block, ename)(make_body(items, ename))


def dma_in(kb, buf, src_ap, dst_sl=None, q="sync"):
    d = dst_sl if dst_sl is not None else buf.t[:]
    return kb.do(q, lambda e, d=d, s=src_ap: e.dma_start(out=d, in_=s), writes=[buf], inc=16,
                 sem_key="ld_" + buf.name)


def dma_out(kb, dst_ap, buf, src_sl=None, q="gpsimd"):
    s = src_sl if src_sl is not None else buf.t[:]
    ch = "st_" + buf.name
    kb.out_channels.add(ch)
    return kb.do(q, lambda e, d=dst_ap, s=s: e.dma_start(out=d, in_=s), reads=[buf], inc=16,
                 sem_key=ch)


def cast(kb, dst_buf, dst_sl, src_buf, src_sl, eng="vector"):
    kb.do(eng, lambda v, d=dst_sl, s=src_sl: v.tensor_copy(out=d, in_=s),
          reads=[src_buf], writes=[dst_buf])


def mk_consts(nc, es, kb):
    c = {}

    def sb(n, sh, dt):
        return Buf(es.enter_context(nc.sbuf_tensor(n, sh, dt)), n)

    c["ones_row"] = sb("c_ones_row", [1, 128], F32)
    c["ones_col"] = sb("c_ones_col", [128, 1], F32)
    c["ones_bcol"] = sb("c_ones_bcol", [128, 1], BF16)
    c["ident"] = sb("c_ident", [128, 128], F32)
    kb.do("vector", lambda v: v.memset(c["ones_row"].t[:], 1.0), writes=[c["ones_row"]])
    kb.do("vector", lambda v: v.memset(c["ones_col"].t[:], 1.0), writes=[c["ones_col"]])
    kb.do("vector", lambda v: v.memset(c["ones_bcol"].t[:], 1.0), writes=[c["ones_bcol"]])
    kb.do("gpsimd", lambda g: g.iota(c["ident"].t[:], pattern=[[1, 128]], base=0,
                                     channel_multiplier=-1, allow_small_or_imprecise_dtypes=True),
          writes=[c["ident"]])
    kb.do("vector", lambda v: v.tensor_scalar(out=c["ident"].t[:], in0=c["ident"].t[:],
                                              scalar1=0.0, scalar2=None, op0=ALU.is_equal),
          reads=[c["ident"]], writes=[c["ident"]])
    return c


def emit_ln(kb, c, x_ap_fn, xn_bufs, xcol0, ncols, P, sc, out_col0=0):
    ntc = ncols // W
    xq = sc["xsq"]
    for tci in range(ntc):
        colg = xcol0 + tci * W
        colo = out_col0 + tci * W
        for ci in range(DC):
            xa = sc["x1"] if ci % 2 == 0 else sc["x2"]
            dma_in(kb, xa, x_ap_fn(ci, colg))
            kb.do("tensor", lambda t, ci=ci, xa=xa: t.matmul(P[0].t[0:1, 0:W], c["ones_col"].t[:],
                                                      xa.t[:], start=(ci == 0), stop=(ci == DC - 1)),
                  reads=[c["ones_col"], xa], writes=[P[0]])
            kb.do("scalar", lambda s, xa=xa: s.activation(out=xq.t[:], in_=xa.t[:], func=AF.Square),
                  reads=[xa], writes=[xq])
            kb.do("tensor", lambda t, ci=ci: t.matmul(P[1].t[0:1, 0:W], c["ones_col"].t[:],
                                                      xq.t[:], start=(ci == 0), stop=(ci == DC - 1)),
                  reads=[c["ones_col"], xq], writes=[P[1]])
        kb.do("scalar", lambda s: s.activation(out=sc["row1"].t[:], in_=P[0].t[0:1, 0:W],
                                               func=AF.Copy, scale=1.0 / D),
              reads=[P[0]], writes=[sc["row1"]])
        kb.do("scalar", lambda s: s.activation(out=sc["row2"].t[:], in_=P[1].t[0:1, 0:W],
                                               func=AF.Copy, scale=1.0 / D),
              reads=[P[1]], writes=[sc["row2"]])
        kb.do("vector", lambda v: v.tensor_mul(out=sc["row3"].t[:], in0=sc["row1"].t[:], in1=sc["row1"].t[:]),
              reads=[sc["row1"]], writes=[sc["row3"]])
        kb.do("vector", lambda v: v.tensor_sub(out=sc["row2"].t[:], in0=sc["row2"].t[:], in1=sc["row3"].t[:]),
              reads=[sc["row2"], sc["row3"]], writes=[sc["row2"]])
        kb.do("vector", lambda v: v.tensor_scalar(out=sc["row2"].t[:], in0=sc["row2"].t[:],
                                                  scalar1=1e-5, scalar2=None, op0=ALU.add),
              reads=[sc["row2"]], writes=[sc["row2"]])
        kb.do("scalar", lambda s: s.activation(out=sc["row2"].t[:], in_=sc["row2"].t[:],
                                               func=AF.Sqrt),
              reads=[sc["row2"]], writes=[sc["row2"]])
        kb.do("vector", lambda v: v.reciprocal(out=sc["row2"].t[:], in_=sc["row2"].t[:]),
              reads=[sc["row2"]], writes=[sc["row2"]])
        kb.do("tensor", lambda t: t.matmul(P[0].t[:, 0:W], c["ones_row"].t[:], sc["row1"].t[:],
                                           start=True, stop=True),
              reads=[c["ones_row"], sc["row1"]], writes=[P[0]])
        kb.do("tensor", lambda t: t.matmul(P[1].t[:, 0:W], c["ones_row"].t[:], sc["row2"].t[:],
                                           start=True, stop=True),
              reads=[c["ones_row"], sc["row2"]], writes=[P[1]])
        for ci in range(DC):
            xa = sc["x1"] if ci % 2 == 0 else sc["x2"]
            dma_in(kb, xa, x_ap_fn(ci, colg))
            kb.do("vector", lambda v, xa=xa: v.tensor_sub(out=xa.t[:], in0=xa.t[:], in1=P[0].t[:, 0:W]),
                  reads=[xa, P[0]], writes=[xa])
            kb.do("vector", lambda v, ci=ci, colo=colo, xa=xa: v.tensor_mul(
                out=xn_bufs[ci].t[:, colo:colo + W], in0=xa.t[:], in1=P[1].t[:, 0:W]),
                reads=[xa, P[1]], writes=[xn_bufs[ci]])


def mk_ln_scratch(nc, es, pref):
    def sb(n, sh, dt):
        return Buf(es.enter_context(nc.sbuf_tensor(pref + n, sh, dt)), pref + n)
    return {"x1": sb("_x1", [128, W], F32), "x2": sb("_x2", [128, W], F32),
            "xsq": sb("_xsq", [128, W], F32),
            "row1": sb("_row1", [1, W], F32), "row2": sb("_row2", [1, W], F32),
            "row3": sb("_row3", [1, W], F32)}



def build_l0a():
    nc = bass.Bass("TRN2", target_bir_lowering=False, debug=False)
    x_d = nc.dram_tensor("x", [D, S], F32, kind="ExternalInput").ap()
    qw_d = nc.dram_tensor("qw", [DC, 128, 512], F32, kind="ExternalInput").ap()
    kw_d = nc.dram_tensor("kw", [DC, 128, 512], F32, kind="ExternalInput").ap()
    vw_d = nc.dram_tensor("vw", [DC, 128, 512], F32, kind="ExternalInput").ap()
    ww_d = nc.dram_tensor("ww", [8, 64, 1024], F32, kind="ExternalInput").ap()
    z_d = nc.dram_tensor("z", [D, S], F32, kind="ExternalOutput").ap()

    kb = KB()
    es = ExitStack()
    with es:
        def sb(n, sh, dt):
            return Buf(es.enter_context(nc.sbuf_tensor(n, sh, dt)), n)
        c = mk_consts(nc, es, kb)
        P = [Buf(es.enter_context(nc.psum_tensor(f"P{i}", [128, 512], F32)), f"P{i}") for i in range(8)]

        mask = sb("mask", [128, 1408], F32)
        kb.do("gpsimd", lambda g: g.iota(mask.t[:], pattern=[[1, 1408]], base=-512,
                                         channel_multiplier=-1, allow_small_or_imprecise_dtypes=True),
              writes=[mask])
        mask_bf = sb("mask_bf", [128, 1408], BF16)
        kb.do("vector", lambda v: v.tensor_scalar(out=mask_bf.t[:], in0=mask.t[:], scalar1=0.0,
                                                  scalar2=None, op0=ALU.is_ge),
              reads=[mask], writes=[mask_bf])

        sc = mk_ln_scratch(nc, es, "ln")

        qwb = sb("qwb", [128, DC * 512], BF16)
        kwb = sb("kwb", [128, DC * 512], BF16)
        vwb = sb("vwb", [128, DC * 512], BF16)
        wwb = sb("wwb", [64, 8 * 1024], BF16)
        wtmp = sb("wtmp", [128, 512], F32)
        wtmp2 = sb("wtmp2", [64, 1024], F32)
        for ci in range(DC):
            for wd, wb in ((qw_d, qwb), (kw_d, kwb), (vw_d, vwb)):
                dma_in(kb, wtmp, wd[ci, :, :])
                cast(kb, wb, wb.t[:, ci * 512:(ci + 1) * 512], wtmp, wtmp.t[:])
        for h in range(8):
            dma_in(kb, wtmp2, ww_d[h, :, :])
            cast(kb, wwb, wwb.t[0:64, h * 1024:(h + 1) * 1024], wtmp2, wtmp2.t[:])

        qf = [sb(f"qf{i}", [128, S], BF16) for i in range(4)]
        kf = [sb(f"kf{i}", [128, S], BF16) for i in range(4)]
        vt = sb("vt", [128, 32 * 512], BF16)
        xn = [sb(f"xn{i}", [128, W], BF16) for i in range(DC)]

        def x_ap(ci, col):
            return x_d[ci * 128:(ci + 1) * 128, col:col + W]

        for tci in range(S // W):
            col = tci * W
            emit_ln(kb, c, x_ap, xn, col, W, P, sc, out_col0=0)
            for hp in range(4):
                for wb, pp, dst in ((qwb, P[4], qf[hp]), (kwb, P[5], kf[hp])):
                    for ci in range(DC):
                        kb.do("tensor", lambda t, ci=ci, hp=hp, wb=wb, pp=pp: t.matmul(
                            pp.t[:, 0:W], wb.t[:, ci * 512 + hp * 128:ci * 512 + hp * 128 + 128],
                            xn[ci].t[:], start=(ci == 0), stop=(ci == DC - 1)),
                            reads=[wb, xn[ci]], writes=[pp], inc=1)
                    kb.do("scalar", lambda s, pp=pp, dst=dst, col=col: s.copy(
                        out=dst.t[:, col:col + W], in_=pp.t[:, 0:W]),
                        reads=[pp], writes=[dst])
            for j in range(4):
                kt = tci * 4 + j
                for ci in range(DC):
                    kb.do("tensor", lambda t, ci=ci, j=j: t.matmul(
                        P[6].t[:, 0:W], xn[ci].t[:, j * 128:(j + 1) * 128],
                        vwb.t[:, ci * 512:(ci + 1) * 512], start=(ci == 0), stop=(ci == DC - 1)),
                        reads=[xn[ci], vwb], writes=[P[6]], inc=1)
                kb.do("scalar", lambda s, kt=kt: s.copy(out=vt.t[:, kt * 512:(kt + 1) * 512],
                                                        in_=P[6].t[:, 0:W]),
                      reads=[P[6]], writes=[vt])

        e_sb = sb("e_sb", [128, 512], BF16)
        mcol = sb("mcol", [128, 1], F32)
        rcol = sb("rcol", [128, 1], F32)
        bias = sb("bias", [128, 1], F32)
        srow = sb("srow", [1, 512], F32)
        ysb = sb("ysb", [64, 512], F32)
        yn = sb("yn", [64, 512], BF16)
        zc = [sb(f"zc{i}", [128, 512], F32) for i in range(DC)]

        e_sb2 = sb("e_sb2", [128, 512], BF16)
        for qc in range(8):
            nkt = 4 * (qc + 1)
            for h in range(8):
                hp, ho = h // 2, 64 * (h % 2)
                for kt in range(nkt):
                    ps = P[0] if kt % 2 == 0 else P[6]
                    eb = e_sb if kt % 2 == 0 else e_sb2
                    kb.do("tensor", lambda t, kt=kt, hp=hp, ho=ho, qc=qc, ps=ps: t.matmul(
                        ps.t[:, 0:W], kf[hp].t[ho:ho + 64, kt * 128:(kt + 1) * 128],
                        qf[hp].t[ho:ho + 64, qc * W:(qc + 1) * W], start=True, stop=True),
                        reads=[kf[hp], qf[hp]], writes=[ps])
                    kb.do("scalar", lambda s, kt=kt, ps=ps, eb=eb: s.activation(
                        out=eb.t[:], in_=ps.t[:, 0:W],
                        func=AF.Exp, scale=1.0 / 1024.0),
                        reads=[ps], writes=[eb])
                    off = kt * 128 - qc * 512
                    if 0 <= off < 512:
                        kb.do("vector", lambda v, off=off, eb=eb: v.tensor_mul(
                            out=eb.t[:], in0=eb.t[:], in1=mask_bf.t[:, 512 - off:1024 - off]),
                            reads=[eb, mask_bf], writes=[eb])
                    kb.do("tensor", lambda t, kt=kt, h=h, eb=eb: t.matmul(
                        P[2].t[0:64, 0:W], vt.t[:, kt * 512 + h * 64:kt * 512 + h * 64 + 64],
                        eb.t[:], start=(kt == 0), stop=(kt == nkt - 1)),
                        reads=[vt, eb], writes=[P[2]], inc=1)
                    kb.do("tensor", lambda t, kt=kt, eb=eb: t.matmul(
                        P[3].t[0:1, 0:W], c["ones_bcol"].t[:], eb.t[:],
                        start=(kt == 0), stop=(kt == nkt - 1)),
                        reads=[c["ones_bcol"], eb], writes=[P[3]], inc=1)
                kb.do("scalar", lambda s: s.copy(out=srow.t[:], in_=P[3].t[0:1, 0:W]),
                      reads=[P[3]], writes=[srow])
                kb.do("vector", lambda v: v.reciprocal(out=srow.t[:], in_=srow.t[:]),
                      reads=[srow], writes=[srow])
                kb.do("tensor", lambda t: t.matmul(P[4].t[0:64, 0:W], c["ones_row"].t[0:1, 0:64],
                                                   srow.t[:], start=True, stop=True),
                      reads=[c["ones_row"], srow], writes=[P[4]])
                kb.do("scalar", lambda s: s.copy(out=ysb.t[:], in_=P[2].t[0:64, 0:W]),
                      reads=[P[2]], writes=[ysb])
                kb.do("vector", lambda v: v.tensor_mul(out=yn.t[:], in0=ysb.t[:], in1=P[4].t[0:64, 0:W]),
                      reads=[ysb, P[4]], writes=[yn])
                for ci in range(DC):
                    kb.do("tensor", lambda t, ci=ci, h=h: t.matmul(
                        P[5].t[:, 0:W], wwb.t[0:64, h * 1024 + ci * 128:h * 1024 + ci * 128 + 128],
                        yn.t[:], start=True, stop=True),
                        reads=[wwb, yn], writes=[P[5]])
                    if h == 0:
                        kb.do("vector", lambda v, ci=ci: v.tensor_copy(out=zc[ci].t[:], in_=P[5].t[:, 0:W]),
                              reads=[P[5]], writes=[zc[ci]])
                    else:
                        kb.do("vector", lambda v, ci=ci: v.tensor_add(out=zc[ci].t[:], in0=zc[ci].t[:],
                                                                     in1=P[5].t[:, 0:W]),
                              reads=[P[5], zc[ci]], writes=[zc[ci]])
            for ci in range(DC):
                dma_out(kb, z_d[ci * 128:(ci + 1) * 128, qc * W:(qc + 1) * W], zc[ci])

        kb.finalize(nc, es)
    return nc



DEBUG_L0B = False


def build_l0b():
    TT = 1024
    nc = bass.Bass("TRN2", target_bir_lowering=False, debug=False)
    if DEBUG_L0B:
        dbg_a = nc.dram_tensor("dbg_a", [128, 32 * TT], F32, kind="ExternalOutput").ap()
        dbg_b = nc.dram_tensor("dbg_b", [128, 32 * TT], F32, kind="ExternalOutput").ap()
        dbg_c = nc.dram_tensor("dbg_c", [128, 32 * TT], F32, kind="ExternalOutput").ap()
        dbg_xn = nc.dram_tensor("dbg_xn", [128, DC * 512], F32, kind="ExternalOutput").ap()
    x_d = nc.dram_tensor("x", [D, NT], F32, kind="ExternalInput").ap()
    za_d = nc.dram_tensor("za", [D, NT], F32, kind="ExternalInput").ap()
    zb_d = nc.dram_tensor("zb", [D, NT], F32, kind="ExternalInput").ap()
    w1_d = nc.dram_tensor("w1", [32, 128, 1024], F32, kind="ExternalInput").ap()
    w2_d = nc.dram_tensor("w2", [32, 128, 1024], F32, kind="ExternalInput").ap()
    h1_d = nc.dram_tensor("h1", [32, 8, 128, 512], F32, kind="ExternalInput").ap()
    h2_d = nc.dram_tensor("h2", [32, 8, 128, 512], F32, kind="ExternalInput").ap()
    w3_d = nc.dram_tensor("w3", [8, 8, 128, 512], F32, kind="ExternalInput").ap()
    s1_d = nc.dram_tensor("s1", [128, 32], F32, kind="ExternalInput").ap()
    s2_d = nc.dram_tensor("s2", [128, 32], F32, kind="ExternalInput").ap()
    sh1_d = nc.dram_tensor("sh1", [128, 32], F32, kind="ExternalInput").ap()
    sh2_d = nc.dram_tensor("sh2", [128, 32], F32, kind="ExternalInput").ap()
    s3_d = nc.dram_tensor("s3", [128, 8], F32, kind="ExternalInput").ap()
    xo_d = nc.dram_tensor("xo", [D, NT], F32, kind="ExternalOutput").ap()

    kb = KB()
    es = ExitStack()
    with es:
        def sb(n, sh, dt):
            return Buf(es.enter_context(nc.sbuf_tensor(n, sh, dt)), n)
        c = mk_consts(nc, es, kb)
        P = [Buf(es.enter_context(nc.psum_tensor(f"P{i}", [128, 512], F32)), f"P{i}") for i in range(8)]
        sc = mk_ln_scratch(nc, es, "ln")

        xn = [sb(f"xn{i}", [128, NT], BF16) for i in range(DC)]
        szt = sb("szt", [128, W], F32)
        szt2 = sb("szt2", [128, W], F32)

        def x_ap(ci, col):
            return x_d[ci * 128:(ci + 1) * 128, col:col + W]

        emit_ln(kb, c, x_ap, xn, 0, NT, P, sc, out_col0=0)

        sca = sb("sca", [128, 32 * 5], F32)
        for i, sd in enumerate((s1_d, s2_d, sh1_d, sh2_d)):
            dma_in(kb, sca, sd[:, :], dst_sl=sca.t[:, i * 32:(i + 1) * 32])
        dma_in(kb, sca, s3_d[:, :], dst_sl=sca.t[:, 128:136])

        a_sb = sb("a_sb", [128, 32 * TT], BF16)
        b_sb = sb("b_sb", [128, 32 * TT], BF16)
        wstp = [sb(f"wst{i}", [128, 1024], F32) for i in range(2)]
        wbfp = [sb(f"wbf{i}", [128, 1024], BF16) for i in range(2)]
        t1 = sb("t1", [128, TT], F32)
        t2 = sb("t2", [128, TT], F32)
        xr = sb("xr", [128, W], F32)
        sg = sb("sg", [128, W], BF16)

        NN = TT // W
        h1_r = h1_d
        h2_r = h2_d
        w3_r = w3_d
        w1_r = w1_d
        w2_r = w2_d

        for tc in range(NT // TT):
            tc0 = tc * TT
            for fo in range(32):
                pps = (P[0], P[1]) if fo % 2 == 0 else (P[6], P[7])
                for wi, (wr, pp) in enumerate(((w1_r, pps[0]), (w2_r, pps[1]))):
                    wst, wbf = wstp[wi], wbfp[wi]
                    kb.do("sync", lambda e, wr=wr, fo=fo, wst=wst: e.dma_start(
                        out=wst.t[:], in_=wr[fo, :, :]),
                        writes=[wst], inc=16, sem_key="ld_" + wst.name)
                    cast(kb, wbf, wbf.t[:, 0:1024], wst, wst.t[:, 0:1024])
                    for nn in range(NN):
                        for ci in range(DC):
                            kb.do("tensor", lambda t, ci=ci, nn=nn, pp=pp, tc0=tc0, wbf=wbf: t.matmul(
                                pp.t[:, 0:W],
                                wbf.t[:, ci * 128:(ci + 1) * 128],
                                xn[ci].t[:, tc0 + nn * W:tc0 + nn * W + W],
                                start=(ci == 0), stop=(ci == DC - 1)),
                                reads=[wbf, xn[ci]], writes=[pp], inc=1)
                        if pp is P[0]:
                            kb.do("scalar", lambda s, fo=fo, nn=nn: s.activation(
                                out=t1.t[:, nn * W:nn * W + W], in_=P[0].t[:, 0:W], func=AF.Silu,
                                scale=sca.t[:, fo:fo + 1]),
                                reads=[P[0], sca], writes=[t1])
                        else:
                            kb.do("scalar", lambda s, fo=fo, nn=nn: s.activation(
                                out=t2.t[:, nn * W:nn * W + W], in_=P[1].t[:, 0:W], func=AF.Copy,
                                scale=sca.t[:, 32 + fo:32 + fo + 1]),
                                reads=[P[1], sca], writes=[t2])
                kb.do("vector", lambda v, fo=fo: v.tensor_mul(
                    out=a_sb.t[:, fo * TT:(fo + 1) * TT], in0=t1.t[:], in1=t2.t[:]),
                    reads=[t1, t2], writes=[a_sb])
            if DEBUG_L0B and tc == 1:
                for ci in range(DC):
                    cast(kb, sc["x1"], sc["x1"].t[:], xn[ci], xn[ci].t[:, 0:512])
                    dma_out(kb, dbg_xn[:, ci * 512:(ci + 1) * 512], sc["x1"])
                for fo in range(32):
                    cast(kb, t1, t1.t[:], a_sb, a_sb.t[:, fo * TT:(fo + 1) * TT])
                    dma_out(kb, dbg_a[:, fo * TT:(fo + 1) * TT], t1)
            for asrc, dst, hr, scoff in ((a_sb, b_sb, h1_r, 64), (b_sb, a_sb, h2_r, 96)):
                for fo in range(32):
                    pbase = 2 if fo % 2 == 0 else 4
                    for fig in range(DC):
                        wst, wbf = wstp[fig % 2], wbfp[fig % 2]
                        kb.do("sync", lambda e, hr=hr, fig=fig, fo=fo, wst=wst: e.dma_start(
                            out=wst.t[:, 0:512], in_=hr[fo, fig, :, :]),
                            writes=[wst], inc=16, sem_key="ld_" + wst.name)
                        cast(kb, wbf, wbf.t[:, 0:512], wst, wst.t[:, 0:512])
                        for nn in range(NN):
                            for k in range(4):
                                fi = fig * 4 + k
                                kb.do("tensor", lambda t, k=k, nn=nn, fi=fi, asrc=asrc, wbf=wbf, pbase=pbase: t.matmul(
                                    P[pbase + nn].t[:, 0:W], wbf.t[:, k * 128:(k + 1) * 128],
                                    asrc.t[:, fi * TT + nn * W:fi * TT + nn * W + W],
                                    start=(fi == 0), stop=(fi == 31)),
                                    reads=[wbf, asrc], writes=[P[pbase + nn]],
                                    inc=1)
                    for nn in range(NN):
                        kb.do("scalar", lambda s, fo=fo, nn=nn, scoff=scoff, dst=dst, pbase=pbase: s.activation(
                            out=dst.t[:, fo * TT + nn * W:fo * TT + nn * W + W],
                            in_=P[pbase + nn].t[:, 0:W], func=AF.Gelu, scale=sca.t[:, scoff + fo:scoff + fo + 1]),
                            reads=[P[pbase + nn], sca], writes=[dst])
            if DEBUG_L0B and tc == 1:
                for fo in range(32):
                    cast(kb, t1, t1.t[:], b_sb, b_sb.t[:, fo * TT:(fo + 1) * TT])
                    dma_out(kb, dbg_b[:, fo * TT:(fo + 1) * TT], t1)
                for fo in range(32):
                    cast(kb, t1, t1.t[:], a_sb, a_sb.t[:, fo * TT:(fo + 1) * TT])
                    dma_out(kb, dbg_c[:, fo * TT:(fo + 1) * TT], t1)
            for oo in range(DC):
                pbase = 2 if oo % 2 == 0 else 4
                for fig in range(DC):
                    wst, wbf = wstp[fig % 2], wbfp[fig % 2]
                    kb.do("sync", lambda e, fig=fig, oo=oo, wst=wst: e.dma_start(
                        out=wst.t[:, 0:512], in_=w3_r[oo, fig, :, :]),
                        writes=[wst], inc=16, sem_key="ld_" + wst.name)
                    cast(kb, wbf, wbf.t[:, 0:512], wst, wst.t[:, 0:512])
                    for nn in range(NN):
                        for k in range(4):
                            fi = fig * 4 + k
                            kb.do("tensor", lambda t, k=k, nn=nn, fi=fi, wbf=wbf, pbase=pbase: t.matmul(
                                P[pbase + nn].t[:, 0:W], wbf.t[:, k * 128:(k + 1) * 128],
                                a_sb.t[:, fi * TT + nn * W:fi * TT + nn * W + W],
                                start=(fi == 0), stop=(fi == 31)),
                                reads=[wbf, a_sb], writes=[P[pbase + nn]],
                                inc=1)
                for nn in range(NN):
                    col = tc0 + nn * W
                    kb.do("scalar", lambda s, oo=oo, nn=nn, pbase=pbase: s.activation(
                        out=sg.t[:], in_=P[pbase + nn].t[:, 0:W], func=AF.Silu,
                        scale=sca.t[:, 128 + oo:128 + oo + 1]),
                        reads=[P[pbase + nn], sca], writes=[sg])
                    dma_in(kb, szt, za_d[oo * 128:(oo + 1) * 128, col:col + W])
                    dma_in(kb, szt2, zb_d[oo * 128:(oo + 1) * 128, col:col + W])
                    dma_in(kb, xr, x_d[oo * 128:(oo + 1) * 128, col:col + W])
                    kb.do("vector", lambda v: v.tensor_add(out=szt.t[:], in0=szt.t[:], in1=szt2.t[:]),
                          reads=[szt, szt2], writes=[szt])
                    kb.do("vector", lambda v: v.scalar_tensor_tensor(
                        out=szt.t[:], in0=szt.t[:], scalar=1.0 / 16.0, in1=sg.t[:],
                        op0=ALU.mult, op1=ALU.mult),
                        reads=[szt, sg], writes=[szt])
                    kb.do("vector", lambda v: v.tensor_add(out=xr.t[:], in0=xr.t[:], in1=szt.t[:]),
                          reads=[szt, xr], writes=[xr])
                    dma_out(kb, xo_d[oo * 128:(oo + 1) * 128, col:col + W], xr)

        kb.finalize(nc, es)
    return nc




def build_l1():
    nc = bass.Bass("TRN2", target_bir_lowering=False, debug=False)
    x_d = nc.dram_tensor("x", [D, S], F32, kind="ExternalInput").ap()
    wq_d = nc.dram_tensor("wq", [128, DC * 1024], F32, kind="ExternalInput").ap()
    wk_d = nc.dram_tensor("wk", [128, DC * 1024], F32, kind="ExternalInput").ap()
    wv_d = nc.dram_tensor("wv", [128, DC * 1024], F32, kind="ExternalInput").ap()
    wo_d = nc.dram_tensor("wo", [128, DC * 1024], F32, kind="ExternalInput").ap()
    bq_d = nc.dram_tensor("bq", [128, 8], F32, kind="ExternalInput").ap()
    bk_d = nc.dram_tensor("bk", [1, 1024], F32, kind="ExternalInput").ap()
    bv_d = nc.dram_tensor("bv", [1, 1024], F32, kind="ExternalInput").ap()
    bo_d = nc.dram_tensor("bo", [128, 8], F32, kind="ExternalInput").ap()
    stdp_d = nc.dram_tensor("stdp", [1, 16], F32, kind="ExternalInput").ap()
    rw_d = nc.dram_tensor("rw", [128, DC * 16], F32, kind="ExternalInput").ap()
    eb_d = nc.dram_tensor("eb", [1, 16], F32, kind="ExternalInput").ap()
    gw_d = nc.dram_tensor("gw", [16, 128, DC * 512], F32, kind="ExternalInput").ap()
    uw_d = nc.dram_tensor("uw", [16, 128, DC * 512], F32, kind="ExternalInput").ap()
    dw_d = nc.dram_tensor("dw", [16, 128, 4 * 1024], F32, kind="ExternalInput").ap()
    shg_d = nc.dram_tensor("shg", [128, DC * 1024], F32, kind="ExternalInput").ap()
    shu_d = nc.dram_tensor("shu", [128, DC * 1024], F32, kind="ExternalInput").ap()
    shd_d = nc.dram_tensor("shd", [128, DC * 1024], F32, kind="ExternalInput").ap()
    s1sp_d = nc.dram_tensor("s1sp", [D, NT], F32, kind="Internal").ap()
    kvb_d = nc.dram_tensor("kvb", [64, 1024], BF16, kind="Internal").ap()
    xo_d = nc.dram_tensor("xo", [D, NT], F32, kind="ExternalOutput").ap()

    kb = KB()
    es = ExitStack()
    with es:
        def sb(n, sh, dt, stack=None):
            st = stack or es
            return Buf(st.enter_context(nc.sbuf_tensor(n, sh, dt)), n)
        c = mk_consts(nc, es, kb)
        P = [Buf(es.enter_context(nc.psum_tensor(f"P{i}", [128, 512], F32)), f"P{i}") for i in range(5)]
        sc = mk_ln_scratch(nc, es, "ln")
        xn = [sb(f"xn{i}", [128, NT], BF16) for i in range(DC)]
        kv_sb = sb("kv_sb", [128, 1024], BF16)
        ks_sb = sb("ks_sb", [128, 8], BF16)
        comb = sb("comb", [128, 16 * 16], F32)
        ctr = sb("ctr", [16, NT], F32)
        sigB = sb("sigB", [64, 16], F32)
        rowt = sb("rowt", [1, 512], F32)

        def x_ap(ci, col):
            return x_d[ci * 128:(ci + 1) * 128, col:col + W]

        s1 = ExitStack()
        with s1:
            wkb = sb("wkb", [128, DC * 1024], BF16, s1)
            wvb = sb("wvb", [128, DC * 1024], BF16, s1)
            wstg = sb("wstg", [128, 1024], F32, s1)
            bkB = sb("bkB", [128, 1024], F32, s1)
            bvB = sb("bvB", [128, 1024], F32, s1)
            brow = sb("brow", [1, 1024], F32, s1)
            ktileP = [sb(f"ktile{i}", [128, 1024], BF16, s1) for i in range(2)]
            vtileP = [sb(f"vtile{i}", [128, 1024], BF16, s1) for i in range(2)]
            tfaP = [sb(f"tfa{i}", [128, 1024], F32, s1) for i in range(2)]
            tfbP = [sb(f"tfb{i}", [128, 1024], F32, s1) for i in range(2)]
            xnt = [sb(f"xnt{i}", [128, W], BF16, s1) for i in range(DC)]
            Pkv = Buf(es.enter_context(nc.psum_tensor("Pkv", [64, 1024], F32)), "Pkv")
            Pks = Buf(es.enter_context(nc.psum_tensor("Pks", [128, 8], F32)), "Pks")
            for ci in range(DC):
                dma_in(kb, wstg, wk_d[:, ci * 1024:(ci + 1) * 1024])
                cast(kb, wkb, wkb.t[:, ci * 1024:(ci + 1) * 1024], wstg, wstg.t[:])
                dma_in(kb, wstg, wv_d[:, ci * 1024:(ci + 1) * 1024])
                cast(kb, wvb, wvb.t[:, ci * 1024:(ci + 1) * 1024], wstg, wstg.t[:])
            dma_in(kb, brow, bk_d[:, :])
            kb.do("tensor", lambda t: t.matmul(P[4].t[:, 0:512], c["ones_row"].t[:], brow.t[0:1, 0:512],
                                               start=True, stop=True), reads=[c["ones_row"], brow], writes=[P[4]])
            kb.do("vector", lambda v: v.tensor_copy(out=bkB.t[:, 0:512], in_=P[4].t[:, 0:512]),
                  reads=[P[4]], writes=[bkB])
            kb.do("tensor", lambda t: t.matmul(P[4].t[:, 0:512], c["ones_row"].t[:], brow.t[0:1, 512:1024],
                                               start=True, stop=True), reads=[c["ones_row"], brow], writes=[P[4]])
            kb.do("vector", lambda v: v.tensor_copy(out=bkB.t[:, 512:1024], in_=P[4].t[:, 0:512]),
                  reads=[P[4]], writes=[bkB])
            dma_in(kb, brow, bv_d[:, :])
            kb.do("tensor", lambda t: t.matmul(P[4].t[:, 0:512], c["ones_row"].t[:], brow.t[0:1, 0:512],
                                               start=True, stop=True), reads=[c["ones_row"], brow], writes=[P[4]])
            kb.do("vector", lambda v: v.tensor_copy(out=bvB.t[:, 0:512], in_=P[4].t[:, 0:512]),
                  reads=[P[4]], writes=[bvB])
            kb.do("tensor", lambda t: t.matmul(P[4].t[:, 0:512], c["ones_row"].t[:], brow.t[0:1, 512:1024],
                                               start=True, stop=True), reads=[c["ones_row"], brow], writes=[P[4]])
            kb.do("vector", lambda v: v.tensor_copy(out=bvB.t[:, 512:1024], in_=P[4].t[:, 0:512]),
                  reads=[P[4]], writes=[bvB])

            nkt_total = S // 128
            for tci in range(S // W):
                col = tci * W
                emit_ln(kb, c, x_ap, xnt, col, W, P, sc, out_col0=0)
                if tci < NT // W:
                    for ci in range(DC):
                        kb.do("vector", lambda v, ci=ci, col=col: v.tensor_copy(
                            out=xn[ci].t[:, col:col + W], in_=xnt[ci].t[:]),
                            reads=[xnt[ci]], writes=[xn[ci]])
                for j in range(4):
                    kt = tci * 4 + j
                    par = kt % 2
                    ktile, vtile = ktileP[par], vtileP[par]
                    tfa, tfb = tfaP[par], tfbP[par]
                    psK = P[4] if par == 0 else P[3]
                    psV = P[2] if par == 0 else P[1]
                    for wb, bB, dst, tf, do_elu, ps in ((wkb, bkB, ktile, tfa, True, psK),
                                                        (wvb, bvB, vtile, tfb, False, psV)):
                        for o2 in range(2):
                            for ci in range(DC):
                                kb.do("tensor", lambda t, ci=ci, o2=o2, j=j, wb=wb, ps=ps: t.matmul(
                                    ps.t[:, 0:W], xnt[ci].t[:, j * 128:(j + 1) * 128],
                                    wb.t[:, ci * 1024 + o2 * 512:ci * 1024 + o2 * 512 + 512],
                                    start=(ci == 0), stop=(ci == DC - 1)),
                                    reads=[xnt[ci], wb], writes=[ps], inc=1)
                            if do_elu:
                                kb.do("vector", lambda v, o2=o2, tf=tf, bB=bB, ps=ps: v.tensor_add(
                                    out=tf.t[:, o2 * 512:o2 * 512 + 512], in0=bB.t[:, o2 * 512:o2 * 512 + 512],
                                    in1=ps.t[:, 0:W]), reads=[ps, bB], writes=[tf])
                            else:
                                kb.do("vector", lambda v, o2=o2, dst=dst, bB=bB, ps=ps: v.tensor_add(
                                    out=dst.t[:, o2 * 512:o2 * 512 + 512], in0=bB.t[:, o2 * 512:o2 * 512 + 512],
                                    in1=ps.t[:, 0:W]), reads=[ps, bB], writes=[dst])
                        if do_elu:
                            kb.do("vector", lambda v, tf=tf, tfb=tfb: v.tensor_scalar(
                                out=tfb.t[:], in0=tf.t[:], scalar1=0.0, scalar2=None, op0=ALU.min),
                                reads=[tf], writes=[tfb])
                            kb.do("scalar", lambda s, tfb=tfb: s.activation(out=tfb.t[:], in_=tfb.t[:], func=AF.Exp),
                                  reads=[tfb], writes=[tfb])
                            kb.do("vector", lambda v, tf=tf: v.tensor_scalar(
                                out=tf.t[:], in0=tf.t[:], scalar1=0.0, scalar2=None, op0=ALU.max),
                                reads=[tf], writes=[tf])
                            kb.do("vector", lambda v, tf=tf, tfb=tfb, dst=dst: v.tensor_add(
                                out=dst.t[:], in0=tf.t[:], in1=tfb.t[:]),
                                reads=[tf, tfb], writes=[dst])
                    for h in range(16):
                        kb.do("tensor", lambda t, h=h, kt=kt, ktile=ktile, vtile=vtile: t.matmul(
                            Pkv.t[0:64, h * 64:(h + 1) * 64], ktile.t[:, h * 64:(h + 1) * 64],
                            vtile.t[:, h * 64:(h + 1) * 64], start=(kt == 0), stop=(kt == nkt_total - 1)),
                            reads=[ktile, vtile], writes=[Pkv], inc=1)
                    for oo in range(DC):
                        kb.do("tensor", lambda t, oo=oo, kt=kt, ktile=ktile: t.matmul(
                            Pks.t[:, oo:oo + 1], ktile.t[:, oo * 128:(oo + 1) * 128],
                            c["ones_bcol"].t[:], start=(kt == 0), stop=(kt == nkt_total - 1)),
                            reads=[ktile, c["ones_bcol"]], writes=[Pks], inc=1)
            kb.do("vector", lambda v: v.tensor_copy(out=kv_sb.t[0:64, :], in_=Pkv.t[0:64, :]),
                  reads=[Pkv], writes=[kv_sb])
            dma_out(kb, kvb_d[:, :], kv_sb, src_sl=kv_sb.t[0:64, :])
            dma_in(kb, kv_sb, kvb_d[:, :], dst_sl=kv_sb.t[64:128, :])
            kb.do("vector", lambda v: v.tensor_copy(out=ks_sb.t[:], in_=Pks.t[:, :]),
                  reads=[Pks], writes=[ks_sb])
            kb.barrier()

        s2 = ExitStack()
        with s2:
            wqb = sb("wqb", [128, DC * 1024], BF16, s2)
            wstg = sb("wstg2", [128, 1024], F32, s2)
            q_sb = sb("q_sb", [128, DC * NT], BF16, s2)
            att = sb("att", [128, DC * NT], BF16, s2)
            bcol = sb("bcol", [128, 8], F32, s2)
            qta = sb("qta2", [128, W], F32, s2)
            qtb = sb("qtb2", [128, W], F32, s2)
            nsb = sb("nsb", [64, W], F32, s2)
            srow = sb("srow1", [1, 16], F32, s2)
            for ci in range(DC):
                dma_in(kb, wstg, wq_d[:, ci * 1024:(ci + 1) * 1024])
                cast(kb, wqb, wqb.t[:, ci * 1024:(ci + 1) * 1024], wstg, wstg.t[:])
            dma_in(kb, bcol, bq_d[:, :])
            dma_in(kb, srow, stdp_d[:, :])
            kb.do("scalar", lambda s: s.activation(out=srow.t[:], in_=srow.t[:], func=AF.Sigmoid),
                  reads=[srow], writes=[srow])
            kb.do("tensor", lambda t: t.matmul(P[4].t[0:64, 0:16], c["ones_row"].t[0:1, 0:64], srow.t[:],
                                               start=True, stop=True), reads=[c["ones_row"], srow], writes=[P[4]])
            kb.do("vector", lambda v: v.tensor_copy(out=sigB.t[:], in_=P[4].t[0:64, 0:16]),
                  reads=[P[4]], writes=[sigB])
            for oo in range(DC):
                for tc in range(NT // W):
                    col = tc * W
                    for ci in range(DC):
                        kb.do("tensor", lambda t, ci=ci, oo=oo, col=col: t.matmul(
                            P[0].t[:, 0:W], wqb.t[:, ci * 1024 + oo * 128:ci * 1024 + oo * 128 + 128],
                            xn[ci].t[:, col:col + W], start=(ci == 0), stop=(ci == DC - 1)),
                            reads=[wqb, xn[ci]], writes=[P[0]], inc=1)
                    kb.do("vector", lambda v, oo=oo: v.tensor_scalar(
                        out=qta.t[:], in0=P[0].t[:, 0:W], scalar1=bcol.t[:, oo:oo + 1], scalar2=None,
                        op0=ALU.add), reads=[P[0], bcol], writes=[qta])
                    kb.do("vector", lambda v: v.tensor_scalar(
                        out=qtb.t[:], in0=qta.t[:], scalar1=0.0, scalar2=None, op0=ALU.min),
                        reads=[qta], writes=[qtb])
                    kb.do("scalar", lambda s: s.activation(out=qtb.t[:], in_=qtb.t[:], func=AF.Exp),
                          reads=[qtb], writes=[qtb])
                    kb.do("vector", lambda v: v.tensor_scalar(
                        out=qta.t[:], in0=qta.t[:], scalar1=0.0, scalar2=None, op0=ALU.max),
                        reads=[qta], writes=[qta])
                    kb.do("vector", lambda v, oo=oo, col=col: v.tensor_add(
                        out=q_sb.t[:, oo * NT + col:oo * NT + col + W], in0=qta.t[:], in1=qtb.t[:]),
                        reads=[qta, qtb], writes=[q_sb])
            for h in range(16):
                hp, ho = h // 2, 64 * (h % 2)
                for tc in range(NT // W):
                    col = tc * W
                    qsl = q_sb.t[ho:ho + 64, hp * NT + col:hp * NT + col + W]
                    kb.do("tensor", lambda t, h=h, ho=ho, qsl=qsl: t.matmul(
                        P[0].t[0:64, 0:W], kv_sb.t[ho:ho + 64, h * 64:(h + 1) * 64], qsl,
                        start=True, stop=True), reads=[kv_sb, q_sb], writes=[P[0]])
                    kb.do("tensor", lambda t, h=h, hp=hp, ho=ho, qsl=qsl: t.matmul(
                        P[1].t[0:1, 0:W], ks_sb.t[ho:ho + 64, hp:hp + 1], qsl,
                        start=True, stop=True), reads=[ks_sb, q_sb], writes=[P[1]])
                    kb.do("vector", lambda v: v.tensor_scalar(
                        out=rowt.t[:], in0=P[1].t[0:1, 0:W], scalar1=1e-6, scalar2=None, op0=ALU.add),
                        reads=[P[1]], writes=[rowt])
                    kb.do("vector", lambda v: v.reciprocal(out=rowt.t[:], in_=rowt.t[:]),
                          reads=[rowt], writes=[rowt])
                    kb.do("tensor", lambda t: t.matmul(P[2].t[0:64, 0:W], c["ones_row"].t[0:1, 0:64],
                                                       rowt.t[:], start=True, stop=True),
                          reads=[c["ones_row"], rowt], writes=[P[2]])
                    kb.do("scalar", lambda s: s.copy(out=nsb.t[:], in_=P[0].t[0:64, 0:W]),
                          reads=[P[0]], writes=[nsb])
                    kb.do("vector", lambda v, h=h, hp=hp, ho=ho, col=col: v.scalar_tensor_tensor(
                        out=att.t[ho:ho + 64, hp * NT + col:hp * NT + col + W], in0=nsb.t[:],
                        scalar=sigB.t[0:64, h:h + 1], in1=P[2].t[0:64, 0:W],
                        op0=ALU.mult, op1=ALU.mult), reads=[nsb, sigB, P[2]], writes=[att])
            for ci in range(DC):
                dma_in(kb, wstg, wo_d[:, ci * 1024:(ci + 1) * 1024])
                cast(kb, wqb, wqb.t[:, ci * 1024:(ci + 1) * 1024], wstg, wstg.t[:])
            dma_in(kb, bcol, bo_d[:, :])
            for oo in range(DC):
                for tc in range(NT // W):
                    col = tc * W
                    for ci in range(DC):
                        kb.do("tensor", lambda t, ci=ci, oo=oo, col=col: t.matmul(
                            P[0].t[:, 0:W], wqb.t[:, ci * 1024 + oo * 128:ci * 1024 + oo * 128 + 128],
                            att.t[:, ci * NT + col:ci * NT + col + W], start=(ci == 0), stop=(ci == DC - 1)),
                            reads=[wqb, att], writes=[P[0]], inc=1)
                    kb.do("vector", lambda v, oo=oo: v.tensor_scalar(
                        out=qta.t[:], in0=P[0].t[:, 0:W], scalar1=bcol.t[:, oo:oo + 1], scalar2=None,
                        op0=ALU.add), reads=[P[0], bcol], writes=[qta])
                    dma_out(kb, s1sp_d[oo * 128:(oo + 1) * 128, col:col + W], qta)
            kb.barrier()

        s3 = ExitStack()
        with s3:
            racc = [sb(f"racc{i}", [128, NT], F32, s3) for i in range(DC)]
            wstg = sb("wstg3", [128, 1024], F32, s3)
            rwb = sb("rwb", [128, DC * 16], F32, s3)
            rwbb = sb("rwbb", [128, DC * 16], BF16, s3)
            ebB = sb("ebB", [128, 16], F32, s3)
            erow = sb("erow", [1, 16], F32, s3)
            iot = sb("iot", [128, 16], F32, s3)
            iot99 = sb("iot99", [128, 16], F32, s3)
            sct = sb("sct", [128, 16], F32, s3)
            scs = sb("scs", [128, 16], F32, s3)
            cur = sb("cur", [128, 16], F32, s3)
            tmp = sb("tmp", [128, 16], F32, s3)
            tmp2 = sb("tmp2", [128, 16], F32, s3)
            soh = sb("soh", [128, 16], F32, s3)
            hi = sb("hi", [128, 8], F32, s3)
            lo = sb("lo", [128, 8], F32, s3)
            gsb = sb("gsb", [128, 4], F32, s3)
            col1 = sb("col1", [128, 1], F32, s3)
            col2 = sb("col2", [128, 1], F32, s3)
            col3 = sb("col3", [128, 1], F32, s3)
            tsum = sb("tsum", [128, 1], F32, s3)
            tw = sb("tw", [128, 1], F32, s3)
            t1 = sb("t1l", [128, W], F32, s3)
            t2 = sb("t2l", [128, W], F32, s3)
            xrr = sb("xrr", [128, W], F32, s3)

            dma_in(kb, rwb, rw_d[:, :])
            cast(kb, rwbb, rwbb.t[:], rwb, rwb.t[:])
            dma_in(kb, erow, eb_d[:, :])
            kb.do("tensor", lambda t: t.matmul(P[4].t[:, 0:16], c["ones_row"].t[:], erow.t[:],
                                               start=True, stop=True), reads=[c["ones_row"], erow], writes=[P[4]])
            kb.do("vector", lambda v: v.tensor_copy(out=ebB.t[:], in_=P[4].t[:, 0:16]),
                  reads=[P[4]], writes=[ebB])
            kb.do("gpsimd", lambda g: g.iota(iot.t[:], pattern=[[1, 16]], base=0, channel_multiplier=0,
                                             allow_small_or_imprecise_dtypes=True), writes=[iot])
            kb.do("gpsimd", lambda g: g.iota(iot99.t[:], pattern=[[1, 16]], base=-99, channel_multiplier=0,
                                             allow_small_or_imprecise_dtypes=True), writes=[iot99])

            for tt in range(NT // 128):
                for ci in range(DC):
                    kb.do("tensor", lambda t, ci=ci, tt=tt: t.matmul(
                        P[0].t[:, 0:16], xn[ci].t[:, tt * 128:(tt + 1) * 128],
                        rwbb.t[:, ci * 16:(ci + 1) * 16], start=(ci == 0), stop=(ci == DC - 1)),
                        reads=[xn[ci], rwbb], writes=[P[0]], inc=1)
                kb.do("scalar", lambda s: s.activation(out=scs.t[:], in_=P[0].t[:, 0:16], func=AF.Sigmoid),
                      reads=[P[0]], writes=[scs])
                kb.do("vector", lambda v: v.tensor_add(out=sct.t[:], in0=scs.t[:], in1=ebB.t[:]),
                      reads=[scs, ebB], writes=[sct])
                kb.do("vector", lambda v: v.tensor_tensor(out=hi.t[:], in0=sct.t[:, 0:15:2],
                                                          in1=sct.t[:, 1:16:2], op=ALU.max),
                      reads=[sct], writes=[hi])
                kb.do("vector", lambda v: v.tensor_tensor(out=lo.t[:], in0=sct.t[:, 0:15:2],
                                                          in1=sct.t[:, 1:16:2], op=ALU.min),
                      reads=[sct], writes=[lo])
                kb.do("vector", lambda v: v.tensor_tensor(out=gsb.t[:], in0=hi.t[:, 0:7:2],
                                                          in1=hi.t[:, 1:8:2], op=ALU.max),
                      reads=[hi], writes=[gsb])
                kb.do("vector", lambda v: v.tensor_tensor(out=tmp.t[:, 0:4], in0=hi.t[:, 0:7:2],
                                                          in1=hi.t[:, 1:8:2], op=ALU.min),
                      reads=[hi], writes=[tmp])
                kb.do("vector", lambda v: v.tensor_tensor(out=tmp.t[:, 4:8], in0=lo.t[:, 0:7:2],
                                                          in1=lo.t[:, 1:8:2], op=ALU.max),
                      reads=[lo], writes=[tmp])
                kb.do("vector", lambda v: v.tensor_tensor(out=tmp.t[:, 0:4], in0=tmp.t[:, 0:4],
                                                          in1=tmp.t[:, 4:8], op=ALU.max),
                      reads=[tmp], writes=[tmp])
                kb.do("vector", lambda v: v.tensor_add(out=gsb.t[:], in0=gsb.t[:], in1=tmp.t[:, 0:4]),
                      reads=[tmp, gsb], writes=[gsb])
                kb.do("vector", lambda v: v.tensor_tensor(out=hi.t[:, 0:2], in0=gsb.t[:, 0:3:2],
                                                          in1=gsb.t[:, 1:4:2], op=ALU.max),
                      reads=[gsb], writes=[hi])
                kb.do("vector", lambda v: v.tensor_tensor(out=lo.t[:, 0:2], in0=gsb.t[:, 0:3:2],
                                                          in1=gsb.t[:, 1:4:2], op=ALU.min),
                      reads=[gsb], writes=[lo])
                kb.do("vector", lambda v: v.tensor_tensor(out=col1.t[:], in0=hi.t[:, 0:1],
                                                          in1=hi.t[:, 1:2], op=ALU.min),
                      reads=[hi], writes=[col1])
                kb.do("vector", lambda v: v.tensor_tensor(out=col2.t[:], in0=lo.t[:, 0:1],
                                                          in1=lo.t[:, 1:2], op=ALU.max),
                      reads=[lo], writes=[col2])
                kb.do("vector", lambda v: v.tensor_tensor(out=col1.t[:], in0=col1.t[:],
                                                          in1=col2.t[:], op=ALU.max),
                      reads=[col1, col2], writes=[col1])
                kb.do("vector", lambda v: v.tensor_scalar(out=gsb.t[:], in0=gsb.t[:],
                                                          scalar1=col1.t[:, 0:1], scalar2=None, op0=ALU.is_ge),
                      reads=[gsb, col1], writes=[gsb])
                for g in range(4):
                    kb.do("vector", lambda v, g=g: v.tensor_scalar(
                        out=cur.t[:, g * 4:(g + 1) * 4], in0=sct.t[:, g * 4:(g + 1) * 4],
                        scalar1=gsb.t[:, g:g + 1], scalar2=None, op0=ALU.mult),
                        reads=[sct, gsb], writes=[cur])
                kb.do("vector", lambda v: v.memset(tsum.t[:], 1e-20), writes=[tsum])
                kb.do("vector", lambda v, tt=tt: v.memset(comb.t[:, tt * 16:(tt + 1) * 16], 0.0),
                      writes=[comb])
                for j in range(4):
                    kb.do("vector", lambda v: v.reduce_max(col1.t[:], cur.t[:], axis=AX.X),
                          reads=[cur], writes=[col1])
                    kb.do("vector", lambda v: v.tensor_scalar(out=tmp.t[:], in0=cur.t[:],
                                                              scalar1=col1.t[:, 0:1], scalar2=None, op0=ALU.is_ge),
                          reads=[cur, col1], writes=[tmp])
                    kb.do("vector", lambda v: v.tensor_mul(out=tmp.t[:], in0=tmp.t[:], in1=iot99.t[:]),
                          reads=[tmp, iot99], writes=[tmp])
                    kb.do("vector", lambda v: v.tensor_scalar(out=tmp.t[:], in0=tmp.t[:],
                                                              scalar1=99.0, scalar2=None, op0=ALU.add),
                          reads=[tmp], writes=[tmp])
                    kb.do("vector", lambda v: v.tensor_reduce(out=col2.t[:], in_=tmp.t[:], axis=AX.X,
                                                              op=ALU.min),
                          reads=[tmp], writes=[col2])
                    kb.do("vector", lambda v: v.tensor_scalar(out=soh.t[:], in0=iot.t[:],
                                                              scalar1=col2.t[:, 0:1], scalar2=None, op0=ALU.is_equal),
                          reads=[iot, col2], writes=[soh])
                    kb.do("vector", lambda v: v.tensor_mul(out=tmp2.t[:], in0=soh.t[:], in1=scs.t[:]),
                          reads=[soh, scs], writes=[tmp2])
                    kb.do("vector", lambda v: v.reduce_sum(tw.t[:], tmp2.t[:], axis=AX.X),
                          reads=[tmp2], writes=[tw])
                    kb.do("vector", lambda v: v.tensor_add(out=tsum.t[:], in0=tsum.t[:], in1=tw.t[:]),
                          reads=[tw, tsum], writes=[tsum])
                    kb.do("vector", lambda v, tt=tt: v.scalar_tensor_tensor(
                        out=comb.t[:, tt * 16:(tt + 1) * 16], in0=soh.t[:], scalar=tw.t[:, 0:1],
                        in1=comb.t[:, tt * 16:(tt + 1) * 16], op0=ALU.mult, op1=ALU.add),
                        reads=[soh, tw, comb], writes=[comb])
                    kb.do("vector", lambda v: v.scalar_tensor_tensor(
                        out=cur.t[:], in0=soh.t[:], scalar=-1e9, in1=cur.t[:], op0=ALU.mult, op1=ALU.add),
                        reads=[soh, cur], writes=[cur])
                kb.do("vector", lambda v: v.reciprocal(out=tsum.t[:], in_=tsum.t[:]),
                      reads=[tsum], writes=[tsum])
                kb.do("vector", lambda v: v.tensor_scalar(out=tsum.t[:], in0=tsum.t[:],
                                                          scalar1=2.5, scalar2=None, op0=ALU.mult),
                      reads=[tsum], writes=[tsum])
                kb.do("vector", lambda v, tt=tt: v.tensor_scalar(
                    out=comb.t[:, tt * 16:(tt + 1) * 16], in0=comb.t[:, tt * 16:(tt + 1) * 16],
                    scalar1=tsum.t[:, 0:1], scalar2=None, op0=ALU.mult),
                    reads=[comb, tsum], writes=[comb])
                kb.do("tensor", lambda t, tt=tt: t.transpose(P[3].t[0:16, 0:128],
                                                             comb.t[:, tt * 16:(tt + 1) * 16], c["ident"].t[:]),
                      reads=[comb, c["ident"]], writes=[P[3]])
                kb.do("vector", lambda v, tt=tt: v.tensor_copy(out=ctr.t[0:16, tt * 128:(tt + 1) * 128],
                                                               in_=P[3].t[0:16, 0:128]),
                      reads=[P[3]], writes=[ctr])

            s3b = ExitStack()
            s3b.__enter__()
            shw = sb("shw", [128, DC * 1024], BF16, s3b)
            hs = sb("hs", [128, DC * NT], BF16, s3b)
            for ci in range(DC):
                dma_in(kb, wstg, shg_d[:, ci * 1024:(ci + 1) * 1024])
                cast(kb, shw, shw.t[:, ci * 1024:(ci + 1) * 1024], wstg, wstg.t[:])
            for oo in range(DC):
                for tc in range(NT // W):
                    col = tc * W
                    for ci in range(DC):
                        kb.do("tensor", lambda t, ci=ci, oo=oo, col=col: t.matmul(
                            P[0].t[:, 0:W], shw.t[:, ci * 1024 + oo * 128:ci * 1024 + oo * 128 + 128],
                            xn[ci].t[:, col:col + W], start=(ci == 0), stop=(ci == DC - 1)),
                            reads=[shw, xn[ci]], writes=[P[0]], inc=1)
                    kb.do("scalar", lambda s, oo=oo, col=col: s.activation(
                        out=hs.t[:, oo * NT + col:oo * NT + col + W], in_=P[0].t[:, 0:W], func=AF.Silu),
                        reads=[P[0]], writes=[hs])
            for ci in range(DC):
                dma_in(kb, wstg, shu_d[:, ci * 1024:(ci + 1) * 1024])
                cast(kb, shw, shw.t[:, ci * 1024:(ci + 1) * 1024], wstg, wstg.t[:])
            for oo in range(DC):
                for tc in range(NT // W):
                    col = tc * W
                    for ci in range(DC):
                        kb.do("tensor", lambda t, ci=ci, oo=oo, col=col: t.matmul(
                            P[0].t[:, 0:W], shw.t[:, ci * 1024 + oo * 128:ci * 1024 + oo * 128 + 128],
                            xn[ci].t[:, col:col + W], start=(ci == 0), stop=(ci == DC - 1)),
                            reads=[shw, xn[ci]], writes=[P[0]], inc=1)
                    kb.do("vector", lambda v, oo=oo, col=col: v.tensor_mul(
                        out=hs.t[:, oo * NT + col:oo * NT + col + W],
                        in0=hs.t[:, oo * NT + col:oo * NT + col + W], in1=P[0].t[:, 0:W]),
                        reads=[P[0], hs], writes=[hs])
            for ci in range(DC):
                dma_in(kb, wstg, shd_d[:, ci * 1024:(ci + 1) * 1024])
                cast(kb, shw, shw.t[:, ci * 1024:(ci + 1) * 1024], wstg, wstg.t[:])
            for oo in range(DC):
                for tc in range(NT // W):
                    col = tc * W
                    for ci in range(DC):
                        kb.do("tensor", lambda t, ci=ci, oo=oo, col=col: t.matmul(
                            P[0].t[:, 0:W], shw.t[:, ci * 1024 + oo * 128:ci * 1024 + oo * 128 + 128],
                            hs.t[:, ci * NT + col:ci * NT + col + W], start=(ci == 0), stop=(ci == DC - 1)),
                            reads=[shw, hs], writes=[P[0]], inc=1)
                    kb.do("scalar", lambda s, oo=oo, col=col: s.copy(
                        out=racc[oo].t[:, col:col + W], in_=P[0].t[:, 0:W]),
                        reads=[P[0]], writes=[racc[oo]])

            kb.barrier()
            s3b.__exit__(None, None, None)

            s3c = ExitStack()
            s3c.__enter__()
            h_all = sb("h_all", [128, 4 * NT], BF16, s3c)
            gwb = sb("gwb", [128, DC * 512], BF16, s3c)
            uwb = sb("uwb", [128, DC * 512], BF16, s3c)
            dwb = sb("dwb", [128, 4 * 1024], BF16, s3c)
            oneh = sb("oneh", [16, 16 * 128], F32, s3c)
            kb.do("gpsimd", lambda g: g.iota(oneh.t[:], pattern=[[1, 16], [0, 128]], base=0,
                                             channel_multiplier=-1, allow_small_or_imprecise_dtypes=True),
                  writes=[oneh])
            kb.do("vector", lambda v: v.tensor_scalar(out=oneh.t[:], in0=oneh.t[:], scalar1=0.0,
                                                      scalar2=None, op0=ALU.is_equal),
                  reads=[oneh], writes=[oneh])
            for e in range(16):
                for wd, wb, ncol in ((gw_d, gwb, DC * 512), (uw_d, uwb, DC * 512), (dw_d, dwb, 4 * 1024)):
                    for g4 in range(ncol // 1024):
                        kb.do("sync", lambda eng, wd=wd, e=e, g4=g4: eng.dma_start(
                            out=wstg.t[:], in_=wd[e, :, g4 * 1024:(g4 + 1) * 1024]),
                            writes=[wstg], inc=16, sem_key="ld_" + wstg.name)
                        cast(kb, wb, wb.t[:, g4 * 1024:(g4 + 1) * 1024], wstg, wstg.t[:])
                for tc in range(NT // W):
                    col = tc * W
                    kb.do("tensor", lambda t, e=e, col=col: t.matmul(
                        P[4].t[:, 0:W], oneh.t[0:16, e * 128:(e + 1) * 128],
                        ctr.t[0:16, col:col + W], start=True, stop=True),
                        reads=[oneh, ctr], writes=[P[4]])
                    for io in range(4):
                        for ci in range(DC):
                            kb.do("tensor", lambda t, ci=ci, io=io, col=col: t.matmul(
                                P[0].t[:, 0:W], gwb.t[:, ci * 512 + io * 128:ci * 512 + io * 128 + 128],
                                xn[ci].t[:, col:col + W], start=(ci == 0), stop=(ci == DC - 1)),
                                reads=[gwb, xn[ci]], writes=[P[0]], inc=1)
                            kb.do("tensor", lambda t, ci=ci, io=io, col=col: t.matmul(
                                P[1].t[:, 0:W], uwb.t[:, ci * 512 + io * 128:ci * 512 + io * 128 + 128],
                                xn[ci].t[:, col:col + W], start=(ci == 0), stop=(ci == DC - 1)),
                                reads=[uwb, xn[ci]], writes=[P[1]], inc=1)
                        kb.do("scalar", lambda s: s.activation(out=t1.t[:], in_=P[0].t[:, 0:W], func=AF.Silu),
                              reads=[P[0]], writes=[t1])
                        kb.do("vector", lambda v: v.tensor_mul(out=t2.t[:], in0=t1.t[:], in1=P[1].t[:, 0:W]),
                              reads=[t1, P[1]], writes=[t2])
                        kb.do("vector", lambda v, io=io, col=col: v.tensor_mul(
                            out=h_all.t[:, io * NT + col:io * NT + col + W], in0=t2.t[:], in1=P[4].t[:, 0:W]),
                            reads=[t2, P[4]], writes=[h_all])
                    for oo in range(DC):
                        for io in range(4):
                            kb.do("tensor", lambda t, io=io, oo=oo, col=col: t.matmul(
                                P[2].t[:, 0:W], dwb.t[:, io * 1024 + oo * 128:io * 1024 + oo * 128 + 128],
                                h_all.t[:, io * NT + col:io * NT + col + W],
                                start=(io == 0), stop=(io == 3)),
                                reads=[dwb, h_all], writes=[P[2]], inc=1)
                        kb.do("vector", lambda v, oo=oo, col=col: v.tensor_add(
                            out=racc[oo].t[:, col:col + W], in0=racc[oo].t[:, col:col + W],
                            in1=P[2].t[:, 0:W]), reads=[P[2], racc[oo]], writes=[racc[oo]])

            for oo in range(DC):
                for tc in range(NT // W):
                    col = tc * W
                    kb.do("scalar", lambda s, oo=oo, col=col: s.activation(
                        out=t1.t[:], in_=racc[oo].t[:, col:col + W], func=AF.Silu),
                        reads=[racc[oo]], writes=[t1])
                    dma_in(kb, t2, s1sp_d[oo * 128:(oo + 1) * 128, col:col + W])
                    dma_in(kb, xrr, x_d[oo * 128:(oo + 1) * 128, col:col + W])
                    kb.do("vector", lambda v: v.tensor_mul(out=t1.t[:], in0=t1.t[:], in1=t2.t[:]),
                          reads=[t1, t2], writes=[t1])
                    kb.do("vector", lambda v: v.tensor_add(out=xrr.t[:], in0=xrr.t[:], in1=t1.t[:]),
                          reads=[t1, xrr], writes=[xrr])
                    dma_out(kb, xo_d[oo * 128:(oo + 1) * 128, col:col + W], xrr)
            kb.barrier()
            s3c.__exit__(None, None, None)
        kb.finalize(nc, es)
    return nc



def _fm(x):
    return np.ascontiguousarray(np.transpose(np.asarray(x), (0, 2, 1)))


def _prep_l0a(inputs):
    Q = np.asarray(inputs["L0_attn_Q"])
    K = np.asarray(inputs["L0_attn_K"])
    V = np.asarray(inputs["L0_attn_V"])
    Wt = np.asarray(inputs["L0_attn_W"])
    Qm = np.ascontiguousarray(np.transpose(Q, (1, 2, 0)).reshape(D, 1024))
    Km = np.ascontiguousarray(np.transpose(K, (1, 2, 0)).reshape(D, 1024))
    Vm = np.ascontiguousarray(np.transpose(V, (1, 2, 0)).reshape(D, 1024))
    Wm = np.ascontiguousarray(np.transpose(Wt, (1, 2, 0)).reshape(1024, 1024))
    ww_all = np.ascontiguousarray(np.transpose(Wt, (2, 1, 0)))
    out = []
    for hh in range(2):
        cols = slice(hh * 512, (hh + 1) * 512)
        out.append({
            "qw": np.ascontiguousarray(Qm[:, cols].reshape(DC, 128, 512)),
            "kw": np.ascontiguousarray(Km[:, cols].reshape(DC, 128, 512)),
            "vw": np.ascontiguousarray(Vm[:, cols].reshape(DC, 128, 512)),
            "ww": np.ascontiguousarray(ww_all[hh * 8:(hh + 1) * 8]),
        })
    return out


def _chunk_w(wT, ncols):
    din, dout = wT.shape
    nch = din // 128
    return np.ascontiguousarray(wT.reshape(nch, 128, dout).transpose(1, 0, 2).reshape(128, nch * dout))


def _prep_l0b(inputs):
    w1T = np.ascontiguousarray(np.asarray(inputs["L0_ffn_w1_w"]).T)
    w2T = np.ascontiguousarray(np.asarray(inputs["L0_ffn_w2_w"]).T)
    h1T = np.ascontiguousarray(np.asarray(inputs["L0_ffn_h1_w"]).T)
    h2T = np.ascontiguousarray(np.asarray(inputs["L0_ffn_h2_w"]).T)
    w3T = np.ascontiguousarray(np.asarray(inputs["L0_ffn_w3_w"]).T)
    def lay_w(wT):
        return np.ascontiguousarray(wT.reshape(8, 128, 32, 128).transpose(2, 1, 0, 3).reshape(32, 128, 1024))

    def lay_h(hT, nfo):
        din, dout = hT.shape
        return np.ascontiguousarray(hT.reshape(din // 512, 4, 128, nfo, 128)
                                    .transpose(3, 0, 2, 1, 4).reshape(nfo, din // 512, 128, 512))

    d = {
        "w1": lay_w(w1T), "w2": lay_w(w2T),
        "h1": lay_h(h1T, 32), "h2": lay_h(h2T, 32),
        "w3": lay_h(w3T, 8),
        "s1": np.ascontiguousarray(np.asarray(inputs["L0_ffn_w1_s"]).reshape(32, 128).T),
        "s2": np.ascontiguousarray(np.asarray(inputs["L0_ffn_w2_s"]).reshape(32, 128).T),
        "sh1": np.ascontiguousarray(np.asarray(inputs["L0_ffn_h1_s"]).reshape(32, 128).T),
        "sh2": np.ascontiguousarray(np.asarray(inputs["L0_ffn_h2_s"]).reshape(32, 128).T),
        "s3": np.ascontiguousarray(np.asarray(inputs["L0_ffn_w3_s"]).reshape(8, 128).T),
    }
    return d


def run_l0(X0fm, inputs):
    heads = _prep_l0a(inputs)
    nc_a = build_l0a()
    in_maps = []
    for core in range(8):
        b, hh = core // 2, core % 2
        m = {"x": X0fm[b]}
        m.update(heads[hh])
        in_maps.append(m)
    res = run_bass_kernel_spmd(nc_a, in_maps, core_ids=list(range(8)))
    zparts = [res.results[c]["z"] for c in range(8)]

    ffn = _prep_l0b(inputs)
    nc_b = build_l0b()
    in_maps = []
    for core in range(8):
        b, half = core // 2, core % 2
        colsl = slice(half * NT, (half + 1) * NT)
        m = {"x": np.ascontiguousarray(X0fm[b][:, colsl]),
             "za": np.ascontiguousarray(zparts[2 * b][:, colsl]),
             "zb": np.ascontiguousarray(zparts[2 * b + 1][:, colsl])}
        m.update(ffn)
        in_maps.append(m)
    res = run_bass_kernel_spmd(nc_b, in_maps, core_ids=list(range(8)))
    X1 = np.empty((B, D, S), np.float32)
    for core in range(8):
        b, half = core // 2, core % 2
        X1[b][:, half * NT:(half + 1) * NT] = res.results[core]["xo"]
    return X1



def build_l2():
    nc = bass.Bass("TRN2", target_bir_lowering=False, debug=False)
    x_d = nc.dram_tensor("x", [D, S], F32, kind="ExternalInput").ap()
    wq_d = nc.dram_tensor("wq", [128, DC * 1024], F32, kind="ExternalInput").ap()
    wk_d = nc.dram_tensor("wk", [128, DC * 1024], F32, kind="ExternalInput").ap()
    wv_d = nc.dram_tensor("wv", [128, DC * 1024], F32, kind="ExternalInput").ap()
    wo_d = nc.dram_tensor("wo", [128, DC * 1024], F32, kind="ExternalInput").ap()
    bq_d = nc.dram_tensor("bq", [128, 8], F32, kind="ExternalInput").ap()
    bk_d = nc.dram_tensor("bk", [128, 8], F32, kind="ExternalInput").ap()
    bv_d = nc.dram_tensor("bv", [1, 1024], F32, kind="ExternalInput").ap()
    bo_d = nc.dram_tensor("bo", [128, 8], F32, kind="ExternalInput").ap()
    rm_d = nc.dram_tensor("rm", [64, 256], F32, kind="ExternalInput").ap()
    wr_d = nc.dram_tensor("wr", [128, DC * 1024], F32, kind="ExternalInput").ap()
    wk2_d = nc.dram_tensor("wk2", [128, DC * 1024], F32, kind="ExternalInput").ap()
    wv2_d = nc.dram_tensor("wv2", [128, DC * 1024], F32, kind="ExternalInput").ap()
    mur_d = nc.dram_tensor("mur", [128, 8], F32, kind="ExternalInput").ap()
    muk_d = nc.dram_tensor("muk", [128, 8], F32, kind="ExternalInput").ap()
    pm_d = nc.dram_tensor("pm", [1, 1], F32, kind="ExternalInput").ap()
    s2sp_d = nc.dram_tensor("s2sp", [D, NT], F32, kind="Internal").ap()
    xo_d = nc.dram_tensor("xo", [D, NT], F32, kind="ExternalOutput").ap()

    kb = KB()
    es = ExitStack()
    with es:
        def sb(n, sh, dt, stack=None):
            st = stack or es
            return Buf(st.enter_context(nc.sbuf_tensor(n, sh, dt)), n)
        c = mk_consts(nc, es, kb)
        P = [Buf(es.enter_context(nc.psum_tensor(f"P{i}", [128, 512], F32)), f"P{i}") for i in range(3)]
        sc = mk_ln_scratch(nc, es, "ln")
        xn = [sb(f"xn{i}", [128, NT], BF16) for i in range(DC)]
        xn_last = sb("xn_last", [128, 8], BF16)
        kv_sb = [sb(f"kv2_{i}", [128, 1024], BF16) for i in range(2)]
        ks_sb = sb("ks2", [128, 32], BF16)
        rmb = sb("rmb", [128, 256], BF16)
        rmf = sb("rmf", [64, 256], F32)
        rowt = sb("rowt2", [1, 512], F32)

        dma_in(kb, rmf, rm_d[:, :])
        cast(kb, rmb, rmb.t[0:64, :], rmf, rmf.t[:])
        cast(kb, rmb, rmb.t[64:128, :], rmf, rmf.t[:])

        def x_ap(ci, col):
            return x_d[ci * 128:(ci + 1) * 128, col:col + W]

        s1 = ExitStack()
        with s1:
            Pkv = [Buf(es.enter_context(nc.psum_tensor(f"Pkv2_{i}", [128, 1024], F32)), f"Pkv2_{i}")
                   for i in range(2)]
            Pks = Buf(es.enter_context(nc.psum_tensor("Pks2", [128, 32], F32)), "Pks2")
            wkb = sb("wkb", [128, DC * 1024], BF16, s1)
            wvb = sb("wvb", [128, DC * 1024], BF16, s1)
            wstg = sb("wstgA", [128, 1024], F32, s1)
            bkc = sb("bkc", [128, 8], F32, s1)
            bvB = sb("bvB", [128, 1024], F32, s1)
            brow = sb("browA", [1, 1024], F32, s1)
            kfm = sb("kfm", [128, DC * 512], BF16, s1)
            vtm = sb("vtm", [128, 4 * 1024], BF16, s1)
            kp = sb("kp", [128, 256], BF16, s1)
            fta = sb("ftaA", [128, 256], F32, s1)
            ftb = sb("ftbA", [128, 256], F32, s1)
            xnt = [sb(f"xntA{i}", [128, W], BF16, s1) for i in range(DC)]
            for ci in range(DC):
                dma_in(kb, wstg, wk_d[:, ci * 1024:(ci + 1) * 1024])
                cast(kb, wkb, wkb.t[:, ci * 1024:(ci + 1) * 1024], wstg, wstg.t[:])
                dma_in(kb, wstg, wv_d[:, ci * 1024:(ci + 1) * 1024])
                cast(kb, wvb, wvb.t[:, ci * 1024:(ci + 1) * 1024], wstg, wstg.t[:])
            dma_in(kb, bkc, bk_d[:, :])
            dma_in(kb, brow, bv_d[:, :])
            for o2 in range(2):
                kb.do("tensor", lambda t, o2=o2: t.matmul(P[2].t[:, 0:512], c["ones_row"].t[:],
                                                          brow.t[0:1, o2 * 512:(o2 + 1) * 512],
                                                          start=True, stop=True),
                      reads=[c["ones_row"], brow], writes=[P[2]])
                kb.do("vector", lambda v, o2=o2: v.tensor_copy(out=bvB.t[:, o2 * 512:(o2 + 1) * 512],
                                                               in_=P[2].t[:, 0:512]),
                      reads=[P[2]], writes=[bvB])
            for tci in range(S // W):
                col = tci * W
                emit_ln(kb, c, x_ap, xnt, col, W, P, sc, out_col0=0)
                if tci < NT // W:
                    for ci in range(DC):
                        kb.do("vector", lambda v, ci=ci, col=col: v.tensor_copy(
                            out=xn[ci].t[:, col:col + W], in_=xnt[ci].t[:]),
                            reads=[xnt[ci]], writes=[xn[ci]])
                if tci == S // W - 1:
                    for ci in range(DC):
                        kb.do("vector", lambda v, ci=ci: v.tensor_copy(
                            out=xn_last.t[:, ci:ci + 1], in_=xnt[ci].t[:, W - 1:W]),
                            reads=[xnt[ci]], writes=[xn_last])
                for oo in range(DC):
                    for ci in range(DC):
                        kb.do("tensor", lambda t, ci=ci, oo=oo: t.matmul(
                            P[2].t[:, 0:W], wkb.t[:, ci * 1024 + oo * 128:ci * 1024 + oo * 128 + 128],
                            xnt[ci].t[:], start=(ci == 0), stop=(ci == DC - 1)),
                            reads=[wkb, xnt[ci]], writes=[P[2]], inc=1)
                    kb.do("vector", lambda v, oo=oo: v.tensor_scalar(
                        out=kfm.t[:, oo * W:(oo + 1) * W], in0=P[2].t[:, 0:W],
                        scalar1=bkc.t[:, oo:oo + 1], scalar2=None, op0=ALU.add),
                        reads=[P[2], bkc], writes=[kfm])
                for j in range(4):
                    for o2 in range(2):
                        for ci in range(DC):
                            kb.do("tensor", lambda t, ci=ci, j=j, o2=o2: t.matmul(
                                P[2].t[:, 0:W], xnt[ci].t[:, j * 128:(j + 1) * 128],
                                wvb.t[:, ci * 1024 + o2 * 512:ci * 1024 + o2 * 512 + 512],
                                start=(ci == 0), stop=(ci == DC - 1)),
                                reads=[xnt[ci], wvb], writes=[P[2]], inc=1)
                        kb.do("vector", lambda v, j=j, o2=o2: v.tensor_add(
                            out=vtm.t[:, j * 1024 + o2 * 512:j * 1024 + o2 * 512 + 512],
                            in0=bvB.t[:, o2 * 512:(o2 + 1) * 512], in1=P[2].t[:, 0:W]),
                            reads=[P[2], bvB], writes=[vtm])
                for h in range(16):
                    hp, ho = h // 2, 64 * (h % 2)
                    for j in range(4):
                        kt_first = (tci == 0 and j == 0)
                        kt_last = (tci == S // W - 1 and j == 3)
                        kb.do("tensor", lambda t, h=h, hp=hp, ho=ho, j=j: t.matmul(
                            P[2].t[:, 0:256], kfm.t[ho:ho + 64, hp * W + j * 128:hp * W + j * 128 + 128],
                            rmb.t[ho:ho + 64, :], start=True, stop=True),
                            reads=[kfm, rmb], writes=[P[2]])
                        kb.do("vector", lambda v: v.tensor_scalar(
                            out=fta.t[:], in0=P[2].t[:, 0:256], scalar1=0.0, scalar2=None, op0=ALU.min),
                            reads=[P[2]], writes=[fta])
                        kb.do("scalar", lambda s: s.activation(out=fta.t[:], in_=fta.t[:], func=AF.Exp),
                              reads=[fta], writes=[fta])
                        kb.do("vector", lambda v: v.tensor_scalar(
                            out=ftb.t[:], in0=P[2].t[:, 0:256], scalar1=0.0, scalar2=None, op0=ALU.max),
                            reads=[P[2]], writes=[ftb])
                        kb.do("vector", lambda v: v.tensor_add(out=fta.t[:], in0=fta.t[:], in1=ftb.t[:]),
                              reads=[ftb, fta], writes=[fta])
                        kb.do("vector", lambda v: v.tensor_scalar(
                            out=kp.t[:], in0=fta.t[:], scalar1=0.0625, scalar2=None, op0=ALU.mult),
                            reads=[fta], writes=[kp])
                        for mc in range(2):
                            kb.do("tensor", lambda t, mc=mc, h=h, j=j, kt_first=kt_first, kt_last=kt_last: t.matmul(
                                Pkv[mc].t[:, h * 64:(h + 1) * 64], kp.t[:, mc * 128:(mc + 1) * 128],
                                vtm.t[:, j * 1024 + h * 64:j * 1024 + h * 64 + 64],
                                start=kt_first, stop=kt_last),
                                reads=[kp, vtm], writes=[Pkv[mc]], inc=1)
                            kb.do("tensor", lambda t, mc=mc, h=h, kt_first=kt_first, kt_last=kt_last: t.matmul(
                                Pks.t[:, mc * 16 + h:mc * 16 + h + 1], kp.t[:, mc * 128:(mc + 1) * 128],
                                c["ones_bcol"].t[:], start=kt_first, stop=kt_last),
                                reads=[kp, c["ones_bcol"]], writes=[Pks], inc=1)
            for mc in range(2):
                kb.do("vector", lambda v, mc=mc: v.tensor_copy(out=kv_sb[mc].t[:], in_=Pkv[mc].t[:, :]),
                      reads=[Pkv[mc]], writes=[kv_sb[mc]])
            kb.do("vector", lambda v: v.tensor_copy(out=ks_sb.t[:], in_=Pks.t[:, :]),
                  reads=[Pks], writes=[ks_sb])
            kb.barrier()

        s2s = ExitStack()
        with s2s:
            wqb = sb("wqbC", [128, DC * 1024], BF16, s2s)
            wstg = sb("wstgC", [128, 1024], F32, s2s)
            bqc = sb("bqc", [128, 8], F32, s2s)
            qfm = sb("qfm", [128, DC * 512], BF16, s2s)
            qp = sb("qp", [128, 1024], BF16, s2s)
            att = sb("attC", [128, DC * NT], BF16, s2s)
            cta = sb("ctaC", [128, 512], F32, s2s)
            ctb = sb("ctbC", [128, 512], F32, s2s)
            nsb = sb("nsbC", [64, 512], F32, s2s)
            for ci in range(DC):
                dma_in(kb, wstg, wq_d[:, ci * 1024:(ci + 1) * 1024])
                cast(kb, wqb, wqb.t[:, ci * 1024:(ci + 1) * 1024], wstg, wstg.t[:])
            dma_in(kb, bqc, bq_d[:, :])
            for tc in range(NT // W):
                col = tc * W
                for oo in range(DC):
                    for ci in range(DC):
                        kb.do("tensor", lambda t, ci=ci, oo=oo, col=col: t.matmul(
                            P[2].t[:, 0:W], wqb.t[:, ci * 1024 + oo * 128:ci * 1024 + oo * 128 + 128],
                            xn[ci].t[:, col:col + W], start=(ci == 0), stop=(ci == DC - 1)),
                            reads=[wqb, xn[ci]], writes=[P[2]], inc=1)
                    kb.do("vector", lambda v, oo=oo: v.tensor_scalar(
                        out=qfm.t[:, oo * W:(oo + 1) * W], in0=P[2].t[:, 0:W],
                        scalar1=bqc.t[:, oo:oo + 1], scalar2=None, op0=ALU.add),
                        reads=[P[2], bqc], writes=[qfm])
                for h in range(16):
                    hp, ho = h // 2, 64 * (h % 2)
                    for mc in range(2):
                        kb.do("tensor", lambda t, h=h, hp=hp, ho=ho, mc=mc: t.matmul(
                            P[2].t[:, 0:W], rmb.t[ho:ho + 64, mc * 128:(mc + 1) * 128],
                            qfm.t[ho:ho + 64, hp * W:hp * W + W], start=True, stop=True),
                            reads=[rmb, qfm], writes=[P[2]])
                        kb.do("vector", lambda v: v.tensor_scalar(
                            out=cta.t[:], in0=P[2].t[:, 0:W], scalar1=0.0, scalar2=None, op0=ALU.min),
                            reads=[P[2]], writes=[cta])
                        kb.do("scalar", lambda s: s.activation(out=cta.t[:], in_=cta.t[:], func=AF.Exp),
                              reads=[cta], writes=[cta])
                        kb.do("vector", lambda v: v.tensor_scalar(
                            out=ctb.t[:], in0=P[2].t[:, 0:W], scalar1=0.0, scalar2=None, op0=ALU.max),
                            reads=[P[2]], writes=[ctb])
                        kb.do("vector", lambda v: v.tensor_add(out=cta.t[:], in0=cta.t[:], in1=ctb.t[:]),
                              reads=[ctb, cta], writes=[cta])
                        kb.do("vector", lambda v, mc=mc: v.tensor_scalar(
                            out=qp.t[:, mc * 512:(mc + 1) * 512], in0=cta.t[:], scalar1=0.0625,
                            scalar2=None, op0=ALU.mult),
                            reads=[cta], writes=[qp])
                    for mc in range(2):
                        kb.do("tensor", lambda t, h=h, mc=mc: t.matmul(
                            P[0].t[0:64, 0:W], kv_sb[mc].t[:, h * 64:(h + 1) * 64],
                            qp.t[:, mc * 512:(mc + 1) * 512], start=(mc == 0), stop=(mc == 1)),
                            reads=[kv_sb[mc], qp], writes=[P[0]], inc=1)
                        kb.do("tensor", lambda t, h=h, mc=mc: t.matmul(
                            P[1].t[0:1, 0:W], ks_sb.t[:, mc * 16 + h:mc * 16 + h + 1],
                            qp.t[:, mc * 512:(mc + 1) * 512], start=(mc == 0), stop=(mc == 1)),
                            reads=[ks_sb, qp], writes=[P[1]], inc=1)
                    kb.do("vector", lambda v: v.tensor_scalar(
                        out=rowt.t[:], in0=P[1].t[0:1, 0:W], scalar1=1e-6, scalar2=None, op0=ALU.max),
                        reads=[P[1]], writes=[rowt])
                    kb.do("vector", lambda v: v.reciprocal(out=rowt.t[:], in_=rowt.t[:]),
                          reads=[rowt], writes=[rowt])
                    kb.do("tensor", lambda t: t.matmul(P[1].t[0:64, 0:W], c["ones_row"].t[0:1, 0:64],
                                                       rowt.t[:], start=True, stop=True),
                          reads=[c["ones_row"], rowt], writes=[P[1]])
                    kb.do("scalar", lambda s: s.copy(out=nsb.t[:], in_=P[0].t[0:64, 0:W]),
                          reads=[P[0]], writes=[nsb])
                    kb.do("vector", lambda v, h=h, hp=hp, ho=ho, col=col: v.tensor_mul(
                        out=att.t[ho:ho + 64, hp * NT + col:hp * NT + col + W], in0=nsb.t[:],
                        in1=P[1].t[0:64, 0:W]), reads=[nsb, P[1]], writes=[att])
            for ci in range(DC):
                dma_in(kb, wstg, wo_d[:, ci * 1024:(ci + 1) * 1024])
                cast(kb, wqb, wqb.t[:, ci * 1024:(ci + 1) * 1024], wstg, wstg.t[:])
            dma_in(kb, bqc, bo_d[:, :])
            for oo in range(DC):
                for tc in range(NT // W):
                    col = tc * W
                    for ci in range(DC):
                        kb.do("tensor", lambda t, ci=ci, oo=oo, col=col: t.matmul(
                            P[2].t[:, 0:W], wqb.t[:, ci * 1024 + oo * 128:ci * 1024 + oo * 128 + 128],
                            att.t[:, ci * NT + col:ci * NT + col + W], start=(ci == 0), stop=(ci == DC - 1)),
                            reads=[wqb, att], writes=[P[2]], inc=1)
                    kb.do("vector", lambda v, oo=oo: v.tensor_scalar(
                        out=cta.t[:], in0=P[2].t[:, 0:W], scalar1=bqc.t[:, oo:oo + 1], scalar2=None,
                        op0=ALU.add), reads=[P[2], bqc], writes=[cta])
                    dma_out(kb, s2sp_d[oo * 128:(oo + 1) * 128, col:col + W], cta)
            kb.barrier()

        s3s = ExitStack()
        with s3s:
            r_all = [sb(f"r{i}", [128, NT], BF16, s3s) for i in range(DC)]
            ka_all = [sb(f"ka{i}", [128, NT], BF16, s3s) for i in range(DC)]
            wrb = sb("wrbE", [128, DC * 1024], BF16, s3s)
            wstg = sb("wstgE", [128, 1024], F32, s3s)
            tfa = sb("tfaE", [128, 512], F32, s3s)
            tfb = sb("tfbE", [128, 512], F32, s3s)
            xrr = sb("xrrE", [128, 512], F32, s3s)
            murc = sb("murc", [128, 8], F32, s3s)
            mukc = sb("mukc", [128, 8], F32, s3s)
            pcol = sb("pcol", [128, 1], F32, s3s)
            prow = sb("prow", [1, 1], F32, s3s)
            dd = sb("dd", [128, NT], F32, s3s)
            colt = sb("colt", [128, 1], F32, s3s)
            colt2 = sb("colt2", [128, 1], F32, s3s)
            dma_in(kb, murc, mur_d[:, :])
            dma_in(kb, mukc, muk_d[:, :])
            dma_in(kb, prow, pm_d[:, :])
            kb.do("tensor", lambda t: t.matmul(P[2].t[:, 0:1], c["ones_row"].t[:], prow.t[:],
                                               start=True, stop=True),
                  reads=[c["ones_row"], prow], writes=[P[2]])
            kb.do("vector", lambda v: v.tensor_copy(out=pcol.t[:], in_=P[2].t[:, 0:1]),
                  reads=[P[2]], writes=[pcol])
            def build_shift(mu, xall):
                for ci in range(DC):
                    dst = xall[ci]
                    kb.do("vector", lambda v, ci=ci: v.tensor_scalar(
                        out=colt.t[:], in0=xn_last.t[:, ci:ci + 1], scalar1=pcol.t[:, 0:1],
                        scalar2=None, op0=ALU.mult), reads=[xn_last, pcol], writes=[colt])
                    kb.do("vector", lambda v, ci=ci: v.tensor_tensor(
                        out=colt2.t[:], in0=xn[ci].t[:, 0:1], in1=colt.t[:], op=ALU.subtract),
                        reads=[xn[ci], colt], writes=[colt2])
                    kb.do("vector", lambda v, ci=ci, mu=mu, dst=dst: v.scalar_tensor_tensor(
                        out=dst.t[:, 0:1], in0=colt2.t[:], scalar=mu.t[:, ci:ci + 1], in1=colt.t[:],
                        op0=ALU.mult, op1=ALU.add), reads=[colt2, mu, colt], writes=[dst])
                    kb.do("vector", lambda v, ci=ci: v.tensor_tensor(
                        out=dd.t[:, 1:NT], in0=xn[ci].t[:, 1:NT], in1=xn[ci].t[:, 0:NT - 1],
                        op=ALU.subtract), reads=[xn[ci]], writes=[dd])
                    kb.do("vector", lambda v, ci=ci, mu=mu, dst=dst: v.scalar_tensor_tensor(
                        out=dst.t[:, 1:NT], in0=dd.t[:, 1:NT], scalar=mu.t[:, ci:ci + 1],
                        in1=xn[ci].t[:, 0:NT - 1], op0=ALU.mult, op1=ALU.add),
                        reads=[dd, mu, xn[ci]], writes=[dst])

            sE1 = ExitStack()
            sE1.__enter__()
            xr_all = [sb(f"xr{i}", [128, NT], BF16, sE1) for i in range(DC)]
            build_shift(murc, xr_all)
            for ci in range(DC):
                dma_in(kb, wstg, wr_d[:, ci * 1024:(ci + 1) * 1024])
                cast(kb, wrb, wrb.t[:, ci * 1024:(ci + 1) * 1024], wstg, wstg.t[:])
            for oo in range(DC):
                for tc in range(NT // W):
                    col = tc * W
                    for ci in range(DC):
                        kb.do("tensor", lambda t, ci=ci, oo=oo, col=col: t.matmul(
                            P[2].t[:, 0:W], wrb.t[:, ci * 1024 + oo * 128:ci * 1024 + oo * 128 + 128],
                            xr_all[ci].t[:, col:col + W], start=(ci == 0), stop=(ci == DC - 1)),
                            reads=[wrb, xr_all[ci]], writes=[P[2]], inc=1)
                    kb.do("scalar", lambda s, oo=oo, col=col: s.activation(
                        out=r_all[oo].t[:, col:col + W], in_=P[2].t[:, 0:W], func=AF.Sigmoid),
                        reads=[P[2]], writes=[r_all[oo]])
            kb.barrier()
            sE1.__exit__(None, None, None)
            sE2 = ExitStack()
            sE2.__enter__()
            xk_all = [sb(f"xk{i}", [128, NT], BF16, sE2) for i in range(DC)]
            build_shift(mukc, xk_all)
            for ci in range(DC):
                dma_in(kb, wstg, wk2_d[:, ci * 1024:(ci + 1) * 1024])
                cast(kb, wrb, wrb.t[:, ci * 1024:(ci + 1) * 1024], wstg, wstg.t[:])
            for oo in range(DC):
                for tc in range(NT // W):
                    col = tc * W
                    for ci in range(DC):
                        kb.do("tensor", lambda t, ci=ci, oo=oo, col=col: t.matmul(
                            P[2].t[:, 0:W], wrb.t[:, ci * 1024 + oo * 128:ci * 1024 + oo * 128 + 128],
                            xk_all[ci].t[:, col:col + W], start=(ci == 0), stop=(ci == DC - 1)),
                            reads=[wrb, xk_all[ci]], writes=[P[2]], inc=1)
                    kb.do("vector", lambda v: v.tensor_scalar(
                        out=tfa.t[:], in0=P[2].t[:, 0:W], scalar1=0.0, scalar2=None, op0=ALU.max),
                        reads=[P[2]], writes=[tfa])
                    kb.do("scalar", lambda s, oo=oo, col=col: s.activation(
                        out=ka_all[oo].t[:, col:col + W], in_=tfa.t[:], func=AF.Square),
                        reads=[tfa], writes=[ka_all[oo]])
            kb.barrier()
            sE2.__exit__(None, None, None)
            for ci in range(DC):
                dma_in(kb, wstg, wv2_d[:, ci * 1024:(ci + 1) * 1024])
                cast(kb, wrb, wrb.t[:, ci * 1024:(ci + 1) * 1024], wstg, wstg.t[:])
            for oo in range(DC):
                for tc in range(NT // W):
                    col = tc * W
                    for ci in range(DC):
                        kb.do("tensor", lambda t, ci=ci, oo=oo, col=col: t.matmul(
                            P[2].t[:, 0:W], wrb.t[:, ci * 1024 + oo * 128:ci * 1024 + oo * 128 + 128],
                            ka_all[ci].t[:, col:col + W], start=(ci == 0), stop=(ci == DC - 1)),
                            reads=[wrb, ka_all[ci]], writes=[P[2]], inc=1)
                    kb.do("vector", lambda v, oo=oo, col=col: v.tensor_mul(
                        out=tfa.t[:], in0=r_all[oo].t[:, col:col + W], in1=P[2].t[:, 0:W]),
                        reads=[r_all[oo], P[2]], writes=[tfa])
                    kb.do("scalar", lambda s: s.activation(out=tfa.t[:], in_=tfa.t[:], func=AF.Silu),
                          reads=[tfa], writes=[tfa])
                    dma_in(kb, tfb, s2sp_d[oo * 128:(oo + 1) * 128, col:col + W])
                    dma_in(kb, xrr, x_d[oo * 128:(oo + 1) * 128, col:col + W])
                    kb.do("vector", lambda v: v.tensor_mul(out=tfa.t[:], in0=tfa.t[:], in1=tfb.t[:]),
                          reads=[tfa, tfb], writes=[tfa])
                    kb.do("vector", lambda v: v.tensor_add(out=xrr.t[:], in0=xrr.t[:], in1=tfa.t[:]),
                          reads=[tfa, xrr], writes=[xrr])
                    dma_out(kb, xo_d[oo * 128:(oo + 1) * 128, col:col + W], xrr)
        kb.finalize(nc, es)
    return nc


def _prep_l2(inputs):
    T = lambda a: np.ascontiguousarray(np.asarray(a).T)
    A = lambda a: np.ascontiguousarray(np.asarray(a))
    return {
        "wq": _chunk_w(A(inputs["L2_attn_Wq"]), 1024),
        "wk": _chunk_w(A(inputs["L2_attn_Wk"]), 1024),
        "wv": _chunk_w(A(inputs["L2_attn_Wv"]), 1024),
        "wo": _chunk_w(A(inputs["L2_attn_Wo"]), 1024),
        "bq": np.ascontiguousarray(np.asarray(inputs["L2_attn_bq"]).reshape(8, 128).T),
        "bk": np.ascontiguousarray(np.asarray(inputs["L2_attn_bk"]).reshape(8, 128).T),
        "bv": np.ascontiguousarray(np.asarray(inputs["L2_attn_bv"]).reshape(1, 1024)),
        "bo": np.ascontiguousarray(np.asarray(inputs["L2_attn_bo"]).reshape(8, 128).T),
        "rm": A(inputs["L2_attn_random_matrix"]),
        "wr": _chunk_w(T(inputs["L2_ffn_W_r"]), 1024),
        "wk2": _chunk_w(T(inputs["L2_ffn_W_k"]), 1024),
        "wv2": _chunk_w(T(inputs["L2_ffn_W_v"]), 1024),
        "mur": np.ascontiguousarray(np.asarray(inputs["L2_ffn_mu_r"]).reshape(8, 128).T),
        "muk": np.ascontiguousarray(np.asarray(inputs["L2_ffn_mu_k"]).reshape(8, 128).T),
    }


def run_l2(Xfm, inputs):
    pre = _prep_l2(inputs)
    nc = build_l2()
    in_maps = []
    for core in range(8):
        half = core % 2
        m = {"x": _perm_x(Xfm, core), "pm": np.full((1, 1), float(half), np.float32)}
        m.update(pre)
        in_maps.append(m)
    res = run_bass_kernel_spmd(nc, in_maps, core_ids=list(range(8)))
    X3 = np.empty((B, D, S), np.float32)
    for core in range(8):
        b, half = core // 2, core % 2
        X3[b][:, half * NT:(half + 1) * NT] = res.results[core]["xo"]
    return X3



def build_l3():
    TT = 1024
    nc = bass.Bass("TRN2", target_bir_lowering=False, debug=False)
    xa_d = nc.dram_tensor("xa", [4, D, NT], F32, kind="ExternalInput").ap()
    xo_in_d = nc.dram_tensor("xown", [D, NT], F32, kind="ExternalInput").ap()
    dcol_d = nc.dram_tensor("dcol", [128, DC * 16], F32, kind="ExternalInput").ap()
    acol_d = nc.dram_tensor("acol", [128, DC * 16], F32, kind="ExternalInput").ap()
    bcol_d = nc.dram_tensor("bcol", [128, DC * 16], F32, kind="ExternalInput").ap()
    gcol_d = nc.dram_tensor("gcol", [128, DC * 16], F32, kind="ExternalInput").ap()
    om_d = nc.dram_tensor("om", [128, 8], F32, kind="ExternalInput").ap()
    msel_d = nc.dram_tensor("msel", [1, 16], F32, kind="ExternalInput").ap()
    w1_d = nc.dram_tensor("w1", [128, DC * 4096], F32, kind="ExternalInput").ap()
    b1_d = nc.dram_tensor("b1", [128, 32], F32, kind="ExternalInput").ap()
    w2_d = nc.dram_tensor("w2", [128, 32 * 1024], F32, kind="ExternalInput").ap()
    b2_d = nc.dram_tensor("b2", [128, 8], F32, kind="ExternalInput").ap()
    w1t_d = nc.dram_tensor("w1t", [128, DC * 4096], BF16, kind="Internal").ap()
    w2t_d = nc.dram_tensor("w2t", [128, 32 * 1024], BF16, kind="Internal").ap()
    xo_d = nc.dram_tensor("xo", [D, NT], F32, kind="ExternalOutput").ap()

    kb = KB()
    es = ExitStack()
    with es:
        def sb(n, sh, dt, stack=None):
            st = stack or es
            return Buf(st.enter_context(nc.sbuf_tensor(n, sh, dt)), n)
        c = mk_consts(nc, es, kb)
        P = [Buf(es.enter_context(nc.psum_tensor(f"P{i}", [128, 512], F32)), f"P{i}") for i in range(4)]
        sc = mk_ln_scratch(nc, es, "ln")
        xn_own = [sb(f"xno{i}", [128, NT], BF16) for i in range(DC)]
        s3buf = [sb(f"s3_{i}", [128, NT], BF16) for i in range(DC)]
        omc = sb("omc", [128, 8], F32)
        ccols = sb("ccols", [128, DC * 4], F32)
        dma_in(kb, omc, om_d[:, :])

        s0 = ExitStack()
        with s0:
            dk = sb("dk", [128, DC * 16], F32, s0)
            ak = sb("ak", [128, DC * 16], F32, s0)
            bk = sb("bk", [128, DC * 16], F32, s0)
            gk = sb("gk", [128, DC * 16], F32, s0)
            base = sb("base", [128, DC * 16], F32, s0)
            qq = sb("qq", [128, DC * 16], F32, s0)
            kcol = sb("kcol", [128, DC * 4], F32, s0)
            mrow = sb("mrow", [1, 16], F32, s0)
            mB = sb("mB", [128, 16], F32, s0)
            tmp4 = sb("tmp4", [128, 4], F32, s0)
            dma_in(kb, dk, dcol_d[:, :])
            dma_in(kb, ak, acol_d[:, :])
            dma_in(kb, bk, bcol_d[:, :])
            dma_in(kb, gk, gcol_d[:, :])
            kb.do("scalar", lambda s: s.activation(out=dk.t[:], in_=dk.t[:], func=AF.Sigmoid),
                  reads=[dk], writes=[dk])
            kb.do("scalar", lambda s: s.activation(out=ak.t[:], in_=ak.t[:], func=AF.Sigmoid),
                  reads=[ak], writes=[ak])
            kb.do("vector", lambda v: v.tensor_mul(out=qq.t[:], in0=dk.t[:], in1=ak.t[:]),
                  reads=[dk, ak], writes=[qq])
            kb.do("vector", lambda v: v.tensor_scalar(out=qq.t[:], in0=qq.t[:], scalar1=-1.0,
                                                      scalar2=1.0, op0=ALU.mult, op1=ALU.add),
                  reads=[qq], writes=[qq])
            kb.do("vector", lambda v: v.tensor_mul(out=base.t[:], in0=dk.t[:], in1=bk.t[:]),
                  reads=[dk, bk], writes=[base])
            kb.do("vector", lambda v: v.tensor_mul(out=base.t[:], in0=base.t[:], in1=gk.t[:]),
                  reads=[gk, base], writes=[base])
            kb.do("vector", lambda v: v.tensor_scalar(out=base.t[:], in0=base.t[:], scalar1=0.25,
                                                      scalar2=None, op0=ALU.mult),
                  reads=[base], writes=[base])
            for l in range(4):
                for ci in range(DC):
                    kb.do("vector", lambda v, l=l, ci=ci: v.reduce_sum(
                        kcol.t[:, ci * 4 + l:ci * 4 + l + 1], base.t[:, ci * 16:(ci + 1) * 16], axis=AX.X),
                        reads=[base], writes=[kcol])
                if l < 3:
                    kb.do("vector", lambda v: v.tensor_mul(out=base.t[:], in0=base.t[:], in1=qq.t[:]),
                          reads=[qq, base], writes=[base])
            dma_in(kb, mrow, msel_d[:, :])
            kb.do("tensor", lambda t: t.matmul(P[3].t[:, 0:16], c["ones_row"].t[:], mrow.t[:],
                                               start=True, stop=True),
                  reads=[c["ones_row"], mrow], writes=[P[3]])
            kb.do("vector", lambda v: v.tensor_copy(out=mB.t[:], in_=P[3].t[:, 0:16]),
                  reads=[P[3]], writes=[mB])
            for ci in range(DC):
                for tau in range(4):
                    kb.do("vector", lambda v, ci=ci, tau=tau: v.tensor_mul(
                        out=tmp4.t[:], in0=kcol.t[:, ci * 4:(ci + 1) * 4], in1=mB.t[:, tau * 4:(tau + 1) * 4]),
                        reads=[kcol, mB], writes=[tmp4])
                    kb.do("vector", lambda v, ci=ci, tau=tau: v.reduce_sum(
                        ccols.t[:, ci * 4 + tau:ci * 4 + tau + 1], tmp4.t[:], axis=AX.X),
                        reads=[tmp4], writes=[ccols])
            kb.barrier()

        def xo_ap(ci, col):
            return xo_in_d[ci * 128:(ci + 1) * 128, col:col + W]

        emit_ln(kb, c, xo_ap, xn_own, 0, NT, P, sc, out_col0=0)

        s1 = ExitStack()
        with s1:
            xtile = sb("xtile", [128, W], F32, s1)
            xtile2 = sb("xtile2", [128, W], F32, s1)
            for tc in range(NT // W):
                col = tc * W
                for tau in range(4):
                    for ci in range(DC):
                        xa = xtile if ci % 2 == 0 else xtile2
                        dma_in(kb, xa, xa_d[tau, ci * 128:(ci + 1) * 128, col:col + W])
                        kb.do("tensor", lambda t, ci=ci, xa=xa: t.matmul(
                            P[0].t[0:1, 0:W], c["ones_col"].t[:], xa.t[:],
                            start=(ci == 0), stop=(ci == DC - 1)),
                            reads=[c["ones_col"], xa], writes=[P[0]], inc=1)
                        kb.do("scalar", lambda s, xa=xa: s.activation(out=sc["xsq"].t[:], in_=xa.t[:],
                                                                      func=AF.Square),
                              reads=[xa], writes=[sc["xsq"]])
                        kb.do("tensor", lambda t, ci=ci: t.matmul(
                            P[1].t[0:1, 0:W], c["ones_col"].t[:], sc["xsq"].t[:],
                            start=(ci == 0), stop=(ci == DC - 1)),
                            reads=[c["ones_col"], sc["xsq"]], writes=[P[1]], inc=1)
                    kb.do("scalar", lambda s: s.activation(out=sc["row1"].t[:], in_=P[0].t[0:1, 0:W],
                                                           func=AF.Copy, scale=1.0 / D),
                          reads=[P[0]], writes=[sc["row1"]])
                    kb.do("scalar", lambda s: s.activation(out=sc["row2"].t[:], in_=P[1].t[0:1, 0:W],
                                                           func=AF.Copy, scale=1.0 / D),
                          reads=[P[1]], writes=[sc["row2"]])
                    kb.do("vector", lambda v: v.tensor_mul(out=sc["row3"].t[:], in0=sc["row1"].t[:],
                                                           in1=sc["row1"].t[:]),
                          reads=[sc["row1"]], writes=[sc["row3"]])
                    kb.do("vector", lambda v: v.tensor_sub(out=sc["row2"].t[:], in0=sc["row2"].t[:],
                                                           in1=sc["row3"].t[:]),
                          reads=[sc["row2"], sc["row3"]], writes=[sc["row2"]])
                    kb.do("vector", lambda v: v.tensor_scalar(out=sc["row2"].t[:], in0=sc["row2"].t[:],
                                                              scalar1=1e-5, scalar2=None, op0=ALU.add),
                          reads=[sc["row2"]], writes=[sc["row2"]])
                    kb.do("scalar", lambda s: s.activation(out=sc["row2"].t[:], in_=sc["row2"].t[:],
                                                           func=AF.Sqrt),
                          reads=[sc["row2"]], writes=[sc["row2"]])
                    kb.do("vector", lambda v: v.reciprocal(out=sc["row2"].t[:], in_=sc["row2"].t[:]),
                          reads=[sc["row2"]], writes=[sc["row2"]])
                    kb.do("tensor", lambda t: t.matmul(P[0].t[:, 0:W], c["ones_row"].t[:], sc["row1"].t[:],
                                                       start=True, stop=True),
                          reads=[c["ones_row"], sc["row1"]], writes=[P[0]])
                    kb.do("tensor", lambda t: t.matmul(P[1].t[:, 0:W], c["ones_row"].t[:], sc["row2"].t[:],
                                                       start=True, stop=True),
                          reads=[c["ones_row"], sc["row2"]], writes=[P[1]])
                    for ci in range(DC):
                        xa = xtile if ci % 2 == 0 else xtile2
                        dma_in(kb, xa, xa_d[tau, ci * 128:(ci + 1) * 128, col:col + W])
                        kb.do("vector", lambda v, xa=xa: v.tensor_sub(out=xa.t[:], in0=xa.t[:],
                                                                      in1=P[0].t[:, 0:W]),
                              reads=[xa, P[0]], writes=[xa])
                        kb.do("vector", lambda v, xa=xa: v.tensor_mul(out=xa.t[:], in0=xa.t[:],
                                                                      in1=P[1].t[:, 0:W]),
                              reads=[xa, P[1]], writes=[xa])
                        if tau == 0:
                            kb.do("vector", lambda v, ci=ci, tau=tau, xa=xa, col=col: v.tensor_scalar(
                                out=s3buf[ci].t[:, col:col + W], in0=xa.t[:],
                                scalar1=ccols.t[:, ci * 4 + tau:ci * 4 + tau + 1], scalar2=None,
                                op0=ALU.mult), reads=[xa, ccols], writes=[s3buf[ci]])
                        else:
                            kb.do("vector", lambda v, ci=ci, tau=tau, xa=xa, col=col: v.scalar_tensor_tensor(
                                out=s3buf[ci].t[:, col:col + W], in0=xa.t[:],
                                scalar=ccols.t[:, ci * 4 + tau:ci * 4 + tau + 1],
                                in1=s3buf[ci].t[:, col:col + W], op0=ALU.mult, op1=ALU.add),
                                reads=[xa, ccols, s3buf[ci]], writes=[s3buf[ci]])
                for ci in range(DC):
                    kb.do("vector", lambda v, ci=ci, col=col: v.scalar_tensor_tensor(
                        out=sc["x1"].t[:], in0=xn_own[ci].t[:, col:col + W],
                        scalar=omc.t[:, ci:ci + 1], in1=s3buf[ci].t[:, col:col + W],
                        op0=ALU.mult, op1=ALU.add),
                        reads=[xn_own[ci], omc, s3buf[ci]], writes=[sc["x1"]])
                    kb.do("scalar", lambda s, ci=ci, col=col: s.activation(
                        out=s3buf[ci].t[:, col:col + W], in_=sc["x1"].t[:], func=AF.Silu),
                        reads=[sc["x1"]], writes=[s3buf[ci]])
            kb.barrier()

        s2t = ExitStack()
        with s2t:
            wt = sb("wtT", [128, 512], F32, s2t)
            wt2 = sb("wtT2", [128, 512], F32, s2t)
            colsum = sb("colsumT", [128, 1], F32, s2t)
            rcol = sb("rcolT", [128, 1], F32, s2t)
            one11 = sb("one11", [1, 1], F32, s2t)
            wob = sb("wobT", [128, 512], BF16, s2t)
            for wd, wtd, ncols in ((w1_d, w1t_d, DC * 4096), (w2_d, w2t_d, 32 * 1024)):
                ntile = ncols // 512
                for i in range(ntile):
                    wx = wt if i % 2 == 0 else wt2
                    dma_in(kb, wx, wd[:, i * 512:(i + 1) * 512])
                    kb.do("vector", lambda v, wx=wx: v.reduce_sum(
                        sc["x1"].t[:, 0:1], wx.t[:], axis=AX.X, apply_absolute_value=True),
                        reads=[wx], writes=[sc["x1"]])
                    if i == 0:
                        kb.do("vector", lambda v: v.tensor_copy(out=colsum.t[:], in_=sc["x1"].t[:, 0:1]),
                              reads=[sc["x1"]], writes=[colsum])
                    else:
                        kb.do("vector", lambda v: v.tensor_add(out=colsum.t[:], in0=colsum.t[:],
                                                               in1=sc["x1"].t[:, 0:1]),
                              reads=[sc["x1"], colsum], writes=[colsum])
                kb.do("tensor", lambda t: t.matmul(P[3].t[0:1, 0:1], colsum.t[:, 0:1],
                                                   c["ones_col"].t[:, 0:1], start=True, stop=True),
                      reads=[colsum, c["ones_col"]], writes=[P[3]])
                kb.do("vector", lambda v, ncols=ncols: v.tensor_scalar(
                    out=one11.t[:], in0=P[3].t[0:1, 0:1], scalar1=1.0 / (128.0 * ncols),
                    scalar2=1e-8, op0=ALU.mult, op1=ALU.max),
                    reads=[P[3]], writes=[one11])
                kb.do("vector", lambda v: v.reciprocal(out=one11.t[:], in_=one11.t[:]),
                      reads=[one11], writes=[one11])
                kb.do("tensor", lambda t: t.matmul(P[3].t[:, 0:1], c["ones_row"].t[:], one11.t[:],
                                                   start=True, stop=True),
                      reads=[c["ones_row"], one11], writes=[P[3]])
                kb.do("vector", lambda v: v.tensor_copy(out=rcol.t[:], in_=P[3].t[:, 0:1]),
                      reads=[P[3]], writes=[rcol])
                for i in range(ntile):
                    wx = wt if i % 2 == 0 else wt2
                    dma_in(kb, wx, wd[:, i * 512:(i + 1) * 512])
                    kb.do("vector", lambda v, wx=wx: v.tensor_scalar(
                        out=wx.t[:], in0=wx.t[:], scalar1=rcol.t[:, 0:1], scalar2=MAGIC,
                        op0=ALU.mult, op1=ALU.add), reads=[wx, rcol], writes=[wx])
                    kb.do("vector", lambda v, wx=wx: v.tensor_scalar(
                        out=wx.t[:], in0=wx.t[:], scalar1=MAGIC, scalar2=1.0,
                        op0=ALU.subtract, op1=ALU.min), reads=[wx], writes=[wx])
                    kb.do("vector", lambda v, wx=wx: v.tensor_scalar(
                        out=wob.t[:], in0=wx.t[:], scalar1=-1.0, scalar2=None, op0=ALU.max),
                        reads=[wx], writes=[wob])
                    dma_out(kb, wtd[:, i * 512:(i + 1) * 512], wob)
            kb.barrier()

        s3f = ExitStack()
        with s3f:
            h_sb = sb("h_sb", [128, 32 * TT], BF16, s3f)
            wbf = sb("wbfL3", [128, 1024], BF16, s3f)
            wbf2 = sb("wbfL3b", [128, 1024], BF16, s3f)
            b1c = sb("b1c", [128, 32], F32, s3f)
            b2c = sb("b2c", [128, 8], F32, s3f)
            tfa = sb("tfaL3", [128, W], F32, s3f)
            xrr = sb("xrrL3", [128, W], F32, s3f)
            dma_in(kb, b1c, b1_d[:, :])
            dma_in(kb, b2c, b2_d[:, :])
            w1t_r = w1t_d.rearrange("p (k f) -> p k f", f=4096)
            w2t_r = w2t_d.rearrange("p (k f) -> p k f", f=1024)
            NN = TT // W
            for tc in range(NT // TT):
                tc0 = tc * TT
                for fo in range(32):
                    wx = wbf if fo % 2 == 0 else wbf2
                    pbase = 2 if fo % 2 == 0 else 0
                    kb.do("sync", lambda e, fo=fo, wx=wx: e.dma_start(
                        out=wx.t[:].rearrange("p (k f) -> p k f", f=128),
                        in_=w1t_r[:, :, fo * 128:fo * 128 + 128]),
                        writes=[wx], inc=16, sem_key="ld_" + wx.name)
                    for nn in range(NN):
                        for ci in range(DC):
                            kb.do("tensor", lambda t, ci=ci, nn=nn, tc0=tc0, wx=wx, pbase=pbase: t.matmul(
                                P[pbase + nn].t[:, 0:W], wx.t[:, ci * 128:(ci + 1) * 128],
                                xn_own[ci].t[:, tc0 + nn * W:tc0 + nn * W + W],
                                start=(ci == 0), stop=(ci == DC - 1)),
                                reads=[wx, xn_own[ci]], writes=[P[pbase + nn]], inc=1)
                        kb.do("scalar", lambda s, fo=fo, nn=nn, pbase=pbase: s.activation(
                            out=h_sb.t[:, fo * TT + nn * W:fo * TT + nn * W + W],
                            in_=P[pbase + nn].t[:, 0:W], func=AF.Gelu, bias=b1c.t[:, fo:fo + 1]),
                            reads=[P[pbase + nn], b1c], writes=[h_sb])
                for oo in range(DC):
                    pbase = 2 if oo % 2 == 0 else 0
                    for fig in range(DC):
                        wx = wbf if fig % 2 == 0 else wbf2
                        kb.do("sync", lambda e, fig=fig, oo=oo, wx=wx: e.dma_start(
                            out=wx.t[:, 0:512].rearrange("p (k f) -> p k f", f=128),
                            in_=w2t_r[:, fig * 4:fig * 4 + 4, oo * 128:oo * 128 + 128]),
                            writes=[wx], inc=16, sem_key="ld_" + wx.name)
                        for nn in range(NN):
                            for k in range(4):
                                fi = fig * 4 + k
                                kb.do("tensor", lambda t, k=k, nn=nn, fi=fi, wx=wx, pbase=pbase: t.matmul(
                                    P[pbase + nn].t[:, 0:W], wx.t[:, k * 128:(k + 1) * 128],
                                    h_sb.t[:, fi * TT + nn * W:fi * TT + nn * W + W],
                                    start=(fi == 0), stop=(fi == 31)),
                                    reads=[wx, h_sb], writes=[P[pbase + nn]], inc=1)
                    for nn in range(NN):
                        col = tc0 + nn * W
                        kb.do("scalar", lambda s, oo=oo, nn=nn, pbase=pbase: s.activation(
                            out=tfa.t[:], in_=P[pbase + nn].t[:, 0:W], func=AF.Silu,
                            bias=b2c.t[:, oo:oo + 1]),
                            reads=[P[pbase + nn], b2c], writes=[tfa])
                        dma_in(kb, xrr, xo_in_d[oo * 128:(oo + 1) * 128, col:col + W])
                        kb.do("vector", lambda v, oo=oo, col=col: v.tensor_mul(
                            out=tfa.t[:], in0=tfa.t[:], in1=s3buf[oo].t[:, col:col + W]),
                            reads=[tfa, s3buf[oo]], writes=[tfa])
                        kb.do("vector", lambda v: v.tensor_add(out=xrr.t[:], in0=xrr.t[:], in1=tfa.t[:]),
                              reads=[tfa, xrr], writes=[xrr])
                        dma_out(kb, xo_d[oo * 128:(oo + 1) * 128, col:col + W], xrr)

        kb.finalize(nc, es)
    return nc


def _prep_l1(inputs):
    T = lambda a: np.ascontiguousarray(np.asarray(a).T)
    d = {
        "wq": _chunk_w(T(inputs["L1_attn_Wq"]), 1024),
        "wk": _chunk_w(T(inputs["L1_attn_Wk"]), 1024),
        "wv": _chunk_w(T(inputs["L1_attn_Wv"]), 1024),
        "wo": _chunk_w(T(inputs["L1_attn_Wo"]), 1024),
        "bq": np.ascontiguousarray(np.asarray(inputs["L1_attn_bq"]).reshape(8, 128).T),
        "bk": np.ascontiguousarray(np.asarray(inputs["L1_attn_bk"]).reshape(1, 1024)),
        "bv": np.ascontiguousarray(np.asarray(inputs["L1_attn_bv"]).reshape(1, 1024)),
        "bo": np.ascontiguousarray(np.asarray(inputs["L1_attn_bo"]).reshape(8, 128).T),
        "stdp": np.ascontiguousarray(np.asarray(inputs["L1_attn_stdp"]).reshape(1, 16)),
        "rw": _chunk_w(T(inputs["L1_ffn_router_weight"]), 16),
        "eb": np.ascontiguousarray(np.asarray(inputs["L1_ffn_e_bias"]).reshape(1, 16)),
        "gw": np.stack([_chunk_w(T(np.asarray(inputs["L1_ffn_gate_w"])[e]), 512) for e in range(16)]),
        "uw": np.stack([_chunk_w(T(np.asarray(inputs["L1_ffn_up_w"])[e]), 512) for e in range(16)]),
        "dw": np.stack([_chunk_w(T(np.asarray(inputs["L1_ffn_down_w"])[e]), 1024) for e in range(16)]),
        "shg": _chunk_w(T(inputs["L1_ffn_sh_gate_w"]), 1024),
        "shu": _chunk_w(T(inputs["L1_ffn_sh_up_w"]), 1024),
        "shd": _chunk_w(T(inputs["L1_ffn_sh_down_w"]), 1024),
    }
    return d


def _perm_x(Xfm, core):
    b, half = core // 2, core % 2
    xb = Xfm[b]
    if half == 0:
        return np.ascontiguousarray(xb)
    return np.ascontiguousarray(np.concatenate([xb[:, NT:], xb[:, :NT]], axis=1))


def run_l1(Xfm, inputs):
    pre = _prep_l1(inputs)
    nc = build_l1()
    in_maps = []
    for core in range(8):
        m = {"x": _perm_x(Xfm, core)}
        m.update(pre)
        in_maps.append(m)
    res = run_bass_kernel_spmd(nc, in_maps, core_ids=list(range(8)))
    X2 = np.empty((B, D, S), np.float32)
    for core in range(8):
        b, half = core // 2, core % 2
        X2[b][:, half * NT:(half + 1) * NT] = res.results[core]["xo"]
    return X2


def _prep_l3(inputs):
    def colchunk(a):
        a = np.asarray(a).reshape(1024, -1)
        k = a.shape[1]
        return np.ascontiguousarray(a.reshape(8, 128, k).transpose(1, 0, 2).reshape(128, 8 * k))
    return {
        "dcol": colchunk(inputs["L3_attn_delta"]),
        "acol": colchunk(inputs["L3_attn_alpha"]),
        "bcol": colchunk(inputs["L3_attn_beta"]),
        "gcol": colchunk(inputs["L3_attn_gamma"]),
        "om": np.ascontiguousarray(np.asarray(inputs["L3_attn_omega"]).reshape(8, 128).T),
        "w1": _chunk_w(np.ascontiguousarray(np.asarray(inputs["L3_ffn_W1"]).T), 4096),
        "b1": np.ascontiguousarray(np.asarray(inputs["L3_ffn_b1"]).reshape(32, 128).T),
        "w2": _chunk_w(np.ascontiguousarray(np.asarray(inputs["L3_ffn_W2"]).T), 1024),
        "b2": np.ascontiguousarray(np.asarray(inputs["L3_ffn_b2"]).reshape(8, 128).T),
    }


def run_l3(Xfm, inputs):
    pre = _prep_l3(inputs)
    nc = build_l3()
    in_maps = []
    for core in range(8):
        b, half = core // 2, core % 2
        sl = slice(half * NT, (half + 1) * NT)
        msel = np.zeros((1, 16), np.float32)
        for tau in range(4):
            l = b - tau
            if 0 <= l < 4:
                msel[0, tau * 4 + l] = 1.0
        m = {"xa": np.ascontiguousarray(Xfm[:, :, sl]),
             "xown": np.ascontiguousarray(Xfm[b][:, sl]),
             "msel": msel}
        m.update(pre)
        in_maps.append(m)
    res = run_bass_kernel_spmd(nc, in_maps, core_ids=list(range(8)))
    X4 = np.empty((B, D, S), np.float32)
    for core in range(8):
        b, half = core // 2, core % 2
        X4[b][:, half * NT:(half + 1) * NT] = res.results[core]["xo"]
    return X4


def kernel(**inputs):
    X = _fm(inputs["L0_attn_x"])
    X = run_l0(X, inputs)
    X = run_l1(X, inputs)
    X = run_l2(X, inputs)
    X = run_l3(X, inputs)
    return np.ascontiguousarray(np.transpose(X, (0, 2, 1)))
